# version 36
# baseline (speedup 1.0000x reference)
import contextlib
import os
import numpy as np
import ml_dtypes
import concourse.bass as bass
import concourse.mybir as mybir
from concourse.bass_utils import run_bass_kernel_spmd

CHAIN_WAITS = (os.environ.get("BASS_SIM_FP64") == "1"
               or os.environ.get("KERNEL_CHAIN_WAITS") == "1")

F32 = mybir.dt.float32
BF16 = mybir.dt.bfloat16
AL = mybir.AluOpType
AF = mybir.ActivationFunctionType
AX = mybir.AxisListType

NCORES = 8
S, D = 16384, 1024
T = S // NCORES
NT = T // 128
H, HD = 16, 64
NWIN = T // 256
I_FFN = 2816
FMIN = float(np.finfo(np.float32).min)
M2 = [0.0625, 0.5625, 1.5625, 3.0625, 6.25, 12.25, 25.0]
WTS = [0.5, 0.5, 0.5, 0.5, 1.0, 1.0, 2.0]
RMS_EPS = 1.1920928955078125e-07

G_SZ = 128 * 2816
DN_SZ = 128 * 1024
SH = 2 * G_SZ + 3 * DN_SZ

ENGS = ("sync", "gpsimd", "tensor", "vector", "scalar")


class Sched:

    CHAIN_SEM = {"vector": "e_v", "scalar": "e_s", "gpsimd": "e_g"}

    def __init__(self, nc, sems, dpool):
        self.nc = nc
        self.sem = sems
        self.cnt = {k: 0 for k in sems}
        self.waited = {e: {} for e in ENGS}
        self.dpool = dpool
        self.didx = {e: 0 for e in dpool}
        self.q = None

    def phase(self):
        self.q = {e: [] for e in ENGS}

    def emit(self, eng, fn, waits=(), inc=None, amt=1, chain=True):
        waits = list(waits)
        cs = self.CHAIN_SEM.get(eng)
        if fn is not None and cs is not None:
            if chain and self.cnt.get(cs, 0):
                waits.append((cs, self.cnt[cs]))
            if inc is None:
                inc = cs
        flat = []
        for ev in waits:
            if isinstance(ev, list):
                flat.extend(ev)
            else:
                flat.append(ev)
        wd = self.waited[eng]
        ws = []
        for ev in flat:
            if ev is None:
                continue
            s, v = ev
            if v is None or v <= 0 or wd.get(s, 0) >= v:
                continue
            wd[s] = v
            ws.append((s, v))
        ev = None
        if inc is not None:
            self.cnt[inc] += amt
            ev = (inc, self.cnt[inc])

        def thunk(eo, fn=fn, ws=ws, inc=inc, amt=amt):
            for s, v in ws:
                eo.wait_ge(self.sem[s], v)
            ins = fn(eo) if fn is not None else None
            if inc is not None:
                if ins is None:
                    ins = eo.nop()
                ins.then_inc(self.sem[inc], amt)

        self.q[eng].append(thunk)
        return ev

    def emit_dma(self, eng, fn, waits=()):
        pool = self.dpool[eng]
        i = self.didx[eng]
        self.didx[eng] += 1
        s = pool[i % len(pool)]
        rounds = i // len(pool)
        w = list(waits) + [(s, 16 * rounds)]
        return self.emit(eng, fn, waits=w, inc=s, amt=16)

    def barrier(self):
        for eng, pool in self.dpool.items():
            dws = [(s, self.cnt[s]) for s in pool if self.cnt[s]]
            if dws:
                self.emit(eng, None, waits=dws)
        evs = [self.emit(e, None, inc="b_" + e) for e in ENGS]
        for e in ENGS:
            self.emit(e, None, waits=evs)

    def run_block(self):
        with self.nc.Block() as blk:
            for e in ENGS:
                q = self.q[e]
                if not q:
                    continue

                def runner(eo, q=q):
                    for t in q:
                        t(eo)

                getattr(blk, e)(runner)
        self.q = None


def emit_qd_head_part(P, xin, scr, F, c0, c1, par=0, in_ev=None,
                      rstd=None, rstd6=None, last=True):
    nb = F // 16
    b0, b1 = c0 // 16, c1 // 16
    bm, rsc = scr["bm"], scr["rsc"]
    sc_, rsq = scr["sc"][par], scr["rsq"][par]
    a2, sgn = scr["a2"][par], scr["sgn"][par]
    P.emit("vector", lambda eo, xin=xin, b0=b0, b1=b1, c0=c0, c1=c1:
           eo.tensor_reduce(
        bm[:, b0:b1], xin[:, c0:c1].rearrange("p (b k) -> p b k", k=16),
        AX.X, AL.max, apply_absolute_value=True), inc="e_v")
    if last:
        s6 = rstd6 if rstd6 is not None else 1.0 / 6.0
        P.emit("vector", lambda eo, nb=nb, sc_=sc_, s6=s6: eo.tensor_scalar(
            sc_[:, :nb], bm[:, :nb], s6, 1e-12, AL.mult, AL.max),
            waits=[scr["ev_sc_free"][par]], inc="e_v")
        P.emit("vector", lambda eo, nb=nb, sc_=sc_: eo.reciprocal(
            rsc[:, :nb], sc_[:, :nb]), inc="e_v")
        P.emit("vector", lambda eo, nb=nb, rsq=rsq: eo.tensor_tensor(
            rsq[:, :nb], rsc[:, :nb], rsc[:, :nb], AL.mult), inc="e_v")
    if rstd is not None:
        e_sq = P.emit("scalar", lambda eo, xin=xin, c0=c0, c1=c1, a2=a2,
                      rstd=rstd: eo.activation(
            a2[:, c0:c1], xin[:, c0:c1], AF.Square, scale=rstd),
            waits=[in_ev, scr["ev_cmp_free"][par] if c0 == 0 else None],
            inc="e_s")
    else:
        e_sq = P.emit("scalar", lambda eo, xin=xin, c0=c0, c1=c1, a2=a2:
                      eo.activation(a2[:, c0:c1], xin[:, c0:c1], AF.Square),
            waits=[in_ev, scr["ev_cmp_free"][par] if c0 == 0 else None],
            inc="e_s")
    e_sg = P.emit("scalar", lambda eo, xin=xin, c0=c0, c1=c1, sgn=sgn: eo.sign(
        sgn[:, c0:c1], xin[:, c0:c1]),
        waits=[scr["ev_sgn_free"][par] if c0 == 0 else None], inc="e_s")
    return e_sq, e_sg


def emit_qd_head(P, xin, scr, F, par=0, in_ev=None, rstd=None, rstd6=None):
    return emit_qd_head_part(P, xin, scr, F, 0, F, par=par, in_ev=in_ev,
                             rstd=rstd, rstd6=rstd6, last=True)


def emit_qd_tail(P, qout, scr, F, par=0, out_free=None, sq_ev=None,
                 sign_ev=None):
    nb = F // 16
    cb, g0 = scr["cb"][par], scr["g0"][par]
    sc_, rsq = scr["sc"][par], scr["rsq"][par]
    a2, sgn = scr["a2"][par], scr["sgn"][par]
    sA, gA = scr["sA"], scr["gA"]
    ev_a2 = P.emit("vector", lambda eo, F=F, nb=nb, a2=a2, rsq=rsq:
                   eo.tensor_tensor(
        a2[:, :F].rearrange("p (b k) -> p b k", k=16),
        a2[:, :F].rearrange("p (b k) -> p b k", k=16),
        rsq[:, :nb].unsqueeze(2).broadcast_to([128, nb, 16]), AL.mult),
        waits=[sq_ev, scr["ev_sgn_a2"][par]], inc="e_v")
    for i in range(5):
        dst = g0 if i == 0 else cb
        ev_ts = P.emit("vector", lambda eo, dst=dst, F=F, i=i: eo.tensor_scalar(
            dst[:, :F], a2[:, :F], M2[i], WTS[i], AL.is_gt, AL.mult),
            waits=[scr["ev_g0cb_free"][par]] if i <= 1 else (), inc="e_v")
        if i > 0:
            P.emit("vector", lambda eo, F=F: eo.tensor_tensor(
                g0[:, :F], g0[:, :F], cb[:, :F], AL.add), inc="e_v")
    scr["ev_cmp_free"][par] = ev_ts
    P.emit("scalar", lambda eo, F=F: eo.activation(
        sA[:, :F], a2[:, :F], AF.Sign, bias=scr["m2n"][:, 1:2]),
        waits=[ev_a2], inc="e_s")
    eR5 = P.emit("scalar", lambda eo, F=F: eo.activation(
        gA[:, :F], sA[:, :F], AF.Relu),
        waits=[scr["ev_gA_free"][0]], inc="e_s")
    evA = P.emit("vector", lambda eo, F=F: eo.tensor_tensor(
        g0[:, :F], g0[:, :F], gA[:, :F], AL.add), waits=[eR5], inc="e_v")
    gB = scr["gB"] if scr["gB"] is not None else gA
    gBw = scr["ev_gB_free"][0] if scr["gB"] is not None else evA
    P.emit("scalar", lambda eo, F=F: eo.activation(
        sA[:, :F], a2[:, :F], AF.Sign, bias=scr["m2n"][:, 2:3]), inc="e_s")
    eR6 = P.emit("scalar", lambda eo, F=F, gB=gB: eo.activation(
        gB[:, :F], sA[:, :F], AF.Relu, scale=2.0), waits=[gBw], inc="e_s")
    evC = P.emit("vector", lambda eo, F=F, gB=gB: eo.tensor_tensor(
        g0[:, :F], g0[:, :F], gB[:, :F], AL.add), waits=[eR6], inc="e_v")
    scr["ev_gA_free"][0] = evA if scr["gB"] is not None else evC
    scr["ev_gB_free"][0] = evC
    scr["ev_sgn_a2"][par] = eR6
    ev_cbm = P.emit("gpsimd", lambda eo, F=F, sgn=sgn: eo.tensor_tensor(
        cb[:, :F], g0[:, :F], sgn[:, :F], AL.mult),
        waits=[sign_ev, evC], inc="e_g")
    scr["ev_sgn_free"][par] = ev_cbm
    evq = P.emit("vector", lambda eo, qout=qout, F=F, nb=nb, sc_=sc_:
                 eo.tensor_tensor(
        qout.rearrange("p (b k) -> p b k", k=16),
        cb[:, :F].rearrange("p (b k) -> p b k", k=16),
        sc_[:, :nb].unsqueeze(2).broadcast_to([128, nb, 16]), AL.mult),
        waits=[out_free, ev_cbm], inc="e_v")
    scr["ev_g0cb_free"][par] = evq
    scr["ev_sc_free"][par] = evq
    return evq


def emit_qd(P, xin, qout, scr, F, out_free=None, in_ev=None):
    e_sq, e_sg = emit_qd_head(P, xin, scr, F, par=0, in_ev=in_ev)
    evq = emit_qd_tail(P, qout, scr, F, par=0, out_free=out_free,
                       sq_ev=e_sq, sign_ev=e_sg)
    return evq, e_sg


def build_program():
    nc = bass.Bass("TRN2", target_bir_lowering=False, debug=False)

    x_d = nc.dram_tensor("x", [128, NT * D], F32, kind="ExternalInput").ap()
    wqT_d = nc.dram_tensor("wqT", [128, 8 * D], F32, kind="ExternalInput").ap()
    wkT_d = nc.dram_tensor("wkT", [128, 8 * D], F32, kind="ExternalInput").ap()
    wvT_d = nc.dram_tensor("wvT", [128, 8 * D], F32, kind="ExternalInput").ap()
    woT_d = nc.dram_tensor("woT", [128, 8 * D], F32, kind="ExternalInput").ap()
    wg_d = nc.dram_tensor("wg", [128, I_FFN], F32, kind="ExternalInput").ap()
    wu_d = nc.dram_tensor("wu", [128, I_FFN], F32, kind="ExternalInput").ap()
    wd_d = nc.dram_tensor("wd", [128, 3 * 8 * 128], F32, kind="ExternalInput").ap()
    cos_d = nc.dram_tensor("cosb", [128, NT * 32], BF16, kind="ExternalInput").ap()
    sin_d = nc.dram_tensor("sinb", [128, NT * 32], BF16, kind="ExternalInput").ap()
    gain_d = nc.dram_tensor("gain128", [128, H], F32, kind="ExternalInput").ap()
    mask_d = nc.dram_tensor("mask0", [128, 384], F32, kind="ExternalInput").ap()
    id_d = nc.dram_tensor("ident", [128, 128], BF16, kind="ExternalInput").ap()
    out_d = nc.dram_tensor("out", [128, NT * D], F32, kind="ExternalOutput").ap()
    ag_in = nc.dram_tensor("ag_in", [SH], BF16).ap()
    wffnq_d = nc.dram_tensor("wffnq", [NCORES * SH], BF16,
                             addr_space="Shared").ap()
    xnqfm_d = nc.dram_tensor("xnqfm_dr", [128, 8 * T], BF16).ap()

    root = contextlib.ExitStack()
    sems = {}
    snames = (["d_out", "e_t", "e_v", "e_s", "e_g",
               "b_sync", "b_gpsimd", "b_tensor", "b_vector", "b_scalar"]
              + [f"ds{i}" for i in range(12)] + [f"dg{i}" for i in range(4)])
    for s in snames:
        sems[s] = root.enter_context(nc.semaphore(s))
    P = Sched(nc, sems, {"sync": [f"ds{i}" for i in range(12)],
                         "gpsimd": [f"dg{i}" for i in range(4)]})

    def sb(st, name, shape, dt):
        return st.enter_context(nc.sbuf_tensor(name, shape, dt))

    x_sb = sb(root, "x_sb", [128, NT, D], F32)
    stat = sb(root, "stat", [128, 16], F32)
    m2n = sb(root, "m2n", [128, 3], F32)
    gain_sb = sb(root, "gain_sb", [128, H], F32)
    mask_sb = sb(root, "mask_sb", [128, 384], F32)
    id_sb = sb(root, "id_sb", [128, 128], BF16)
    pm = [root.enter_context(nc.psum_tensor(f"pm{i}", [128, 512], F32))
          for i in range(4)]
    pt = [root.enter_context(nc.psum_tensor(f"pt{i}", [128, 1024], BF16))
          for i in range(2)]
    psc = [root.enter_context(nc.psum_tensor(f"psc{i}", [128, 512], F32))
           for i in range(2)]

    stA = contextlib.ExitStack()
    xnfm = sb(stA, "xnfm", [128, 8, T], BF16)
    wq_sb = sb(stA, "wq_sb", [128, 8, D], BF16)
    wk_sb = sb(stA, "wk_sb", [128, 8, D], BF16)
    wv_sb = sb(stA, "wv_sb", [128, 8, D], BF16)
    wo_sb = sb(stA, "wo_sb", [128, 8, D], BF16)
    cos_sb = sb(stA, "cos_sb", [128, NT, 32], BF16)
    sin_sb = sb(stA, "sin_sb", [128, NT, 32], BF16)

    def alloc_qds(st, F, tag="", npar=1, gb=False):
        return {
            "m2n": m2n,
            "bm": sb(st, f"qd_bm{tag}", [128, F // 16], F32),
            "rsc": sb(st, f"qd_rsc{tag}", [128, F // 16], F32),
            "a2": [sb(st, f"qd_a2{tag}{i}", [128, F], F32)
                   for i in range(npar)],
            "cb": [sb(st, f"qd_cb{tag}{i}", [128, F], BF16)
                   for i in range(npar)],
            "g0": [sb(st, f"qd_g0{tag}{i}", [128, F], BF16)
                   for i in range(npar)],
            "sc": [sb(st, f"qd_sc{tag}{i}", [128, F // 16], F32)
                   for i in range(npar)],
            "rsq": [sb(st, f"qd_rsq{tag}{i}", [128, F // 16], F32)
                    for i in range(npar)],
            "sgn": [sb(st, f"qd_sgn{tag}{i}", [128, F], BF16)
                    for i in range(npar)],
            "sA": sb(st, f"qd_sA{tag}", [128, F], BF16),
            "gA": sb(st, f"qd_gA{tag}", [128, F], BF16),
            "gB": sb(st, f"qd_gB{tag}", [128, F], BF16) if gb else None,
            "ev_gA_free": [None],
            "ev_gB_free": [None],
            "ev_g0cb_free": [None] * npar,
            "ev_sgn_free": [None] * npar,
            "ev_sc_free": [None] * npar,
            "ev_cmp_free": [None] * npar,
            "ev_sgn_a2": [None] * npar,
        }

    stB = contextlib.ExitStack()
    qds = alloc_qds(stB, 1024, npar=2)
    wsh = sb(stB, "wsh", [128, 1024], F32)
    qsh = sb(stB, "qsh", [128, 1024], BF16)
    fmo = sb(stB, "fmo", [128, 1024], BF16)
    xc = sb(stB, "xc", [128, D], F32)
    xnb = sb(stB, "xnb", [128, D], BF16)
    xnq = [sb(stB, f"xnq{i}", [128, D], BF16) for i in range(2)]
    xnqf = xnb

    P.phase()
    for _i in range(3):
        P.emit("vector", lambda eo, _i=_i: eo.memset(
            m2n[:, _i:_i + 1], -M2[4 + _i]), inc="e_v")
    ev_x = []
    for q in range(4):
        ev_x.append(P.emit_dma("sync", lambda eo, q=q: eo.dma_start(
            out=x_sb[:, 4 * q:4 * (q + 1), :],
            in_=x_d.rearrange("p (t d) -> p t d", d=D)[:, 4 * q:4 * (q + 1), :])))
    ev_cos = P.emit_dma("sync", lambda eo: eo.dma_start(
        out=cos_sb[:, :, :], in_=cos_d.rearrange("p (t j) -> p t j", j=32)))
    ev_sin = P.emit_dma("sync", lambda eo: eo.dma_start(
        out=sin_sb[:, :, :], in_=sin_d.rearrange("p (t j) -> p t j", j=32)))
    ev_gld = P.emit_dma("sync", lambda eo: eo.dma_start(out=gain_sb[:, :],
                                                        in_=gain_d))
    ev_mask = P.emit_dma("sync", lambda eo: eo.dma_start(out=mask_sb[:, :],
                                                         in_=mask_d))
    ev_id = P.emit_dma("sync", lambda eo: eo.dma_start(out=id_sb[:, :],
                                                       in_=id_d))
    ev_attnw = []
    attnw_pend = [(wqT_d, wq_sb), (wkT_d, wk_sb), (wvT_d, wv_sb),
                  (woT_d, wo_sb)]
    ev_gain = P.emit("vector", lambda eo: eo.tensor_scalar(
        gain_sb[:, :], gain_sb[:, :], 0.125, None, AL.mult),
        waits=[ev_gld], inc="e_v")

    st_wqd = {"evq": None, "esg": None, "pt": [None, None], "store": None,
              "ptsel": 0}

    def wqd_piece(src_ap, ntile, off, Wrow, col0):
        F = ntile * 128
        vsnap = ("e_v", P.cnt["e_v"])
        evd = P.emit_dma("sync", lambda eo, src_ap=src_ap, F=F: eo.dma_start(
            out=wsh[:, :F], in_=src_ap),
            waits=[st_wqd["evq"], st_wqd["esg"]])
        P.emit("vector", None, waits=[evd])
        e_sq, e_sg = emit_qd_head(P, wsh[:, :F], qds, F, par=0,
                                  in_ev=[evd, vsnap])
        st_wqd["evq"] = ("e_v", P.cnt["e_v"])
        st_wqd["esg"] = e_sg
        evq = emit_qd_tail(P, qsh[:, :F], qds, F, par=0, sq_ev=e_sq,
                           sign_ev=e_sg)
        bank = pt[0]
        evt = None
        for j in range(ntile):
            evt = P.emit("tensor", lambda eo, j=j, bank=bank: eo.transpose(
                bank[:, j * 128:(j + 1) * 128], qsh[:, j * 128:(j + 1) * 128],
                id_sb[:, :]), waits=[evq, ev_id, st_wqd["pt"][0]], inc="e_t")
        ev_ev = P.emit("vector", lambda eo, F=F, bank=bank: eo.tensor_copy(
            fmo[:, :F], bank[:, :F]), waits=[evt, st_wqd["store"]], inc="e_v")
        st_wqd["pt"][0] = ev_ev
        st_wqd["store"] = P.emit_dma("sync", lambda eo, off=off, Wrow=Wrow, F=F,
                                     col0=col0: eo.dma_start(
            out=ag_in[off:off + 128 * Wrow]
                .rearrange("(p i) -> p i", p=128)[:, col0:col0 + F],
            in_=fmo[:, :F]), waits=[ev_ev])
        return st_wqd["store"]

    ag_waits = []
    piece_args = []
    for (w_ap, roff) in ((wg_d, 0), (wu_d, G_SZ)):
        for (t0, ntile) in ((0, 8), (8, 8), (16, 6)):
            piece_args.append((
                w_ap[:, t0 * 128:(t0 + ntile) * 128],
                ntile, roff, 2816, t0 * 128))
    for pos in range(3):
        piece_args.append((
            wd_d[:, 1024 * pos:1024 * (pos + 1)],
            8, 2 * G_SZ + pos * DN_SZ, 1024, 0))

    ev_xnb_free = None
    ev_sg_prev = None
    ev_xnq_free = [None, None]
    ev_qf_free = None
    ln_pend = [None, None]

    def ln_flush():
        if ln_pend[1] is not None:
            ln_pend[1]()
            ln_pend[1] = None
        if ln_pend[0] is not None:
            ln_pend[1] = ln_pend[0]()
            ln_pend[0] = None

    for t in range(NT):
        par = t & 1
        if t >= 2 and t % 2 == 0 and attnw_pend:
            w_d, w_s = attnw_pend.pop(0)
            ev_attnw.append(P.emit_dma("gpsimd", lambda eo, w_d=w_d, w_s=w_s:
                                       eo.dma_start(
                out=w_s[:, :, :].rearrange("p c n -> p (c n)"), in_=w_d)))
        if piece_args:
            ln_flush()
            ag_waits.append(wqd_piece(*piece_args.pop(0)))
        P.emit("vector", lambda eo, t=t: eo.reduce_sum(
            stat[:, 0:1], x_sb[:, t, :], AX.X), waits=[ev_x[t // 4]], inc="e_v")
        P.emit("vector", lambda eo: eo.tensor_scalar(
            stat[:, 1:2], stat[:, 0:1], 1.0 / D, None, AL.mult), inc="e_v")
        ev = P.emit("vector", lambda eo, t=t: eo.tensor_scalar(
            xc[:, :], x_sb[:, t, :], stat[:, 1:2], None, AL.subtract),
            waits=[ev_sg_prev], inc="e_v")
        ev_acc = P.emit("scalar", lambda eo: eo.activation(
            qsh[:, :], xc[:, :], AF.Square, accum_out=stat[:, 2:3]),
            waits=[ev], inc="e_s")
        ln_flush()
        ev = P.emit("vector", lambda eo: eo.tensor_scalar(
            stat[:, 3:4], stat[:, 2:3], 1.0 / D, 1e-5, AL.mult, AL.add),
            waits=[ev_acc], inc="e_v")
        ev = P.emit("scalar", lambda eo: eo.activation(
            stat[:, 4:5], stat[:, 3:4], AF.Ln), waits=[ev], inc="e_s")
        ev = P.emit("scalar", lambda eo: eo.activation(
            stat[:, 5:6], stat[:, 4:5], AF.Exp, scale=-0.5), inc="e_s")
        P.emit("vector", lambda eo: eo.tensor_scalar(
            stat[:, 6:7], stat[:, 5:6], 1.0 / 6.0, None, AL.mult),
            waits=[ev], inc="e_v")
        ev_xnb = P.emit("scalar", lambda eo: eo.activation(
            xnb[:, :], xc[:, :], AF.Copy, scale=stat[:, 5:6]),
            waits=[ev_xnb_free, ev_qf_free], inc="e_s")
        e_sq, e_sg = emit_qd_head(P, xc[:, :], qds, D, par=par,
                                  in_ev=ev_sg_prev, rstd=stat[:, 5:6],
                                  rstd6=stat[:, 6:7])
        ev_sg_prev = e_sg
        evt0 = None
        for c in range(8):
            evt0 = P.emit("tensor", lambda eo, c=c: eo.transpose(
                pt[0][:, c * 128:(c + 1) * 128], xnb[:, c * 128:(c + 1) * 128],
                id_sb[:, :]),
                waits=[ev_xnb, ev_id, st_wqd["pt"][0]], inc="e_t")
        ev_xnb_free = evt0

        def mk_ln_pend(t=t, par=par, evt0=evt0, e_sq=e_sq, e_sg=e_sg):
            def do():
                ev_fm = P.emit("vector", lambda eo, t=t: eo.tensor_copy(
                    xnfm[:, :, 128 * t:128 * (t + 1)],
                    pt[0][:, :].rearrange("p (c k) -> p c k", k=128)),
                    waits=[evt0], inc="e_v")
                st_wqd["pt"][0] = ev_fm
                evq = emit_qd_tail(P, xnq[par][:, :], qds, D, par=par,
                                   out_free=ev_xnq_free[par], sq_ev=e_sq,
                                   sign_ev=e_sg)
                evt1 = None
                for c in range(8):
                    evt1 = P.emit("tensor", lambda eo, c=c, par=par:
                                  eo.transpose(
                        pt[1][:, c * 128:(c + 1) * 128],
                        xnq[par][:, c * 128:(c + 1) * 128], id_sb[:, :]),
                        waits=[evq, st_wqd["pt"][1]], inc="e_t")
                ev_xnq_free[par] = evt1

                def do_evac(evt1=evt1, t=t):
                    nonlocal ev_qf_free
                    ev_qf = P.emit("vector", lambda eo: eo.tensor_copy(
                        xnqf[:, :].rearrange("p (c k) -> p c k", k=128),
                        pt[1][:, :].rearrange("p (c k) -> p c k", k=128)),
                        waits=[evt1, ev_qf_free, ev_xnb_free], inc="e_v")
                    st_wqd["pt"][1] = ev_qf
                    ev_qf_free = P.emit_dma("sync", lambda eo, t=t: eo.dma_start(
                        out=xnqfm_d.rearrange("p (c s) -> p c s", s=T)
                            [:, :, 128 * t:128 * (t + 1)],
                        in_=xnqf[:, :].rearrange("p (c k) -> p c k", k=128)),
                        waits=[ev_qf])
                return do_evac
            return do
        ln_pend[0] = mk_ln_pend()

    for _ in range(2):
        ln_flush()
    ev_wst = P.emit("gpsimd", lambda eo: eo.collective_compute(
        "AllGather", AL.bypass, replica_groups=[list(range(NCORES))],
        ins=[ag_in.opt()], outs=[wffnq_d.opt()]),
        waits=ag_waits, inc="e_g")

    P.barrier()
    P.run_block()
    stB.close()

    stC = contextlib.ExitStack()
    qk_sb = [sb(stC, f"qk{i}", [128, D], BF16) for i in range(2)]
    qsw = sb(stC, "qsw", [128, D], BF16)
    qm1 = sb(stC, "qm1", [128, D], BF16)
    sq3 = sb(stC, "sq3", [128, D], BF16)
    rstd_b = sb(stC, "rstd_b", [128, H], BF16)
    qn_b = sb(stC, "qn_b", [128, 2, D], BF16)
    kn_b = sb(stC, "kn_b", [128, 2, D], BF16)
    v_sb = sb(stC, "v_sb", [128, 2, D], BF16)
    qfm = sb(stC, "qfm", [128, 8, 256], BF16)
    kfm = sb(stC, "kfm", [128, 8, 256], BF16)
    yfm = sb(stC, "yfm", [128, 8, 256], BF16)
    sc_sb = [sb(stC, f"sc{i}", [128, 384], F32) for i in range(2)]
    e_sb = [sb(stC, f"e{i}", [128, 384], BF16) for i in range(3)]
    p_sb = [sb(stC, f"p{i}", [128, 384], BF16) for i in range(2)]
    pT_sb = [sb(stC, f"pT{i}", [128, 384], BF16) for i in range(2)]
    ast = sb(stC, "ast", [128, 32], F32)

    P.phase()
    pmf = [None, None, None, None]
    ptf = [None, None]
    pscf = [None, None]
    sc_free = [None, None]
    e_free = [None] * 3
    p_free = [None] * 2
    pT_free = [None] * 2
    ev_qk_free = [None, None]
    ev_dst_free = None
    ev_v_used = None
    ev_yfm_used = None
    ev_sq3_free = None

    v4 = lambda ap: ap.rearrange("p (h o j) -> p h o j", o=2, j=32)
    for w in range(NWIN):
        rounds = [("q", 0), ("q", 1), ("k", 0), ("k", 1), ("v", 0), ("v", 1)]
        rp = [None, None, None]

        def rflush():
            a = rp[0]() if rp[0] is not None else None
            b = rp[1]() if rp[1] is not None else None
            if rp[2] is not None:
                rp[2]()
            rp[0], rp[1], rp[2] = None, a, b

        def mk_back(kind, ti, par, outs, w=w):
            isq = kind == "q"
            dstb = qn_b if isq else kn_b
            fmdst = qfm if isq else kfm
            t = 2 * w + ti

            def back():
                nonlocal ev_sq3_free
                ev1 = P.emit("scalar", lambda eo, par=par: eo.activation(
                    sq3[:, :], qk_sb[par][:, :], AF.Square),
                    waits=outs + [ev_sq3_free], inc="e_s")
                ev2 = P.emit("vector", lambda eo: eo.tensor_reduce(
                    stat[:, 0:16], sq3[:, :].rearrange("p (h d) -> p h d", d=HD),
                    AX.X, AL.add), waits=[ev1], inc="e_v")
                ev_sq3_free = ev2
                ev3 = P.emit("vector", lambda eo: eo.tensor_scalar(
                    stat[:, 0:16], stat[:, 0:16], 1.0 / HD, RMS_EPS,
                    AL.mult, AL.add), inc="e_v")
                P.emit("scalar", lambda eo: eo.activation(
                    stat[:, 0:16], stat[:, 0:16], AF.Ln), waits=[ev3], inc="e_s")
                ev5 = P.emit("scalar", lambda eo: eo.activation(
                    rstd_b[:, :], stat[:, 0:16], AF.Exp, scale=-0.5), inc="e_s")
                if isq:
                    ev5 = P.emit("vector", lambda eo: eo.tensor_tensor(
                        rstd_b[:, :], rstd_b[:, :], gain_sb[:, :], AL.mult),
                        waits=[ev5, ev_gain], inc="e_v")
                P.emit("vector", lambda eo, t=t, par=par: eo.tensor_tensor(
                    v4(qsw[:, :])[:, :, 0, :], v4(qk_sb[par][:, :])[:, :, 1, :],
                    sin_sb[:, t, :].unsqueeze(1).broadcast_to([128, H, 32]),
                    AL.mult), waits=[ev_sin], inc="e_v")
                P.emit("vector", lambda eo, t=t, par=par:
                       eo.scalar_tensor_tensor(
                    v4(qsw[:, :])[:, :, 1, :], v4(qk_sb[par][:, :])[:, :, 0, :],
                    -1.0,
                    sin_sb[:, t, :].unsqueeze(1).broadcast_to([128, H, 32]),
                    AL.mult, AL.mult), inc="e_v")
                ev_qk_free[par] = P.emit("vector", lambda eo, t=t, par=par:
                                         eo.tensor_tensor(
                    v4(qm1[:, :]), v4(qk_sb[par][:, :]),
                    cos_sb[:, t, :].unsqueeze(1).unsqueeze(2)
                    .broadcast_to([128, H, 2, 32]), AL.mult),
                    waits=[ev_cos], inc="e_v")
                P.emit("vector", lambda eo: eo.tensor_tensor(
                    qm1[:, :], qm1[:, :], qsw[:, :], AL.add), inc="e_v")
                ev8 = P.emit("vector", lambda eo, dstb=dstb, ti=ti:
                             eo.tensor_tensor(
                    dstb[:, ti, :].rearrange("p (h d) -> p h d", d=HD),
                    qm1[:, :].rearrange("p (h d) -> p h d", d=HD),
                    rstd_b[:, :].unsqueeze(2).broadcast_to([128, H, HD]),
                    AL.mult), waits=[ev5, ev_dst_free], inc="e_v")

                def transp():
                    nonlocal ev_dst_free
                    evt = None
                    for c in range(8):
                        evt = P.emit("tensor", lambda eo, c=c, ti=ti, dstb=dstb,
                                     par=par: eo.transpose(
                                         pt[par][:, c * 128:(c + 1) * 128],
                                         dstb[:, ti, c * 128:(c + 1) * 128],
                                         id_sb[:, :]),
                                     waits=[ev8, ptf[par]], inc="e_t")
                    ev_dst_free = evt

                    def fmevac():
                        ptf[par] = P.emit("vector", lambda eo, ti=ti,
                                          fmdst=fmdst, par=par: eo.tensor_copy(
                            fmdst[:, :, 128 * ti:128 * (ti + 1)],
                            pt[par][:, :].rearrange("p (c k) -> p c k", k=128)),
                            waits=[evt], inc="e_v")
                    return fmevac
                return transp
            return back

        for s in range(len(rounds) + 3):
            if s < len(rounds):
                kind, ti = rounds[s]
                par = s & 1
                banks = (2, 3) if par == 0 else (0, 1)
                wsb = wq_sb if kind == "q" else (wk_sb if kind == "k" else wv_sb)
                evms = []
                for nn in range(2):
                    bank = banks[nn]

                    def mms(eo, ti=ti, nn=nn, w=w, wsb=wsb, bank=bank):
                        ins = None
                        for c in range(8):
                            ins = eo.matmul(
                                pm[bank][:, :],
                                xnfm[:, c, (2 * w + ti) * 128:(2 * w + ti + 1) * 128],
                                wsb[:, c, 512 * nn:512 * (nn + 1)],
                                start=(c == 0), stop=(c == 7))
                        return ins
                    evms.append(P.emit("tensor", mms,
                                       waits=[ev_attnw, pmf[bank]], inc="e_t"))
            rflush()
            if s < len(rounds):
                outs = []
                for nn in range(2):
                    bank = banks[nn]
                    if kind == "v":
                        ev = P.emit("scalar", lambda eo, ti=ti, nn=nn, bank=bank:
                                    eo.copy(v_sb[:, ti, 512 * nn:512 * (nn + 1)],
                                            pm[bank][:, :]),
                                    waits=[evms[nn], ev_v_used], inc="e_s")
                    else:
                        ev = P.emit("scalar", lambda eo, nn=nn, bank=bank,
                                    par=par: eo.copy(
                                        qk_sb[par][:, 512 * nn:512 * (nn + 1)],
                                        pm[bank][:, :]),
                                    waits=[evms[nn], ev_qk_free[par]], inc="e_s")
                    pmf[bank] = ev
                    outs.append(ev)
                if kind != "v":
                    rp[0] = mk_back(kind, ti, par, outs)
        ev_qkfm = ("e_v", P.cnt["e_v"])
        ev_v = ("e_s", P.cnt["e_s"])

        exp_evs = [None] * 4
        score_evs = [None, None]

        def stage_a(h):
            par = h & 1
            c, off = h >> 1, 64 * (h & 1)
            score_evs[par] = P.emit("tensor", lambda eo, c=c, off=off, par=par: (
                eo.matmul(psc[par][:, 0:128], qfm[off:off + 64, c, 0:128],
                          kfm[off:off + 64, c, 0:128], start=True, stop=True,
                          tile_position=(off, 0)),
                eo.matmul(psc[par][:, 128:384], qfm[off:off + 64, c, 128:256],
                          kfm[off:off + 64, c, 0:256], start=True, stop=True,
                          tile_position=(off, 0)))[-1],
                waits=[ev_qkfm, pscf[par]], inc="e_t")

        def stage_b(h):
            par = h & 1
            p4 = h & 3
            ev_s = score_evs[par]
            ev_msk = P.emit("vector", lambda eo, par=par: eo.tensor_tensor(
                sc_sb[par][:, :], psc[par][:, 0:384], mask_sb[:, :], AL.add),
                waits=[ev_s, ev_mask, sc_free[par]], inc="e_v")
            pscf[par] = ev_msk
            ev_m0 = P.emit("vector", lambda eo, par=par, p4=p4: eo.reduce_max(
                ast[:, 8 * p4:8 * p4 + 1], sc_sb[par][:, 0:128], AX.X,
                negate=True), inc="e_v")
            ev_m1 = P.emit("vector", lambda eo, par=par, p4=p4: eo.reduce_max(
                ast[:, 8 * p4 + 1:8 * p4 + 2], sc_sb[par][:, 128:384], AX.X,
                negate=True), inc="e_v")
            p3 = h % 3
            P.emit("scalar", lambda eo, par=par, p4=p4, p3=p3: eo.activation(
                e_sb[p3][:, 0:128], sc_sb[par][:, 0:128], AF.Exp,
                bias=ast[:, 8 * p4:8 * p4 + 1],
                accum_out=ast[:, 8 * p4 + 2:8 * p4 + 3]),
                waits=[ev_m0, e_free[p3]], inc="e_s")
            e1 = P.emit("scalar", lambda eo, par=par, p4=p4, p3=p3: eo.activation(
                e_sb[p3][:, 128:384], sc_sb[par][:, 128:384], AF.Exp,
                bias=ast[:, 8 * p4 + 1:8 * p4 + 2],
                accum_out=ast[:, 8 * p4 + 3:8 * p4 + 4]),
                waits=[ev_m1], inc="e_s")
            sc_free[par] = e1
            exp_evs[p4] = e1

        def stage_c(h):
            nonlocal ev_v_used, ev_yfm_used
            par = h & 1
            p4 = h & 3
            p3 = h % 3
            c, off = h >> 1, 64 * (h & 1)
            P.emit("vector", lambda eo, p4=p4: eo.reciprocal(
                ast[:, 8 * p4 + 4:8 * p4 + 6],
                ast[:, 8 * p4 + 2:8 * p4 + 4]),
                waits=[exp_evs[p4]], inc="e_v")
            P.emit("vector", lambda eo, p4=p4, p3=p3, par=par: eo.tensor_scalar(
                p_sb[par][:, 0:128], e_sb[p3][:, 0:128],
                ast[:, 8 * p4 + 4:8 * p4 + 5], None, AL.mult),
                waits=[p_free[par]], inc="e_v")
            ev_p1 = P.emit("vector", lambda eo, p4=p4, p3=p3, par=par:
                           eo.tensor_scalar(
                p_sb[par][:, 128:384], e_sb[p3][:, 128:384],
                ast[:, 8 * p4 + 5:8 * p4 + 6], None, AL.mult), inc="e_v")
            e_free[p3] = ev_p1
            evt = P.emit("tensor", lambda eo, par=par: (
                eo.transpose(pt[par][:, 0:128], p_sb[par][:, 0:128], id_sb[:, :]),
                eo.transpose(pt[par][:, 128:256], p_sb[par][:, 128:256],
                             id_sb[:, :]),
                eo.transpose(pt[par][:, 256:384], p_sb[par][:, 256:384],
                             id_sb[:, :]))[-1],
                waits=[ev_p1, ptf[par]], inc="e_t")
            p_free[par] = evt
            ev_pT = P.emit("scalar", lambda eo, par=par: eo.copy(
                pT_sb[par][:, 0:384], pt[par][:, 0:384]),
                waits=[evt, pT_free[par]], inc="e_s")
            ptf[par] = ev_pT

            def pv(eo, h=h, off=off, par=par):
                eo.matmul(pm[par][off:off + 64, 0:256],
                          v_sb[:, 0, 64 * h:64 * h + 64], pT_sb[par][:, 0:256],
                          start=True, stop=False, tile_position=(0, off))
                return eo.matmul(pm[par][off:off + 64, 128:256],
                                 v_sb[:, 1, 64 * h:64 * h + 64],
                                 pT_sb[par][:, 256:384],
                                 start=False, stop=True, tile_position=(0, off))
            ev_y = P.emit("tensor", pv, waits=[ev_pT, ev_v, pmf[par]],
                          inc="e_t")
            pT_free[par] = ev_y
            ev_v_used = ev_y
            y_evs[par] = ev_y

        def stage_d(h):
            nonlocal ev_yfm_used
            par = h & 1
            c, off = h >> 1, 64 * (h & 1)
            if par == 0:
                ev_ye = P.emit("vector", lambda eo, c=c, off=off, par=par:
                               eo.tensor_copy(
                    yfm[off:off + 64, c, :], pm[par][off:off + 64, 0:256]),
                    waits=[y_evs[par], ev_yfm_used], inc="e_v")
            else:
                ev_ye = P.emit("scalar", lambda eo, c=c, off=off, par=par:
                               eo.copy(
                    yfm[off:off + 64, c, :], pm[par][off:off + 64, 0:256]),
                    waits=[y_evs[par], ev_yfm_used], inc="e_s")
            pmf[par] = ev_ye
            return ev_ye

        y_evs = [None, None]
        ev_yfm = None
        for s in range(H + 4):
            if s < H:
                stage_a(s)
            if 1 <= s <= H:
                stage_b(s - 1)
            if 3 <= s <= H + 2:
                stage_c(s - 3)
            if s >= 4:
                ev_yfm = stage_d(s - 4)

        for ti in range(2):
            for nn in range(2):
                bank = 2 * ti + nn

                def mms(eo, ti=ti, nn=nn, bank=bank):
                    ins = None
                    for c in range(8):
                        ins = eo.matmul(pm[bank][:, :],
                                        yfm[:, c, 128 * ti:128 * (ti + 1)],
                                        wo_sb[:, c, 512 * nn:512 * (nn + 1)],
                                        start=(c == 0), stop=(c == 7))
                    return ins
                evm = P.emit("tensor", mms, waits=[ev_yfm, pmf[bank]],
                             inc="e_t")
                ev_yfm_used = evm
                pmf[bank] = P.emit("vector", lambda eo, ti=ti, nn=nn,
                                   bank=bank, w=w: eo.tensor_tensor(
                                       x_sb[:, 2 * w + ti,
                                            512 * nn:512 * (nn + 1)],
                                       x_sb[:, 2 * w + ti,
                                            512 * nn:512 * (nn + 1)],
                                       pm[bank][:, :], AL.add),
                                   waits=[evm], inc="e_v")

    P.barrier()
    P.run_block()
    stC.close()
    stA.close()

    stD = contextlib.ExitStack()
    qds4 = alloc_qds(stD, 1024, tag="4", npar=2)
    xnqfm = sb(stD, "xnqfm", [128, 8, T], BF16)
    wgq_it = [sb(stD, f"wgq{i}", [128, 8, 512], BF16) for i in range(2)]
    wuq_it = [sb(stD, f"wuq{i}", [128, 8, 512], BF16) for i in range(2)]
    wdq_sb = sb(stD, "wdq_sb", [128, 8, 512], BF16)
    hqfm = sb(stD, "hqfm", [128, 8, T], BF16)
    h_f = [sb(stD, f"h_f{i}", [128, 1024], F32) for i in range(2)]
    hq_b = [sb(stD, f"hq_b{i}", [128, 1024], BF16) for i in range(2)]

    P.phase()
    ev_xnq = P.emit_dma("sync", lambda eo: eo.dma_start(
        out=xnqfm[:, :, :].rearrange("p c s -> p (c s)"), in_=xnqfm_d))

    pmf = [None, None, None, None]
    ptf4 = [None, None]
    h_free = [None, None]
    hq_free = [None, None]
    thirds = [(0, 8, [512, 512]), (8, 8, [512, 512]), (16, 6, [512, 256])]
    buf = 0
    ev_w_used = [None, None]
    pend = [None, None, None]

    dn_jobs = []
    dn_state = {"loaded": -1, "wd_free": None, "pmf": [None, None],
                "ld_evs": {}, "lastmm": {}, "par": 0}

    def emit_dn_load(k, gbase, ncp):
        evs = []
        for cj in range(ncp):
            j = gbase + cj
            off = (j // 3) * SH + 2 * G_SZ + (j % 3) * DN_SZ
            for nn in range(2):
                evs.append(P.emit_dma("sync", lambda eo, cj=cj, nn=nn, off=off:
                                      eo.dma_start(
                    out=wdq_sb[:, 2 * cj + nn, :],
                    in_=wffnq_d[off:off + DN_SZ]
                    .rearrange("(p i) -> p i", p=128)
                    [:, 512 * nn:512 * (nn + 1)]),
                    waits=[ev_wst, dn_state["wd_free"]]))
        dn_state["ld_evs"][k] = evs
        dn_state["loaded"] = k

    def push_dn_jobs(k, cbase, ncp, tte, ev_hq):
        for ttj in (tte, tte + 1):
            for nn in range(2):
                def dj(k=k, cbase=cbase, ncp=ncp, ttj=ttj, nn=nn, ev_hq=ev_hq):
                    par = dn_state["par"]
                    dn_state["par"] ^= 1

                    def dmm(eo, par=par):
                        ins = None
                        for cj in range(ncp):
                            ins = eo.matmul(psc[par][:, :],
                                            hqfm[:, cbase + cj,
                                                 128 * ttj:128 * (ttj + 1)],
                                            wdq_sb[:, 2 * cj + nn, :],
                                            start=(cj == 0),
                                            stop=(cj == ncp - 1))
                        return ins
                    evm = P.emit("tensor", dmm,
                                 waits=[ev_hq, dn_state["ld_evs"][k],
                                        dn_state["pmf"][par]], inc="e_t")
                    dn_state["lastmm"][k] = evm
                    dn_state["wd_free"] = evm
                    dn_state["pmf"][par] = P.emit(
                        "vector", lambda eo, ttj=ttj, nn=nn, par=par:
                        eo.tensor_tensor(
                            x_sb[:, ttj, 512 * nn:512 * (nn + 1)],
                            x_sb[:, ttj, 512 * nn:512 * (nn + 1)],
                            psc[par][:, :], AL.add),
                        waits=[evm], inc="e_v")
                dn_jobs.append((k, dj))

    def pop_dn_jobs(limit):
        n = 0
        while dn_jobs and dn_jobs[0][0] <= dn_state["loaded"] and n < limit:
            dn_jobs.pop(0)[1]()
            n += 1

    pend_evac = [None]

    def flush_pend():
        tc = pend[0]() if pend[0] is not None else None
        ec = pend[1]() if pend[1] is not None else None
        pend_evac[0] = pend[2]
        pend[0], pend[1], pend[2] = None, tc, ec

    def flush_evac():
        if pend_evac[0] is not None:
            pend_evac[0]()
            pend_evac[0] = None

    def drain_pend():
        for _ in range(3):
            flush_pend()
            flush_evac()

    kglob = -1
    part_info = []
    for (ch0, nch, itiles) in thirds:
        ioff = 0
        for it, iw in enumerate(itiles):
            kglob += 1
            part_info.append((ch0 + ioff // 128, (ioff // 128) % 8, iw // 128))
            if kglob > 0:
                emit_dn_load(kglob - 1, part_info[kglob - 1][0],
                             part_info[kglob - 1][2])
            gbuf, ubuf = wgq_it[buf], wuq_it[buf]
            ev_lg = []
            goff = ch0 * 128 + ioff
            for c in range(8):
                base = c * SH
                ev_lg.append(P.emit_dma("sync", lambda eo, base=base, goff=goff,
                                        iw=iw, c=c, gbuf=gbuf: eo.dma_start(
                    out=gbuf[:, c, :iw],
                    in_=wffnq_d[base:base + G_SZ]
                    .rearrange("(p i) -> p i", p=128)[:, goff:goff + iw]),
                    waits=[ev_wst, ev_w_used[buf]]))
                ev_lg.append(P.emit_dma("sync", lambda eo, base=base, goff=goff,
                                        iw=iw, c=c, ubuf=ubuf: eo.dma_start(
                    out=ubuf[:, c, :iw],
                    in_=wffnq_d[base + G_SZ:base + 2 * G_SZ]
                    .rearrange("(p i) -> p i", p=128)[:, goff:goff + iw]),
                    waits=[ev_wst, ev_w_used[buf]]))
            ev_ldw = ev_lg
            chunkbase = ioff // 128
            nblk = iw // 128
            ev_gu_last = None
            for tt in range(NT):
                par = tt & 1
                pp = (tt >> 1) & 1
                pg, pu = pm[par], pm[2 + par]
                if (tt & 1) == 0:
                    pop_dn_jobs(4)
                    flush_pend()

                def gumm(eo, tt=tt, iw=iw, gbuf=gbuf, ubuf=ubuf, pg=pg, pu=pu):
                    ins = None
                    for c in range(8):
                        eo.matmul(pg[:, :iw],
                                  xnqfm[:, c, 128 * tt:128 * (tt + 1)],
                                  gbuf[:, c, :iw],
                                  start=(c == 0), stop=(c == 7))
                        ins = eo.matmul(pu[:, :iw],
                                        xnqfm[:, c, 128 * tt:128 * (tt + 1)],
                                        ubuf[:, c, :iw],
                                        start=(c == 0), stop=(c == 7))
                    return ins
                evg = P.emit("tensor", gumm,
                             waits=[ev_ldw, ev_xnq, pmf[par], pmf[2 + par]],
                             inc="e_t")
                evu = evg
                ev_gu_last = evu
                ev_sl = P.emit("scalar", lambda eo, iw=iw, pg=pg,
                               pp=pp, tt=tt: eo.activation(
                    h_f[pp][:, iw * (tt & 1):iw * ((tt & 1) + 1)],
                    pg[:, :iw], AF.Silu),
                    waits=[evg, h_free[pp] if (tt & 1) == 0 else None],
                    inc="e_s")
                pmf[par] = ev_sl
                ev_h = P.emit("vector", lambda eo, iw=iw, pu=pu,
                              pp=pp, tt=tt: eo.tensor_tensor(
                    h_f[pp][:, iw * (tt & 1):iw * ((tt & 1) + 1)],
                    h_f[pp][:, iw * (tt & 1):iw * ((tt & 1) + 1)],
                    pu[:, :iw], AL.mult),
                    waits=[ev_sl, evu], inc="e_v")
                pmf[2 + par] = ev_h
                if (tt & 1) == 0:
                    continue
                FP = 2 * iw
                e_sq, e_sg = emit_qd_head(P, h_f[pp][:, :FP], qds4, FP,
                                          par=pp, in_ev=ev_h)
                h_free[pp] = e_sg
                flush_evac()

                def mk_pend(pp=pp, tte=tt - 1, nblk=nblk, chunkbase=chunkbase,
                            FP=FP, e_sq=e_sq, e_sg=e_sg, kg=kglob):
                    def do_tail():
                        evq = emit_qd_tail(P, hq_b[pp][:, :FP], qds4, FP,
                                           par=pp, out_free=hq_free[pp],
                                           sq_ev=e_sq, sign_ev=e_sg)

                        def do_transp(evq=evq):
                            evt = None
                            for jb in range(2 * nblk):
                                evt = P.emit("tensor", lambda eo, jb=jb, pp=pp:
                                             eo.transpose(
                                                 pt[pp][:, jb * 128:(jb + 1) * 128],
                                                 hq_b[pp][:, jb * 128:(jb + 1) * 128],
                                                 id_sb[:, :]),
                                             waits=[evq, ptf4[pp],
                                                    dn_state["lastmm"]
                                                    .get(kg - 2)],
                                             inc="e_t")
                            hq_free[pp] = evt

                            def do_evac(evt=evt):
                                ptf4[pp] = P.emit("scalar", lambda eo, tte=tte,
                                                  chunkbase=chunkbase,
                                                  nblk=nblk,
                                                  pp=pp: eo.copy(
                                    hqfm[:, chunkbase:chunkbase + nblk,
                                         128 * tte:128 * (tte + 2)]
                                    .rearrange("p c (o k) -> p o c k", o=2),
                                    pt[pp][:, :2 * nblk * 128]
                                    .rearrange("p (o c k) -> p o c k",
                                               o=2, k=128)),
                                    waits=[evt], inc="e_s")
                                push_dn_jobs(kg, chunkbase, nblk, tte,
                                             ptf4[pp])
                            return do_evac
                        return do_transp
                    return do_tail
                pend[0] = mk_pend()
            ev_w_used[buf] = ev_gu_last
            buf ^= 1
            ioff += iw
    drain_pend()
    pop_dn_jobs(10 ** 9)
    emit_dn_load(kglob, part_info[kglob][0], part_info[kglob][2])
    for p8 in range(8):
        pop_dn_jobs(4)
        evp = ("e_v", P.cnt["e_v"])
        P.emit("sync", lambda eo, p8=p8: eo.dma_start(
            out=out_d.rearrange("p (t d) -> p t d", d=D)[:, 2 * p8:2 * p8 + 2, :],
            in_=x_sb[:, 2 * p8:2 * p8 + 2, :]),
            waits=[evp], inc="d_out", amt=16)
    pop_dn_jobs(10 ** 9)
    P.emit("sync", None, waits=[("d_out", 128)])
    P.barrier()
    P.run_block()
    stD.close()
    root.close()
    return nc


_NC_CACHE = {}


def kernel(L0_attn_x, L0_attn_wq, L0_attn_wk, L0_attn_wv, L0_attn_wo,
           L0_attn_q_gain, L0_ffn_w_gate, L0_ffn_w_up, L0_ffn_w_down):
    f32 = lambda a: np.ascontiguousarray(np.asarray(a, dtype=np.float32))
    x = f32(L0_attn_x).reshape(S, D)
    wqT = np.ascontiguousarray(f32(L0_attn_wq).T)
    wkT = np.ascontiguousarray(f32(L0_attn_wk).T)
    wvT = np.ascontiguousarray(f32(L0_attn_wv).T)
    woT = np.ascontiguousarray(f32(L0_attn_wo).T)
    wg = f32(L0_ffn_w_gate)
    wu = f32(L0_ffn_w_up)
    wd = f32(L0_ffn_w_down)
    wd_pad = np.concatenate(
        [wd, np.zeros((D, 3 * 128 * NCORES - I_FFN), dtype=np.float32)], axis=1)
    gain128 = np.ascontiguousarray(np.tile(f32(L0_attn_q_gain)[None, :],
                                           (128, 1)))

    inv_freq = (1.0 / 10000.0 ** (np.arange(0, HD, 2, dtype=np.float32)
                                  / np.float32(HD))).astype(np.float32)
    freqs = np.outer(np.arange(S, dtype=np.float32), inv_freq).astype(np.float32)
    cosb = np.cos(freqs.astype(np.float64)).astype(ml_dtypes.bfloat16)
    sinb = np.sin(freqs.astype(np.float64)).astype(ml_dtypes.bfloat16)
    tri = np.tril(np.ones((128, 128), dtype=bool))
    m0 = np.where(tri, np.float32(0), np.float32(FMIN)).astype(np.float32)
    mask384 = np.concatenate(
        [m0, np.zeros((128, 128), dtype=np.float32), m0], axis=1)
    ident = np.eye(128, dtype=ml_dtypes.bfloat16)

    if "nc" not in _NC_CACHE:
        _NC_CACHE["nc"] = build_program()
    nc = _NC_CACHE["nc"]

    def pmaj(a, tile):
        nt = a.shape[0] // 128
        return np.ascontiguousarray(
            a.reshape(nt, 128, tile).swapaxes(0, 1).reshape(128, nt * tile))

    wqTs = pmaj(wqT, D)
    wkTs = pmaj(wkT, D)
    wvTs = pmaj(wvT, D)
    woTs = pmaj(woT, D)
    in_maps = []
    for c in range(NCORES):
        sl = slice(c * T, (c + 1) * T)
        wdc = wd_pad[:, 384 * c:384 * (c + 1)]
        wds = np.ascontiguousarray(
            wdc.reshape(8, 128, 3, 128).transpose(1, 2, 0, 3)
            .reshape(128, 3 * 8 * 128))
        in_maps.append({
            "x": pmaj(x[sl], D),
            "wqT": wqTs, "wkT": wkTs, "wvT": wvTs, "woT": woTs,
            "wg": pmaj(np.ascontiguousarray(wg[:, 128 * c:128 * (c + 1)]), 128),
            "wu": pmaj(np.ascontiguousarray(wu[:, 128 * c:128 * (c + 1)]), 128),
            "wd": wds,
            "cosb": pmaj(cosb[sl], 32),
            "sinb": pmaj(sinb[sl], 32),
            "gain128": gain128, "mask0": mask384, "ident": ident,
        })
    res = run_bass_kernel_spmd(nc, in_maps, core_ids=list(range(NCORES)))
    out = np.concatenate(
        [r["out"].reshape(128, NT, D).swapaxes(0, 1).reshape(T, D)
         for r in res.results], axis=0)
    return out.reshape(1, S, D).astype(np.float32)


# version 37
# speedup vs baseline: 1.0485x; 1.0485x over previous
import contextlib
import os
import numpy as np
import ml_dtypes
import concourse.bass as bass
import concourse.mybir as mybir
from concourse.bass_utils import run_bass_kernel_spmd

CHAIN_WAITS = (os.environ.get("BASS_SIM_FP64") == "1"
               or os.environ.get("KERNEL_CHAIN_WAITS") == "1")

F32 = mybir.dt.float32
BF16 = mybir.dt.bfloat16
AL = mybir.AluOpType
AF = mybir.ActivationFunctionType
AX = mybir.AxisListType

NCORES = 8
S, D = 16384, 1024
T = S // NCORES
NT = T // 128
H, HD = 16, 64
NWIN = T // 256
I_FFN = 2816
FMIN = float(np.finfo(np.float32).min)
M2 = [0.0625, 0.5625, 1.5625, 3.0625, 6.25, 12.25, 25.0]
WTS = [0.5, 0.5, 0.5, 0.5, 1.0, 1.0, 2.0]
RMS_EPS = 1.1920928955078125e-07

G_SZ = 128 * 2816
DN_SZ = 128 * 1024
SH = 2 * G_SZ + 3 * DN_SZ

ENGS = ("sync", "gpsimd", "tensor", "vector", "scalar")


class Sched:

    CHAIN_SEM = {"vector": "e_v", "scalar": "e_s", "gpsimd": "e_g"}

    def __init__(self, nc, sems, dpool):
        self.nc = nc
        self.sem = sems
        self.cnt = {k: 0 for k in sems}
        self.waited = {e: {} for e in ENGS}
        self.dpool = dpool
        self.didx = {e: 0 for e in dpool}
        self.q = None

    def phase(self):
        self.q = {e: [] for e in ENGS}

    def emit(self, eng, fn, waits=(), inc=None, amt=1, chain=True):
        waits = list(waits)
        cs = self.CHAIN_SEM.get(eng)
        if fn is not None and cs is not None:
            if chain and self.cnt.get(cs, 0):
                waits.append((cs, self.cnt[cs]))
            if inc is None:
                inc = cs
        flat = []
        for ev in waits:
            if isinstance(ev, list):
                flat.extend(ev)
            else:
                flat.append(ev)
        wd = self.waited[eng]
        ws = []
        for ev in flat:
            if ev is None:
                continue
            s, v = ev
            if v is None or v <= 0 or wd.get(s, 0) >= v:
                continue
            wd[s] = v
            ws.append((s, v))
        ev = None
        if inc is not None:
            self.cnt[inc] += amt
            ev = (inc, self.cnt[inc])

        def thunk(eo, fn=fn, ws=ws, inc=inc, amt=amt):
            for s, v in ws:
                eo.wait_ge(self.sem[s], v)
            ins = fn(eo) if fn is not None else None
            if inc is not None:
                if ins is None:
                    ins = eo.nop()
                ins.then_inc(self.sem[inc], amt)

        self.q[eng].append(thunk)
        return ev

    def emit_dma(self, eng, fn, waits=()):
        pool = self.dpool[eng]
        i = self.didx[eng]
        self.didx[eng] += 1
        s = pool[i % len(pool)]
        rounds = i // len(pool)
        w = list(waits) + [(s, 16 * rounds)]
        return self.emit(eng, fn, waits=w, inc=s, amt=16)

    def barrier(self):
        for eng, pool in self.dpool.items():
            dws = [(s, self.cnt[s]) for s in pool if self.cnt[s]]
            if dws:
                self.emit(eng, None, waits=dws)
        evs = [self.emit(e, None, inc="b_" + e) for e in ENGS]
        for e in ENGS:
            self.emit(e, None, waits=evs)

    def run_block(self):
        with self.nc.Block() as blk:
            for e in ENGS:
                q = self.q[e]
                if not q:
                    continue

                def runner(eo, q=q):
                    for t in q:
                        t(eo)

                getattr(blk, e)(runner)
        self.q = None


def emit_qd_head_part(P, xin, scr, F, c0, c1, par=0, in_ev=None,
                      rstd=None, rstd6=None, last=True):
    nb = F // 16
    b0, b1 = c0 // 16, c1 // 16
    bm, rsc = scr["bm"], scr["rsc"]
    sc_, rsq = scr["sc"][par], scr["rsq"][par]
    a2, sgn = scr["a2"][par], scr["sgn"][par]
    P.emit("vector", lambda eo, xin=xin, b0=b0, b1=b1, c0=c0, c1=c1:
           eo.tensor_reduce(
        bm[:, b0:b1], xin[:, c0:c1].rearrange("p (b k) -> p b k", k=16),
        AX.X, AL.max, apply_absolute_value=True), inc="e_v")
    if last:
        s6 = rstd6 if rstd6 is not None else 1.0 / 6.0
        P.emit("vector", lambda eo, nb=nb, sc_=sc_, s6=s6: eo.tensor_scalar(
            sc_[:, :nb], bm[:, :nb], s6, 1e-12, AL.mult, AL.max),
            waits=[scr["ev_sc_free"][par]], inc="e_v")
        P.emit("vector", lambda eo, nb=nb, sc_=sc_: eo.reciprocal(
            rsc[:, :nb], sc_[:, :nb]), inc="e_v")
        P.emit("vector", lambda eo, nb=nb, rsq=rsq: eo.tensor_tensor(
            rsq[:, :nb], rsc[:, :nb], rsc[:, :nb], AL.mult), inc="e_v")
    if rstd is not None:
        e_sq = P.emit("scalar", lambda eo, xin=xin, c0=c0, c1=c1, a2=a2,
                      rstd=rstd: eo.activation(
            a2[:, c0:c1], xin[:, c0:c1], AF.Square, scale=rstd),
            waits=[in_ev, scr["ev_cmp_free"][par] if c0 == 0 else None],
            inc="e_s")
    else:
        e_sq = P.emit("scalar", lambda eo, xin=xin, c0=c0, c1=c1, a2=a2:
                      eo.activation(a2[:, c0:c1], xin[:, c0:c1], AF.Square),
            waits=[in_ev, scr["ev_cmp_free"][par] if c0 == 0 else None],
            inc="e_s")
    e_sg = P.emit("scalar", lambda eo, xin=xin, c0=c0, c1=c1, sgn=sgn: eo.sign(
        sgn[:, c0:c1], xin[:, c0:c1]),
        waits=[scr["ev_sgn_free"][par] if c0 == 0 else None], inc="e_s")
    return e_sq, e_sg


def emit_qd_head(P, xin, scr, F, par=0, in_ev=None, rstd=None, rstd6=None):
    return emit_qd_head_part(P, xin, scr, F, 0, F, par=par, in_ev=in_ev,
                             rstd=rstd, rstd6=rstd6, last=True)


def emit_qd_tail(P, qout, scr, F, par=0, out_free=None, sq_ev=None,
                 sign_ev=None):
    nb = F // 16
    cb, g0 = scr["cb"][par], scr["g0"][par]
    sc_, rsq = scr["sc"][par], scr["rsq"][par]
    a2, sgn = scr["a2"][par], scr["sgn"][par]
    sA, gA = scr["sA"], scr["gA"]
    ev_a2 = P.emit("vector", lambda eo, F=F, nb=nb, a2=a2, rsq=rsq:
                   eo.tensor_tensor(
        a2[:, :F].rearrange("p (b k) -> p b k", k=16),
        a2[:, :F].rearrange("p (b k) -> p b k", k=16),
        rsq[:, :nb].unsqueeze(2).broadcast_to([128, nb, 16]), AL.mult),
        waits=[sq_ev, scr["ev_sgn_a2"][par]], inc="e_v")
    for i in range(5):
        dst = g0 if i == 0 else cb
        ev_ts = P.emit("vector", lambda eo, dst=dst, F=F, i=i: eo.tensor_scalar(
            dst[:, :F], a2[:, :F], M2[i], WTS[i], AL.is_gt, AL.mult),
            waits=[scr["ev_g0cb_free"][par]] if i <= 1 else (), inc="e_v")
        if i > 0:
            P.emit("vector", lambda eo, F=F: eo.tensor_tensor(
                g0[:, :F], g0[:, :F], cb[:, :F], AL.add), inc="e_v")
    scr["ev_cmp_free"][par] = ev_ts
    P.emit("scalar", lambda eo, F=F: eo.activation(
        sA[:, :F], a2[:, :F], AF.Sign, bias=scr["m2n"][:, 1:2]),
        waits=[ev_a2], inc="e_s")
    eR5 = P.emit("scalar", lambda eo, F=F: eo.activation(
        gA[:, :F], sA[:, :F], AF.Relu),
        waits=[scr["ev_gA_free"][0]], inc="e_s")
    evA = P.emit("vector", lambda eo, F=F: eo.tensor_tensor(
        g0[:, :F], g0[:, :F], gA[:, :F], AL.add), waits=[eR5], inc="e_v")
    gB = scr["gB"] if scr["gB"] is not None else gA
    gBw = scr["ev_gB_free"][0] if scr["gB"] is not None else evA
    P.emit("scalar", lambda eo, F=F: eo.activation(
        sA[:, :F], a2[:, :F], AF.Sign, bias=scr["m2n"][:, 2:3]), inc="e_s")
    eR6 = P.emit("scalar", lambda eo, F=F, gB=gB: eo.activation(
        gB[:, :F], sA[:, :F], AF.Relu, scale=2.0), waits=[gBw], inc="e_s")
    evC = P.emit("vector", lambda eo, F=F, gB=gB: eo.tensor_tensor(
        g0[:, :F], g0[:, :F], gB[:, :F], AL.add), waits=[eR6], inc="e_v")
    scr["ev_gA_free"][0] = evA if scr["gB"] is not None else evC
    scr["ev_gB_free"][0] = evC
    scr["ev_sgn_a2"][par] = eR6
    ev_cbm = P.emit("gpsimd", lambda eo, F=F, sgn=sgn: eo.tensor_tensor(
        cb[:, :F], g0[:, :F], sgn[:, :F], AL.mult),
        waits=[sign_ev, evC], inc="e_g")
    scr["ev_sgn_free"][par] = ev_cbm
    evq = P.emit("gpsimd", lambda eo, qout=qout, F=F, nb=nb, sc_=sc_:
                 eo.tensor_tensor(
        qout.rearrange("p (b k) -> p b k", k=16),
        cb[:, :F].rearrange("p (b k) -> p b k", k=16),
        sc_[:, :nb].unsqueeze(2).broadcast_to([128, nb, 16]), AL.mult),
        waits=[out_free], inc="e_g")
    scr["ev_g0cb_free"][par] = evq
    scr["ev_sc_free"][par] = evq
    return evq


def emit_qd(P, xin, qout, scr, F, out_free=None, in_ev=None):
    e_sq, e_sg = emit_qd_head(P, xin, scr, F, par=0, in_ev=in_ev)
    evq = emit_qd_tail(P, qout, scr, F, par=0, out_free=out_free,
                       sq_ev=e_sq, sign_ev=e_sg)
    return evq, e_sg


def build_program():
    nc = bass.Bass("TRN2", target_bir_lowering=False, debug=False)

    x_d = nc.dram_tensor("x", [128, NT * D], F32, kind="ExternalInput").ap()
    wqT_d = nc.dram_tensor("wqT", [128, 8 * D], F32, kind="ExternalInput").ap()
    wkT_d = nc.dram_tensor("wkT", [128, 8 * D], F32, kind="ExternalInput").ap()
    wvT_d = nc.dram_tensor("wvT", [128, 8 * D], F32, kind="ExternalInput").ap()
    woT_d = nc.dram_tensor("woT", [128, 8 * D], F32, kind="ExternalInput").ap()
    wg_d = nc.dram_tensor("wg", [128, I_FFN], F32, kind="ExternalInput").ap()
    wu_d = nc.dram_tensor("wu", [128, I_FFN], F32, kind="ExternalInput").ap()
    wd_d = nc.dram_tensor("wd", [128, 3 * 8 * 128], F32, kind="ExternalInput").ap()
    cos_d = nc.dram_tensor("cosb", [128, NT * 32], BF16, kind="ExternalInput").ap()
    sin_d = nc.dram_tensor("sinb", [128, NT * 32], BF16, kind="ExternalInput").ap()
    gain_d = nc.dram_tensor("gain128", [128, H], F32, kind="ExternalInput").ap()
    mask_d = nc.dram_tensor("mask0", [128, 384], F32, kind="ExternalInput").ap()
    id_d = nc.dram_tensor("ident", [128, 128], BF16, kind="ExternalInput").ap()
    out_d = nc.dram_tensor("out", [128, NT * D], F32, kind="ExternalOutput").ap()
    ag_in = nc.dram_tensor("ag_in", [SH], BF16).ap()
    wffnq_d = nc.dram_tensor("wffnq", [NCORES * SH], BF16,
                             addr_space="Shared").ap()
    xnqfm_d = nc.dram_tensor("xnqfm_dr", [128, 8 * T], BF16).ap()

    root = contextlib.ExitStack()
    sems = {}
    snames = (["d_out", "e_t", "e_v", "e_s", "e_g",
               "b_sync", "b_gpsimd", "b_tensor", "b_vector", "b_scalar"]
              + [f"ds{i}" for i in range(12)] + [f"dg{i}" for i in range(4)])
    for s in snames:
        sems[s] = root.enter_context(nc.semaphore(s))
    P = Sched(nc, sems, {"sync": [f"ds{i}" for i in range(12)],
                         "gpsimd": [f"dg{i}" for i in range(4)]})

    def sb(st, name, shape, dt):
        return st.enter_context(nc.sbuf_tensor(name, shape, dt))

    x_sb = sb(root, "x_sb", [128, NT, D], F32)
    stat = sb(root, "stat", [128, 16], F32)
    m2n = sb(root, "m2n", [128, 3], F32)
    gain_sb = sb(root, "gain_sb", [128, H], F32)
    mask_sb = sb(root, "mask_sb", [128, 384], F32)
    id_sb = sb(root, "id_sb", [128, 128], BF16)
    pm = [root.enter_context(nc.psum_tensor(f"pm{i}", [128, 512], F32))
          for i in range(4)]
    pt = [root.enter_context(nc.psum_tensor(f"pt{i}", [128, 1024], BF16))
          for i in range(2)]
    psc = [root.enter_context(nc.psum_tensor(f"psc{i}", [128, 512], F32))
           for i in range(2)]

    stA = contextlib.ExitStack()
    xnfm = sb(stA, "xnfm", [128, 8, T], BF16)
    wq_sb = sb(stA, "wq_sb", [128, 8, D], BF16)
    wk_sb = sb(stA, "wk_sb", [128, 8, D], BF16)
    wv_sb = sb(stA, "wv_sb", [128, 8, D], BF16)
    wo_sb = sb(stA, "wo_sb", [128, 8, D], BF16)
    cos_sb = sb(stA, "cos_sb", [128, NT, 32], BF16)
    sin_sb = sb(stA, "sin_sb", [128, NT, 32], BF16)

    def alloc_qds(st, F, tag="", npar=1, gb=False):
        return {
            "m2n": m2n,
            "bm": sb(st, f"qd_bm{tag}", [128, F // 16], F32),
            "rsc": sb(st, f"qd_rsc{tag}", [128, F // 16], F32),
            "a2": [sb(st, f"qd_a2{tag}{i}", [128, F], F32)
                   for i in range(npar)],
            "cb": [sb(st, f"qd_cb{tag}{i}", [128, F], BF16)
                   for i in range(npar)],
            "g0": [sb(st, f"qd_g0{tag}{i}", [128, F], BF16)
                   for i in range(npar)],
            "sc": [sb(st, f"qd_sc{tag}{i}", [128, F // 16], F32)
                   for i in range(npar)],
            "rsq": [sb(st, f"qd_rsq{tag}{i}", [128, F // 16], F32)
                    for i in range(npar)],
            "sgn": [sb(st, f"qd_sgn{tag}{i}", [128, F], BF16)
                    for i in range(npar)],
            "sA": sb(st, f"qd_sA{tag}", [128, F], BF16),
            "gA": sb(st, f"qd_gA{tag}", [128, F], BF16),
            "gB": sb(st, f"qd_gB{tag}", [128, F], BF16) if gb else None,
            "ev_gA_free": [None],
            "ev_gB_free": [None],
            "ev_g0cb_free": [None] * npar,
            "ev_sgn_free": [None] * npar,
            "ev_sc_free": [None] * npar,
            "ev_cmp_free": [None] * npar,
            "ev_sgn_a2": [None] * npar,
        }

    stB = contextlib.ExitStack()
    qds = alloc_qds(stB, 1024, npar=2)
    wsh = sb(stB, "wsh", [128, 1024], F32)
    qsh = sb(stB, "qsh", [128, 1024], BF16)
    fmo = sb(stB, "fmo", [128, 1024], BF16)
    xc = sb(stB, "xc", [128, D], F32)
    xnb = sb(stB, "xnb", [128, D], BF16)
    xnq = [sb(stB, f"xnq{i}", [128, D], BF16) for i in range(2)]
    xnqf = xnb

    P.phase()
    for _i in range(3):
        P.emit("vector", lambda eo, _i=_i: eo.memset(
            m2n[:, _i:_i + 1], -M2[4 + _i]), inc="e_v")
    ev_x = []
    for q in range(4):
        ev_x.append(P.emit_dma("sync", lambda eo, q=q: eo.dma_start(
            out=x_sb[:, 4 * q:4 * (q + 1), :],
            in_=x_d.rearrange("p (t d) -> p t d", d=D)[:, 4 * q:4 * (q + 1), :])))
    ev_cos = P.emit_dma("sync", lambda eo: eo.dma_start(
        out=cos_sb[:, :, :], in_=cos_d.rearrange("p (t j) -> p t j", j=32)))
    ev_sin = P.emit_dma("sync", lambda eo: eo.dma_start(
        out=sin_sb[:, :, :], in_=sin_d.rearrange("p (t j) -> p t j", j=32)))
    ev_gld = P.emit_dma("sync", lambda eo: eo.dma_start(out=gain_sb[:, :],
                                                        in_=gain_d))
    ev_mask = P.emit_dma("sync", lambda eo: eo.dma_start(out=mask_sb[:, :],
                                                         in_=mask_d))
    ev_id = P.emit_dma("sync", lambda eo: eo.dma_start(out=id_sb[:, :],
                                                       in_=id_d))
    ev_attnw = []
    attnw_pend = [(wqT_d, wq_sb), (wkT_d, wk_sb), (wvT_d, wv_sb),
                  (woT_d, wo_sb)]
    ev_gain = P.emit("vector", lambda eo: eo.tensor_scalar(
        gain_sb[:, :], gain_sb[:, :], 0.125, None, AL.mult),
        waits=[ev_gld], inc="e_v")

    st_wqd = {"evq": None, "esg": None, "pt": [None, None], "store": None,
              "ptsel": 0}

    def wqd_piece(src_ap, ntile, off, Wrow, col0):
        F = ntile * 128
        vsnap = ("e_v", P.cnt["e_v"])
        evd = P.emit_dma("sync", lambda eo, src_ap=src_ap, F=F: eo.dma_start(
            out=wsh[:, :F], in_=src_ap),
            waits=[st_wqd["evq"], st_wqd["esg"]])
        P.emit("vector", None, waits=[evd])
        e_sq, e_sg = emit_qd_head(P, wsh[:, :F], qds, F, par=0,
                                  in_ev=[evd, vsnap])
        st_wqd["evq"] = ("e_v", P.cnt["e_v"])
        st_wqd["esg"] = e_sg
        evq = emit_qd_tail(P, qsh[:, :F], qds, F, par=0, sq_ev=e_sq,
                           sign_ev=e_sg)
        bank = pt[0]
        evt = None
        for j in range(ntile):
            evt = P.emit("tensor", lambda eo, j=j, bank=bank: eo.transpose(
                bank[:, j * 128:(j + 1) * 128], qsh[:, j * 128:(j + 1) * 128],
                id_sb[:, :]), waits=[evq, ev_id, st_wqd["pt"][0]], inc="e_t")
        ev_ev = P.emit("vector", lambda eo, F=F, bank=bank: eo.tensor_copy(
            fmo[:, :F], bank[:, :F]), waits=[evt, st_wqd["store"]], inc="e_v")
        st_wqd["pt"][0] = ev_ev
        st_wqd["store"] = P.emit_dma("sync", lambda eo, off=off, Wrow=Wrow, F=F,
                                     col0=col0: eo.dma_start(
            out=ag_in[off:off + 128 * Wrow]
                .rearrange("(p i) -> p i", p=128)[:, col0:col0 + F],
            in_=fmo[:, :F]), waits=[ev_ev])
        return st_wqd["store"]

    ag_waits = []
    piece_args = []
    for (w_ap, roff) in ((wg_d, 0), (wu_d, G_SZ)):
        for (t0, ntile) in ((0, 8), (8, 8), (16, 6)):
            piece_args.append((
                w_ap[:, t0 * 128:(t0 + ntile) * 128],
                ntile, roff, 2816, t0 * 128))
    for pos in range(3):
        piece_args.append((
            wd_d[:, 1024 * pos:1024 * (pos + 1)],
            8, 2 * G_SZ + pos * DN_SZ, 1024, 0))

    ev_xnb_free = None
    ev_sg_prev = None
    ev_xnq_free = [None, None]
    ev_qf_free = None
    ln_pend = [None, None]

    def ln_flush():
        if ln_pend[1] is not None:
            ln_pend[1]()
            ln_pend[1] = None
        if ln_pend[0] is not None:
            ln_pend[1] = ln_pend[0]()
            ln_pend[0] = None

    for t in range(NT):
        par = t & 1
        if t >= 2 and t % 2 == 0 and attnw_pend:
            w_d, w_s = attnw_pend.pop(0)
            ev_attnw.append(P.emit_dma("gpsimd", lambda eo, w_d=w_d, w_s=w_s:
                                       eo.dma_start(
                out=w_s[:, :, :].rearrange("p c n -> p (c n)"), in_=w_d)))
        if piece_args:
            ln_flush()
            ag_waits.append(wqd_piece(*piece_args.pop(0)))
        P.emit("vector", lambda eo, t=t: eo.reduce_sum(
            stat[:, 0:1], x_sb[:, t, :], AX.X), waits=[ev_x[t // 4]], inc="e_v")
        P.emit("vector", lambda eo: eo.tensor_scalar(
            stat[:, 1:2], stat[:, 0:1], 1.0 / D, None, AL.mult), inc="e_v")
        ev = P.emit("vector", lambda eo, t=t: eo.tensor_scalar(
            xc[:, :], x_sb[:, t, :], stat[:, 1:2], None, AL.subtract),
            waits=[ev_sg_prev], inc="e_v")
        ev_acc = P.emit("scalar", lambda eo: eo.activation(
            qsh[:, :], xc[:, :], AF.Square, accum_out=stat[:, 2:3]),
            waits=[ev], inc="e_s")
        ln_flush()
        ev = P.emit("vector", lambda eo: eo.tensor_scalar(
            stat[:, 3:4], stat[:, 2:3], 1.0 / D, 1e-5, AL.mult, AL.add),
            waits=[ev_acc], inc="e_v")
        ev = P.emit("scalar", lambda eo: eo.activation(
            stat[:, 4:5], stat[:, 3:4], AF.Ln), waits=[ev], inc="e_s")
        ev = P.emit("scalar", lambda eo: eo.activation(
            stat[:, 5:6], stat[:, 4:5], AF.Exp, scale=-0.5), inc="e_s")
        P.emit("vector", lambda eo: eo.tensor_scalar(
            stat[:, 6:7], stat[:, 5:6], 1.0 / 6.0, None, AL.mult),
            waits=[ev], inc="e_v")
        ev_xnb = P.emit("scalar", lambda eo: eo.activation(
            xnb[:, :], xc[:, :], AF.Copy, scale=stat[:, 5:6]),
            waits=[ev_xnb_free, ev_qf_free], inc="e_s")
        e_sq, e_sg = emit_qd_head(P, xc[:, :], qds, D, par=par,
                                  in_ev=ev_sg_prev, rstd=stat[:, 5:6],
                                  rstd6=stat[:, 6:7])
        ev_sg_prev = e_sg
        evt0 = None
        for c in range(8):
            evt0 = P.emit("tensor", lambda eo, c=c: eo.transpose(
                pt[0][:, c * 128:(c + 1) * 128], xnb[:, c * 128:(c + 1) * 128],
                id_sb[:, :]),
                waits=[ev_xnb, ev_id, st_wqd["pt"][0]], inc="e_t")
        ev_xnb_free = evt0

        def mk_ln_pend(t=t, par=par, evt0=evt0, e_sq=e_sq, e_sg=e_sg):
            def do():
                ev_fm = P.emit("vector", lambda eo, t=t: eo.tensor_copy(
                    xnfm[:, :, 128 * t:128 * (t + 1)],
                    pt[0][:, :].rearrange("p (c k) -> p c k", k=128)),
                    waits=[evt0], inc="e_v")
                st_wqd["pt"][0] = ev_fm
                evq = emit_qd_tail(P, xnq[par][:, :], qds, D, par=par,
                                   out_free=ev_xnq_free[par], sq_ev=e_sq,
                                   sign_ev=e_sg)
                evt1 = None
                for c in range(8):
                    evt1 = P.emit("tensor", lambda eo, c=c, par=par:
                                  eo.transpose(
                        pt[1][:, c * 128:(c + 1) * 128],
                        xnq[par][:, c * 128:(c + 1) * 128], id_sb[:, :]),
                        waits=[evq, st_wqd["pt"][1]], inc="e_t")
                ev_xnq_free[par] = evt1

                def do_evac(evt1=evt1, t=t):
                    nonlocal ev_qf_free
                    ev_qf = P.emit("vector", lambda eo: eo.tensor_copy(
                        xnqf[:, :].rearrange("p (c k) -> p c k", k=128),
                        pt[1][:, :].rearrange("p (c k) -> p c k", k=128)),
                        waits=[evt1, ev_qf_free, ev_xnb_free], inc="e_v")
                    st_wqd["pt"][1] = ev_qf
                    ev_qf_free = P.emit_dma("sync", lambda eo, t=t: eo.dma_start(
                        out=xnqfm_d.rearrange("p (c s) -> p c s", s=T)
                            [:, :, 128 * t:128 * (t + 1)],
                        in_=xnqf[:, :].rearrange("p (c k) -> p c k", k=128)),
                        waits=[ev_qf])
                return do_evac
            return do
        ln_pend[0] = mk_ln_pend()

    for _ in range(2):
        ln_flush()
    ev_wst = P.emit("gpsimd", lambda eo: eo.collective_compute(
        "AllGather", AL.bypass, replica_groups=[list(range(NCORES))],
        ins=[ag_in.opt()], outs=[wffnq_d.opt()]),
        waits=ag_waits, inc="e_g")

    P.barrier()
    P.run_block()
    stB.close()

    stC = contextlib.ExitStack()
    qk_sb = [sb(stC, f"qk{i}", [128, D], BF16) for i in range(2)]
    qsw = sb(stC, "qsw", [128, D], BF16)
    qm1 = sb(stC, "qm1", [128, D], BF16)
    sq3 = sb(stC, "sq3", [128, D], BF16)
    rstd_b = sb(stC, "rstd_b", [128, H], BF16)
    qn_b = sb(stC, "qn_b", [128, 2, D], BF16)
    kn_b = sb(stC, "kn_b", [128, 2, D], BF16)
    v_sb = sb(stC, "v_sb", [128, 2, D], BF16)
    qfm = sb(stC, "qfm", [128, 8, 256], BF16)
    kfm = sb(stC, "kfm", [128, 8, 256], BF16)
    yfm = sb(stC, "yfm", [128, 8, 256], BF16)
    sc_sb = [sb(stC, f"sc{i}", [128, 384], F32) for i in range(2)]
    e_sb = [sb(stC, f"e{i}", [128, 384], BF16) for i in range(3)]
    p_sb = [sb(stC, f"p{i}", [128, 384], BF16) for i in range(2)]
    pT_sb = [sb(stC, f"pT{i}", [128, 384], BF16) for i in range(2)]
    ast = sb(stC, "ast", [128, 32], F32)

    P.phase()
    pmf = [None, None, None, None]
    ptf = [None, None]
    pscf = [None, None]
    sc_free = [None, None]
    e_free = [None] * 3
    p_free = [None] * 2
    pT_free = [None] * 2
    ev_qk_free = [None, None]
    ev_dst_free = None
    ev_v_used = None
    ev_yfm_used = None
    ev_sq3_free = None

    v4 = lambda ap: ap.rearrange("p (h o j) -> p h o j", o=2, j=32)
    wo_pend = None
    for w in range(NWIN):
        if wo_pend is not None:
            wo_pend()
            wo_pend = None
        rounds = [("q", 0), ("q", 1), ("k", 0), ("k", 1), ("v", 0), ("v", 1)]
        rp = [None, None, None]

        def rflush():
            a = rp[0]() if rp[0] is not None else None
            b = rp[1]() if rp[1] is not None else None
            if rp[2] is not None:
                rp[2]()
            rp[0], rp[1], rp[2] = None, a, b

        def mk_back(kind, ti, par, outs, w=w):
            isq = kind == "q"
            dstb = qn_b if isq else kn_b
            fmdst = qfm if isq else kfm
            t = 2 * w + ti

            def back():
                nonlocal ev_sq3_free
                ev1 = P.emit("scalar", lambda eo, par=par: eo.activation(
                    sq3[:, :], qk_sb[par][:, :], AF.Square),
                    waits=outs + [ev_sq3_free], inc="e_s")
                ev2 = P.emit("vector", lambda eo: eo.tensor_reduce(
                    stat[:, 0:16], sq3[:, :].rearrange("p (h d) -> p h d", d=HD),
                    AX.X, AL.add), waits=[ev1], inc="e_v")
                ev_sq3_free = ev2
                ev3 = P.emit("vector", lambda eo: eo.tensor_scalar(
                    stat[:, 0:16], stat[:, 0:16], 1.0 / HD, RMS_EPS,
                    AL.mult, AL.add), inc="e_v")
                P.emit("scalar", lambda eo: eo.activation(
                    stat[:, 0:16], stat[:, 0:16], AF.Ln), waits=[ev3], inc="e_s")
                ev5 = P.emit("scalar", lambda eo: eo.activation(
                    rstd_b[:, :], stat[:, 0:16], AF.Exp, scale=-0.5), inc="e_s")
                if isq:
                    ev5 = P.emit("vector", lambda eo: eo.tensor_tensor(
                        rstd_b[:, :], rstd_b[:, :], gain_sb[:, :], AL.mult),
                        waits=[ev5, ev_gain], inc="e_v")
                P.emit("vector", lambda eo, t=t, par=par: eo.tensor_tensor(
                    v4(qsw[:, :])[:, :, 0, :], v4(qk_sb[par][:, :])[:, :, 1, :],
                    sin_sb[:, t, :].unsqueeze(1).broadcast_to([128, H, 32]),
                    AL.mult), waits=[ev_sin], inc="e_v")
                P.emit("vector", lambda eo, t=t, par=par:
                       eo.scalar_tensor_tensor(
                    v4(qsw[:, :])[:, :, 1, :], v4(qk_sb[par][:, :])[:, :, 0, :],
                    -1.0,
                    sin_sb[:, t, :].unsqueeze(1).broadcast_to([128, H, 32]),
                    AL.mult, AL.mult), inc="e_v")
                ev_qk_free[par] = P.emit("vector", lambda eo, t=t, par=par:
                                         eo.tensor_tensor(
                    v4(qm1[:, :]), v4(qk_sb[par][:, :]),
                    cos_sb[:, t, :].unsqueeze(1).unsqueeze(2)
                    .broadcast_to([128, H, 2, 32]), AL.mult),
                    waits=[ev_cos], inc="e_v")
                P.emit("vector", lambda eo: eo.tensor_tensor(
                    qm1[:, :], qm1[:, :], qsw[:, :], AL.add), inc="e_v")
                ev8 = P.emit("vector", lambda eo, dstb=dstb, ti=ti:
                             eo.tensor_tensor(
                    dstb[:, ti, :].rearrange("p (h d) -> p h d", d=HD),
                    qm1[:, :].rearrange("p (h d) -> p h d", d=HD),
                    rstd_b[:, :].unsqueeze(2).broadcast_to([128, H, HD]),
                    AL.mult), waits=[ev5, ev_dst_free], inc="e_v")

                def transp():
                    nonlocal ev_dst_free
                    evt = None
                    for c in range(8):
                        evt = P.emit("tensor", lambda eo, c=c, ti=ti, dstb=dstb,
                                     par=par: eo.transpose(
                                         pt[par][:, c * 128:(c + 1) * 128],
                                         dstb[:, ti, c * 128:(c + 1) * 128],
                                         id_sb[:, :]),
                                     waits=[ev8, ptf[par]], inc="e_t")
                    ev_dst_free = evt

                    def fmevac():
                        ptf[par] = P.emit("vector", lambda eo, ti=ti,
                                          fmdst=fmdst, par=par: eo.tensor_copy(
                            fmdst[:, :, 128 * ti:128 * (ti + 1)],
                            pt[par][:, :].rearrange("p (c k) -> p c k", k=128)),
                            waits=[evt], inc="e_v")
                    return fmevac
                return transp
            return back

        for s in range(len(rounds) + 3):
            if s < len(rounds):
                kind, ti = rounds[s]
                par = s & 1
                banks = (2, 3) if par == 0 else (0, 1)
                wsb = wq_sb if kind == "q" else (wk_sb if kind == "k" else wv_sb)
                evms = []
                for nn in range(2):
                    bank = banks[nn]

                    def mms(eo, ti=ti, nn=nn, w=w, wsb=wsb, bank=bank):
                        ins = None
                        for c in range(8):
                            ins = eo.matmul(
                                pm[bank][:, :],
                                xnfm[:, c, (2 * w + ti) * 128:(2 * w + ti + 1) * 128],
                                wsb[:, c, 512 * nn:512 * (nn + 1)],
                                start=(c == 0), stop=(c == 7))
                        return ins
                    evms.append(P.emit("tensor", mms,
                                       waits=[ev_attnw, pmf[bank]], inc="e_t"))
            rflush()
            if s < len(rounds):
                outs = []
                for nn in range(2):
                    bank = banks[nn]
                    if kind == "v":
                        ev = P.emit("scalar", lambda eo, ti=ti, nn=nn, bank=bank:
                                    eo.copy(v_sb[:, ti, 512 * nn:512 * (nn + 1)],
                                            pm[bank][:, :]),
                                    waits=[evms[nn], ev_v_used], inc="e_s")
                    else:
                        ev = P.emit("scalar", lambda eo, nn=nn, bank=bank,
                                    par=par: eo.copy(
                                        qk_sb[par][:, 512 * nn:512 * (nn + 1)],
                                        pm[bank][:, :]),
                                    waits=[evms[nn], ev_qk_free[par]], inc="e_s")
                    pmf[bank] = ev
                    outs.append(ev)
                if kind != "v":
                    rp[0] = mk_back(kind, ti, par, outs)
        ev_qkfm = ("e_v", P.cnt["e_v"])
        ev_v = ("e_s", P.cnt["e_s"])

        exp_evs = [None] * 4
        score_evs = [None, None]

        def stage_a(h):
            par = h & 1
            c, off = h >> 1, 64 * (h & 1)
            score_evs[par] = P.emit("tensor", lambda eo, c=c, off=off, par=par: (
                eo.matmul(psc[par][:, 0:128], qfm[off:off + 64, c, 0:128],
                          kfm[off:off + 64, c, 0:128], start=True, stop=True,
                          tile_position=(off, 0)),
                eo.matmul(psc[par][:, 128:384], qfm[off:off + 64, c, 128:256],
                          kfm[off:off + 64, c, 0:256], start=True, stop=True,
                          tile_position=(off, 0)))[-1],
                waits=[ev_qkfm, pscf[par]], inc="e_t")

        def stage_b(h):
            par = h & 1
            p4 = h & 3
            ev_s = score_evs[par]
            ev_msk = P.emit("vector", lambda eo, par=par: eo.tensor_tensor(
                sc_sb[par][:, :], psc[par][:, 0:384], mask_sb[:, :], AL.add),
                waits=[ev_s, ev_mask, sc_free[par]], inc="e_v")
            pscf[par] = ev_msk
            ev_m0 = P.emit("vector", lambda eo, par=par, p4=p4: eo.reduce_max(
                ast[:, 8 * p4:8 * p4 + 1], sc_sb[par][:, 0:128], AX.X,
                negate=True), inc="e_v")
            ev_m1 = P.emit("vector", lambda eo, par=par, p4=p4: eo.reduce_max(
                ast[:, 8 * p4 + 1:8 * p4 + 2], sc_sb[par][:, 128:384], AX.X,
                negate=True), inc="e_v")
            p3 = h % 3
            P.emit("scalar", lambda eo, par=par, p4=p4, p3=p3: eo.activation(
                e_sb[p3][:, 0:128], sc_sb[par][:, 0:128], AF.Exp,
                bias=ast[:, 8 * p4:8 * p4 + 1],
                accum_out=ast[:, 8 * p4 + 2:8 * p4 + 3]),
                waits=[ev_m0, e_free[p3]], inc="e_s")
            e1 = P.emit("scalar", lambda eo, par=par, p4=p4, p3=p3: eo.activation(
                e_sb[p3][:, 128:384], sc_sb[par][:, 128:384], AF.Exp,
                bias=ast[:, 8 * p4 + 1:8 * p4 + 2],
                accum_out=ast[:, 8 * p4 + 3:8 * p4 + 4]),
                waits=[ev_m1], inc="e_s")
            sc_free[par] = e1
            exp_evs[p4] = e1

        def stage_c(h):
            nonlocal ev_v_used, ev_yfm_used
            par = h & 1
            p4 = h & 3
            p3 = h % 3
            c, off = h >> 1, 64 * (h & 1)
            P.emit("vector", lambda eo, p4=p4: eo.reciprocal(
                ast[:, 8 * p4 + 4:8 * p4 + 6],
                ast[:, 8 * p4 + 2:8 * p4 + 4]),
                waits=[exp_evs[p4]], inc="e_v")
            P.emit("vector", lambda eo, p4=p4, p3=p3, par=par: eo.tensor_scalar(
                p_sb[par][:, 0:128], e_sb[p3][:, 0:128],
                ast[:, 8 * p4 + 4:8 * p4 + 5], None, AL.mult),
                waits=[p_free[par]], inc="e_v")
            ev_p1 = P.emit("vector", lambda eo, p4=p4, p3=p3, par=par:
                           eo.tensor_scalar(
                p_sb[par][:, 128:384], e_sb[p3][:, 128:384],
                ast[:, 8 * p4 + 5:8 * p4 + 6], None, AL.mult), inc="e_v")
            e_free[p3] = ev_p1
            evt = P.emit("tensor", lambda eo, par=par: (
                eo.transpose(pt[par][:, 0:128], p_sb[par][:, 0:128], id_sb[:, :]),
                eo.transpose(pt[par][:, 128:256], p_sb[par][:, 128:256],
                             id_sb[:, :]),
                eo.transpose(pt[par][:, 256:384], p_sb[par][:, 256:384],
                             id_sb[:, :]))[-1],
                waits=[ev_p1, ptf[par]], inc="e_t")
            p_free[par] = evt
            ev_pT = P.emit("scalar", lambda eo, par=par: eo.copy(
                pT_sb[par][:, 0:384], pt[par][:, 0:384]),
                waits=[evt, pT_free[par]], inc="e_s")
            ptf[par] = ev_pT

            def pv(eo, h=h, off=off, par=par):
                eo.matmul(pm[par][off:off + 64, 0:256],
                          v_sb[:, 0, 64 * h:64 * h + 64], pT_sb[par][:, 0:256],
                          start=True, stop=False, tile_position=(0, off))
                return eo.matmul(pm[par][off:off + 64, 128:256],
                                 v_sb[:, 1, 64 * h:64 * h + 64],
                                 pT_sb[par][:, 256:384],
                                 start=False, stop=True, tile_position=(0, off))
            ev_y = P.emit("tensor", pv, waits=[ev_pT, ev_v, pmf[par]],
                          inc="e_t")
            pT_free[par] = ev_y
            ev_v_used = ev_y
            y_evs[par] = ev_y

        def stage_d(h):
            nonlocal ev_yfm_used
            par = h & 1
            c, off = h >> 1, 64 * (h & 1)
            if par == 0:
                ev_ye = P.emit("vector", lambda eo, c=c, off=off, par=par:
                               eo.tensor_copy(
                    yfm[off:off + 64, c, :], pm[par][off:off + 64, 0:256]),
                    waits=[y_evs[par], ev_yfm_used], inc="e_v")
            else:
                ev_ye = P.emit("scalar", lambda eo, c=c, off=off, par=par:
                               eo.copy(
                    yfm[off:off + 64, c, :], pm[par][off:off + 64, 0:256]),
                    waits=[y_evs[par], ev_yfm_used], inc="e_s")
            pmf[par] = ev_ye
            return ev_ye

        y_evs = [None, None]
        ev_yfm = None
        for s in range(H + 4):
            if s < H:
                stage_a(s)
            if 1 <= s <= H:
                stage_b(s - 1)
            if 3 <= s <= H + 2:
                stage_c(s - 3)
            if s >= 4:
                ev_yfm = stage_d(s - 4)

        def mk_wo(w=w, ev_yfm=ev_yfm):
            def do_wo():
                nonlocal ev_yfm_used
                for ti in range(2):
                    for nn in range(2):
                        par = nn

                        def mms(eo, ti=ti, nn=nn, par=par):
                            ins = None
                            for c in range(8):
                                ins = eo.matmul(
                                    psc[par][:, :],
                                    yfm[:, c, 128 * ti:128 * (ti + 1)],
                                    wo_sb[:, c, 512 * nn:512 * (nn + 1)],
                                    start=(c == 0), stop=(c == 7))
                            return ins
                        evm = P.emit("tensor", mms,
                                     waits=[ev_yfm, pscf[par]], inc="e_t")
                        ev_yfm_used = evm
                        pscf[par] = P.emit("vector", lambda eo, ti=ti, nn=nn,
                                           par=par, w=w: eo.tensor_tensor(
                                               x_sb[:, 2 * w + ti,
                                                    512 * nn:512 * (nn + 1)],
                                               x_sb[:, 2 * w + ti,
                                                    512 * nn:512 * (nn + 1)],
                                               psc[par][:, :], AL.add),
                                           waits=[evm], inc="e_v")
            return do_wo
        wo_pend = mk_wo()

    wo_pend()
    P.barrier()
    P.run_block()
    stC.close()
    stA.close()

    stD = contextlib.ExitStack()
    qds4 = alloc_qds(stD, 1024, tag="4", npar=2)
    xnqfm = sb(stD, "xnqfm", [128, 8, T], BF16)
    wgq_it = [sb(stD, f"wgq{i}", [128, 8, 512], BF16) for i in range(2)]
    wuq_it = [sb(stD, f"wuq{i}", [128, 8, 512], BF16) for i in range(2)]
    wdq_sb = sb(stD, "wdq_sb", [128, 8, 512], BF16)
    hqfm = sb(stD, "hqfm", [128, 8, T], BF16)
    h_f = [sb(stD, f"h_f{i}", [128, 1024], F32) for i in range(2)]
    hq_b = [sb(stD, f"hq_b{i}", [128, 1024], BF16) for i in range(2)]

    P.phase()
    ev_xnq = P.emit_dma("sync", lambda eo: eo.dma_start(
        out=xnqfm[:, :, :].rearrange("p c s -> p (c s)"), in_=xnqfm_d))

    pmf = [None, None, None, None]
    ptf4 = [None, None]
    h_free = [None, None]
    hq_free = [None, None]
    thirds = [(0, 8, [512, 512]), (8, 8, [512, 512]), (16, 6, [512, 256])]
    buf = 0
    ev_w_used = [None, None]
    pend = [None, None, None]

    dn_jobs = []
    dn_state = {"loaded": -1, "wd_free": None, "pmf": [None, None],
                "ld_evs": {}, "lastmm": {}, "par": 0}

    def emit_dn_load(k, gbase, ncp):
        evs = []
        for cj in range(ncp):
            j = gbase + cj
            off = (j // 3) * SH + 2 * G_SZ + (j % 3) * DN_SZ
            for nn in range(2):
                evs.append(P.emit_dma("sync", lambda eo, cj=cj, nn=nn, off=off:
                                      eo.dma_start(
                    out=wdq_sb[:, 2 * cj + nn, :],
                    in_=wffnq_d[off:off + DN_SZ]
                    .rearrange("(p i) -> p i", p=128)
                    [:, 512 * nn:512 * (nn + 1)]),
                    waits=[ev_wst, dn_state["wd_free"]]))
        dn_state["ld_evs"][k] = evs
        dn_state["loaded"] = k

    def push_dn_jobs(k, cbase, ncp, tte, ev_hq):
        for ttj in (tte, tte + 1):
            for nn in range(2):
                def dj(k=k, cbase=cbase, ncp=ncp, ttj=ttj, nn=nn, ev_hq=ev_hq):
                    par = dn_state["par"]
                    dn_state["par"] ^= 1

                    def dmm(eo, par=par):
                        ins = None
                        for cj in range(ncp):
                            ins = eo.matmul(psc[par][:, :],
                                            hqfm[:, cbase + cj,
                                                 128 * ttj:128 * (ttj + 1)],
                                            wdq_sb[:, 2 * cj + nn, :],
                                            start=(cj == 0),
                                            stop=(cj == ncp - 1))
                        return ins
                    evm = P.emit("tensor", dmm,
                                 waits=[ev_hq, dn_state["ld_evs"][k],
                                        dn_state["pmf"][par]], inc="e_t")
                    dn_state["lastmm"][k] = evm
                    dn_state["wd_free"] = evm
                    dn_state["pmf"][par] = P.emit(
                        "vector", lambda eo, ttj=ttj, nn=nn, par=par:
                        eo.tensor_tensor(
                            x_sb[:, ttj, 512 * nn:512 * (nn + 1)],
                            x_sb[:, ttj, 512 * nn:512 * (nn + 1)],
                            psc[par][:, :], AL.add),
                        waits=[evm], inc="e_v")
                dn_jobs.append((k, dj))

    def pop_dn_jobs(limit):
        n = 0
        while dn_jobs and dn_jobs[0][0] <= dn_state["loaded"] and n < limit:
            dn_jobs.pop(0)[1]()
            n += 1

    pend_evac = [None]

    def flush_pend():
        tc = pend[0]() if pend[0] is not None else None
        ec = pend[1]() if pend[1] is not None else None
        pend_evac[0] = pend[2]
        pend[0], pend[1], pend[2] = None, tc, ec

    def flush_evac():
        if pend_evac[0] is not None:
            pend_evac[0]()
            pend_evac[0] = None

    def drain_pend():
        for _ in range(3):
            flush_pend()
            flush_evac()

    kglob = -1
    part_info = []
    for (ch0, nch, itiles) in thirds:
        ioff = 0
        for it, iw in enumerate(itiles):
            kglob += 1
            part_info.append((ch0 + ioff // 128, (ioff // 128) % 8, iw // 128))
            if kglob > 0:
                emit_dn_load(kglob - 1, part_info[kglob - 1][0],
                             part_info[kglob - 1][2])
            gbuf, ubuf = wgq_it[buf], wuq_it[buf]
            ev_lg = []
            goff = ch0 * 128 + ioff
            for c in range(8):
                base = c * SH
                ev_lg.append(P.emit_dma("sync", lambda eo, base=base, goff=goff,
                                        iw=iw, c=c, gbuf=gbuf: eo.dma_start(
                    out=gbuf[:, c, :iw],
                    in_=wffnq_d[base:base + G_SZ]
                    .rearrange("(p i) -> p i", p=128)[:, goff:goff + iw]),
                    waits=[ev_wst, ev_w_used[buf]]))
                ev_lg.append(P.emit_dma("sync", lambda eo, base=base, goff=goff,
                                        iw=iw, c=c, ubuf=ubuf: eo.dma_start(
                    out=ubuf[:, c, :iw],
                    in_=wffnq_d[base + G_SZ:base + 2 * G_SZ]
                    .rearrange("(p i) -> p i", p=128)[:, goff:goff + iw]),
                    waits=[ev_wst, ev_w_used[buf]]))
            ev_ldw = ev_lg
            chunkbase = ioff // 128
            nblk = iw // 128
            ev_gu_last = None
            for tt in range(NT):
                par = tt & 1
                pp = (tt >> 1) & 1
                pg, pu = pm[par], pm[2 + par]
                if (tt & 1) == 0:
                    pop_dn_jobs(4)
                    flush_pend()

                def gumm(eo, tt=tt, iw=iw, gbuf=gbuf, ubuf=ubuf, pg=pg, pu=pu):
                    ins = None
                    for c in range(8):
                        eo.matmul(pg[:, :iw],
                                  xnqfm[:, c, 128 * tt:128 * (tt + 1)],
                                  gbuf[:, c, :iw],
                                  start=(c == 0), stop=(c == 7))
                        ins = eo.matmul(pu[:, :iw],
                                        xnqfm[:, c, 128 * tt:128 * (tt + 1)],
                                        ubuf[:, c, :iw],
                                        start=(c == 0), stop=(c == 7))
                    return ins
                evg = P.emit("tensor", gumm,
                             waits=[ev_ldw, ev_xnq, pmf[par], pmf[2 + par]],
                             inc="e_t")
                evu = evg
                ev_gu_last = evu
                ev_sl = P.emit("scalar", lambda eo, iw=iw, pg=pg,
                               pp=pp, tt=tt: eo.activation(
                    h_f[pp][:, iw * (tt & 1):iw * ((tt & 1) + 1)],
                    pg[:, :iw], AF.Silu),
                    waits=[evg, h_free[pp] if (tt & 1) == 0 else None],
                    inc="e_s")
                pmf[par] = ev_sl
                ev_h = P.emit("vector", lambda eo, iw=iw, pu=pu,
                              pp=pp, tt=tt: eo.tensor_tensor(
                    h_f[pp][:, iw * (tt & 1):iw * ((tt & 1) + 1)],
                    h_f[pp][:, iw * (tt & 1):iw * ((tt & 1) + 1)],
                    pu[:, :iw], AL.mult),
                    waits=[ev_sl, evu], inc="e_v")
                pmf[2 + par] = ev_h
                if (tt & 1) == 0:
                    continue
                FP = 2 * iw
                e_sq, e_sg = emit_qd_head(P, h_f[pp][:, :FP], qds4, FP,
                                          par=pp, in_ev=ev_h)
                h_free[pp] = e_sg
                flush_evac()

                def mk_pend(pp=pp, tte=tt - 1, nblk=nblk, chunkbase=chunkbase,
                            FP=FP, e_sq=e_sq, e_sg=e_sg, kg=kglob):
                    def do_tail():
                        evq = emit_qd_tail(P, hq_b[pp][:, :FP], qds4, FP,
                                           par=pp, out_free=hq_free[pp],
                                           sq_ev=e_sq, sign_ev=e_sg)

                        def do_transp(evq=evq):
                            evt = None
                            for jb in range(2 * nblk):
                                evt = P.emit("tensor", lambda eo, jb=jb, pp=pp:
                                             eo.transpose(
                                                 pt[pp][:, jb * 128:(jb + 1) * 128],
                                                 hq_b[pp][:, jb * 128:(jb + 1) * 128],
                                                 id_sb[:, :]),
                                             waits=[evq, ptf4[pp],
                                                    dn_state["lastmm"]
                                                    .get(kg - 2)],
                                             inc="e_t")
                            hq_free[pp] = evt

                            def do_evac(evt=evt):
                                ptf4[pp] = P.emit("scalar", lambda eo, tte=tte,
                                                  chunkbase=chunkbase,
                                                  nblk=nblk,
                                                  pp=pp: eo.copy(
                                    hqfm[:, chunkbase:chunkbase + nblk,
                                         128 * tte:128 * (tte + 2)]
                                    .rearrange("p c (o k) -> p o c k", o=2),
                                    pt[pp][:, :2 * nblk * 128]
                                    .rearrange("p (o c k) -> p o c k",
                                               o=2, k=128)),
                                    waits=[evt], inc="e_s")
                                push_dn_jobs(kg, chunkbase, nblk, tte,
                                             ptf4[pp])
                            return do_evac
                        return do_transp
                    return do_tail
                pend[0] = mk_pend()
            ev_w_used[buf] = ev_gu_last
            buf ^= 1
            ioff += iw
    drain_pend()
    pop_dn_jobs(10 ** 9)
    emit_dn_load(kglob, part_info[kglob][0], part_info[kglob][2])
    for p8 in range(8):
        pop_dn_jobs(4)
        evp = ("e_v", P.cnt["e_v"])
        P.emit("sync", lambda eo, p8=p8: eo.dma_start(
            out=out_d.rearrange("p (t d) -> p t d", d=D)[:, 2 * p8:2 * p8 + 2, :],
            in_=x_sb[:, 2 * p8:2 * p8 + 2, :]),
            waits=[evp], inc="d_out", amt=16)
    pop_dn_jobs(10 ** 9)
    P.emit("sync", None, waits=[("d_out", 128)])
    P.barrier()
    P.run_block()
    stD.close()
    root.close()
    return nc


_NC_CACHE = {}


def kernel(L0_attn_x, L0_attn_wq, L0_attn_wk, L0_attn_wv, L0_attn_wo,
           L0_attn_q_gain, L0_ffn_w_gate, L0_ffn_w_up, L0_ffn_w_down):
    f32 = lambda a: np.ascontiguousarray(np.asarray(a, dtype=np.float32))
    x = f32(L0_attn_x).reshape(S, D)
    wqT = np.ascontiguousarray(f32(L0_attn_wq).T)
    wkT = np.ascontiguousarray(f32(L0_attn_wk).T)
    wvT = np.ascontiguousarray(f32(L0_attn_wv).T)
    woT = np.ascontiguousarray(f32(L0_attn_wo).T)
    wg = f32(L0_ffn_w_gate)
    wu = f32(L0_ffn_w_up)
    wd = f32(L0_ffn_w_down)
    wd_pad = np.concatenate(
        [wd, np.zeros((D, 3 * 128 * NCORES - I_FFN), dtype=np.float32)], axis=1)
    gain128 = np.ascontiguousarray(np.tile(f32(L0_attn_q_gain)[None, :],
                                           (128, 1)))

    inv_freq = (1.0 / 10000.0 ** (np.arange(0, HD, 2, dtype=np.float32)
                                  / np.float32(HD))).astype(np.float32)
    freqs = np.outer(np.arange(S, dtype=np.float32), inv_freq).astype(np.float32)
    cosb = np.cos(freqs.astype(np.float64)).astype(ml_dtypes.bfloat16)
    sinb = np.sin(freqs.astype(np.float64)).astype(ml_dtypes.bfloat16)
    tri = np.tril(np.ones((128, 128), dtype=bool))
    m0 = np.where(tri, np.float32(0), np.float32(FMIN)).astype(np.float32)
    mask384 = np.concatenate(
        [m0, np.zeros((128, 128), dtype=np.float32), m0], axis=1)
    ident = np.eye(128, dtype=ml_dtypes.bfloat16)

    if "nc" not in _NC_CACHE:
        _NC_CACHE["nc"] = build_program()
    nc = _NC_CACHE["nc"]

    def pmaj(a, tile):
        nt = a.shape[0] // 128
        return np.ascontiguousarray(
            a.reshape(nt, 128, tile).swapaxes(0, 1).reshape(128, nt * tile))

    wqTs = pmaj(wqT, D)
    wkTs = pmaj(wkT, D)
    wvTs = pmaj(wvT, D)
    woTs = pmaj(woT, D)
    in_maps = []
    for c in range(NCORES):
        sl = slice(c * T, (c + 1) * T)
        wdc = wd_pad[:, 384 * c:384 * (c + 1)]
        wds = np.ascontiguousarray(
            wdc.reshape(8, 128, 3, 128).transpose(1, 2, 0, 3)
            .reshape(128, 3 * 8 * 128))
        in_maps.append({
            "x": pmaj(x[sl], D),
            "wqT": wqTs, "wkT": wkTs, "wvT": wvTs, "woT": woTs,
            "wg": pmaj(np.ascontiguousarray(wg[:, 128 * c:128 * (c + 1)]), 128),
            "wu": pmaj(np.ascontiguousarray(wu[:, 128 * c:128 * (c + 1)]), 128),
            "wd": wds,
            "cosb": pmaj(cosb[sl], 32),
            "sinb": pmaj(sinb[sl], 32),
            "gain128": gain128, "mask0": mask384, "ident": ident,
        })
    res = run_bass_kernel_spmd(nc, in_maps, core_ids=list(range(NCORES)))
    out = np.concatenate(
        [r["out"].reshape(128, NT, D).swapaxes(0, 1).reshape(T, D)
         for r in res.results], axis=0)
    return out.reshape(1, S, D).astype(np.float32)


# version 38
# speedup vs baseline: 1.0513x; 1.0027x over previous
import contextlib
import os
import numpy as np
import ml_dtypes
import concourse.bass as bass
import concourse.mybir as mybir
from concourse.bass_utils import run_bass_kernel_spmd

CHAIN_WAITS = (os.environ.get("BASS_SIM_FP64") == "1"
               or os.environ.get("KERNEL_CHAIN_WAITS") == "1")

F32 = mybir.dt.float32
BF16 = mybir.dt.bfloat16
AL = mybir.AluOpType
AF = mybir.ActivationFunctionType
AX = mybir.AxisListType

NCORES = 8
S, D = 16384, 1024
T = S // NCORES
NT = T // 128
H, HD = 16, 64
NWIN = T // 256
I_FFN = 2816
FMIN = float(np.finfo(np.float32).min)
M2 = [0.0625, 0.5625, 1.5625, 3.0625, 6.25, 12.25, 25.0]
WTS = [0.5, 0.5, 0.5, 0.5, 1.0, 1.0, 2.0]
RMS_EPS = 1.1920928955078125e-07

G_SZ = 128 * 2816
DN_SZ = 128 * 1024
SH = 2 * G_SZ + 3 * DN_SZ

ENGS = ("sync", "gpsimd", "tensor", "vector", "scalar")


class Sched:

    CHAIN_SEM = {"vector": "e_v", "scalar": "e_s", "gpsimd": "e_g"}

    def __init__(self, nc, sems, dpool):
        self.nc = nc
        self.sem = sems
        self.cnt = {k: 0 for k in sems}
        self.waited = {e: {} for e in ENGS}
        self.dpool = dpool
        self.didx = {e: 0 for e in dpool}
        self.q = None

    def phase(self):
        self.q = {e: [] for e in ENGS}

    def emit(self, eng, fn, waits=(), inc=None, amt=1, chain=True):
        waits = list(waits)
        cs = self.CHAIN_SEM.get(eng)
        if fn is not None and cs is not None:
            if chain and self.cnt.get(cs, 0):
                waits.append((cs, self.cnt[cs]))
            if inc is None:
                inc = cs
        flat = []
        for ev in waits:
            if isinstance(ev, list):
                flat.extend(ev)
            else:
                flat.append(ev)
        wd = self.waited[eng]
        ws = []
        for ev in flat:
            if ev is None:
                continue
            s, v = ev
            if v is None or v <= 0 or wd.get(s, 0) >= v:
                continue
            wd[s] = v
            ws.append((s, v))
        ev = None
        if inc is not None:
            self.cnt[inc] += amt
            ev = (inc, self.cnt[inc])

        def thunk(eo, fn=fn, ws=ws, inc=inc, amt=amt):
            for s, v in ws:
                eo.wait_ge(self.sem[s], v)
            ins = fn(eo) if fn is not None else None
            if inc is not None:
                if ins is None:
                    ins = eo.nop()
                ins.then_inc(self.sem[inc], amt)

        self.q[eng].append(thunk)
        return ev

    def emit_dma(self, eng, fn, waits=()):
        pool = self.dpool[eng]
        i = self.didx[eng]
        self.didx[eng] += 1
        s = pool[i % len(pool)]
        rounds = i // len(pool)
        w = list(waits) + [(s, 16 * rounds)]
        return self.emit(eng, fn, waits=w, inc=s, amt=16)

    def barrier(self):
        for eng, pool in self.dpool.items():
            dws = [(s, self.cnt[s]) for s in pool if self.cnt[s]]
            if dws:
                self.emit(eng, None, waits=dws)
        evs = [self.emit(e, None, inc="b_" + e) for e in ENGS]
        for e in ENGS:
            self.emit(e, None, waits=evs)

    def run_block(self):
        with self.nc.Block() as blk:
            for e in ENGS:
                q = self.q[e]
                if not q:
                    continue

                def runner(eo, q=q):
                    for t in q:
                        t(eo)

                getattr(blk, e)(runner)
        self.q = None


def emit_qd_head_part(P, xin, scr, F, c0, c1, par=0, in_ev=None,
                      rstd=None, rstd6=None, last=True):
    nb = F // 16
    b0, b1 = c0 // 16, c1 // 16
    bm, rsc = scr["bm"], scr["rsc"]
    sc_, rsq = scr["sc"][par], scr["rsq"][par]
    a2, sgn = scr["a2"][par], scr["sgn"][par]
    P.emit("vector", lambda eo, xin=xin, b0=b0, b1=b1, c0=c0, c1=c1:
           eo.tensor_reduce(
        bm[:, b0:b1], xin[:, c0:c1].rearrange("p (b k) -> p b k", k=16),
        AX.X, AL.max, apply_absolute_value=True), inc="e_v")
    if last:
        s6 = rstd6 if rstd6 is not None else 1.0 / 6.0
        P.emit("vector", lambda eo, nb=nb, sc_=sc_, s6=s6: eo.tensor_scalar(
            sc_[:, :nb], bm[:, :nb], s6, 1e-12, AL.mult, AL.max),
            waits=[scr["ev_sc_free"][par]], inc="e_v")
        P.emit("vector", lambda eo, nb=nb, sc_=sc_: eo.reciprocal(
            rsc[:, :nb], sc_[:, :nb]), inc="e_v")
        P.emit("vector", lambda eo, nb=nb, rsq=rsq: eo.tensor_tensor(
            rsq[:, :nb], rsc[:, :nb], rsc[:, :nb], AL.mult), inc="e_v")
    if rstd is not None:
        e_sq = P.emit("scalar", lambda eo, xin=xin, c0=c0, c1=c1, a2=a2,
                      rstd=rstd: eo.activation(
            a2[:, c0:c1], xin[:, c0:c1], AF.Square, scale=rstd),
            waits=[in_ev, scr["ev_cmp_free"][par] if c0 == 0 else None],
            inc="e_s")
    else:
        e_sq = P.emit("scalar", lambda eo, xin=xin, c0=c0, c1=c1, a2=a2:
                      eo.activation(a2[:, c0:c1], xin[:, c0:c1], AF.Square),
            waits=[in_ev, scr["ev_cmp_free"][par] if c0 == 0 else None],
            inc="e_s")
    e_sg = P.emit("scalar", lambda eo, xin=xin, c0=c0, c1=c1, sgn=sgn: eo.sign(
        sgn[:, c0:c1], xin[:, c0:c1]),
        waits=[scr["ev_sgn_free"][par] if c0 == 0 else None], inc="e_s")
    return e_sq, e_sg


def emit_qd_head(P, xin, scr, F, par=0, in_ev=None, rstd=None, rstd6=None):
    return emit_qd_head_part(P, xin, scr, F, 0, F, par=par, in_ev=in_ev,
                             rstd=rstd, rstd6=rstd6, last=True)


def emit_qd_tail(P, qout, scr, F, par=0, out_free=None, sq_ev=None,
                 sign_ev=None):
    nb = F // 16
    cb, g0 = scr["cb"][par], scr["g0"][par]
    sc_, rsq = scr["sc"][par], scr["rsq"][par]
    a2, sgn = scr["a2"][par], scr["sgn"][par]
    sA, gA = scr["sA"], scr["gA"]
    ev_a2 = P.emit("vector", lambda eo, F=F, nb=nb, a2=a2, rsq=rsq:
                   eo.tensor_tensor(
        a2[:, :F].rearrange("p (b k) -> p b k", k=16),
        a2[:, :F].rearrange("p (b k) -> p b k", k=16),
        rsq[:, :nb].unsqueeze(2).broadcast_to([128, nb, 16]), AL.mult),
        waits=[sq_ev, scr["ev_sgn_a2"][par]], inc="e_v")
    for i in range(5):
        dst = g0 if i == 0 else cb
        ev_ts = P.emit("vector", lambda eo, dst=dst, F=F, i=i: eo.tensor_scalar(
            dst[:, :F], a2[:, :F], M2[i], WTS[i], AL.is_gt, AL.mult),
            waits=[scr["ev_g0cb_free"][par]] if i <= 1 else (), inc="e_v")
        if i > 0:
            P.emit("vector", lambda eo, F=F: eo.tensor_tensor(
                g0[:, :F], g0[:, :F], cb[:, :F], AL.add), inc="e_v")
    scr["ev_cmp_free"][par] = ev_ts
    P.emit("scalar", lambda eo, F=F: eo.activation(
        sA[:, :F], a2[:, :F], AF.Sign, bias=scr["m2n"][:, 1:2]),
        waits=[ev_a2], inc="e_s")
    eR5 = P.emit("scalar", lambda eo, F=F: eo.activation(
        gA[:, :F], sA[:, :F], AF.Relu),
        waits=[scr["ev_gA_free"][0]], inc="e_s")
    evA = P.emit("vector", lambda eo, F=F: eo.tensor_tensor(
        g0[:, :F], g0[:, :F], gA[:, :F], AL.add), waits=[eR5], inc="e_v")
    gB = scr["gB"] if scr["gB"] is not None else gA
    gBw = scr["ev_gB_free"][0] if scr["gB"] is not None else evA
    P.emit("scalar", lambda eo, F=F: eo.activation(
        sA[:, :F], a2[:, :F], AF.Sign, bias=scr["m2n"][:, 2:3]), inc="e_s")
    eR6 = P.emit("scalar", lambda eo, F=F, gB=gB: eo.activation(
        gB[:, :F], sA[:, :F], AF.Relu, scale=2.0), waits=[gBw], inc="e_s")
    evC = P.emit("vector", lambda eo, F=F, gB=gB: eo.tensor_tensor(
        g0[:, :F], g0[:, :F], gB[:, :F], AL.add), waits=[eR6], inc="e_v")
    scr["ev_gA_free"][0] = evA if scr["gB"] is not None else evC
    scr["ev_gB_free"][0] = evC
    scr["ev_sgn_a2"][par] = eR6
    ev_cbm = P.emit("gpsimd", lambda eo, F=F, sgn=sgn: eo.tensor_tensor(
        cb[:, :F], g0[:, :F], sgn[:, :F], AL.mult),
        waits=[sign_ev, evC], inc="e_g")
    scr["ev_sgn_free"][par] = ev_cbm
    evq = P.emit("gpsimd", lambda eo, qout=qout, F=F, nb=nb, sc_=sc_:
                 eo.tensor_tensor(
        qout.rearrange("p (b k) -> p b k", k=16),
        cb[:, :F].rearrange("p (b k) -> p b k", k=16),
        sc_[:, :nb].unsqueeze(2).broadcast_to([128, nb, 16]), AL.mult),
        waits=[out_free], inc="e_g")
    scr["ev_g0cb_free"][par] = evq
    scr["ev_sc_free"][par] = evq
    return evq


def emit_qd(P, xin, qout, scr, F, out_free=None, in_ev=None):
    e_sq, e_sg = emit_qd_head(P, xin, scr, F, par=0, in_ev=in_ev)
    evq = emit_qd_tail(P, qout, scr, F, par=0, out_free=out_free,
                       sq_ev=e_sq, sign_ev=e_sg)
    return evq, e_sg


def build_program():
    nc = bass.Bass("TRN2", target_bir_lowering=False, debug=False)

    x_d = nc.dram_tensor("x", [128, NT * D], F32, kind="ExternalInput").ap()
    wqT_d = nc.dram_tensor("wqT", [128, 8 * D], F32, kind="ExternalInput").ap()
    wkT_d = nc.dram_tensor("wkT", [128, 8 * D], F32, kind="ExternalInput").ap()
    wvT_d = nc.dram_tensor("wvT", [128, 8 * D], F32, kind="ExternalInput").ap()
    woT_d = nc.dram_tensor("woT", [128, 8 * D], F32, kind="ExternalInput").ap()
    wg_d = nc.dram_tensor("wg", [128, I_FFN], F32, kind="ExternalInput").ap()
    wu_d = nc.dram_tensor("wu", [128, I_FFN], F32, kind="ExternalInput").ap()
    wd_d = nc.dram_tensor("wd", [128, 3 * 8 * 128], F32, kind="ExternalInput").ap()
    cos_d = nc.dram_tensor("cosb", [128, NT * 32], BF16, kind="ExternalInput").ap()
    sin_d = nc.dram_tensor("sinb", [128, NT * 32], BF16, kind="ExternalInput").ap()
    gain_d = nc.dram_tensor("gain128", [128, H], F32, kind="ExternalInput").ap()
    mask_d = nc.dram_tensor("mask0", [128, 384], F32, kind="ExternalInput").ap()
    id_d = nc.dram_tensor("ident", [128, 128], BF16, kind="ExternalInput").ap()
    out_d = nc.dram_tensor("out", [128, NT * D], F32, kind="ExternalOutput").ap()
    ag_in = nc.dram_tensor("ag_in", [SH], BF16).ap()
    wffnq_d = nc.dram_tensor("wffnq", [NCORES * SH], BF16,
                             addr_space="Shared").ap()
    xnqfm_d = nc.dram_tensor("xnqfm_dr", [128, 8 * T], BF16).ap()

    root = contextlib.ExitStack()
    sems = {}
    snames = (["d_out", "e_t", "e_v", "e_s", "e_g",
               "b_sync", "b_gpsimd", "b_tensor", "b_vector", "b_scalar"]
              + [f"ds{i}" for i in range(12)] + [f"dg{i}" for i in range(4)])
    for s in snames:
        sems[s] = root.enter_context(nc.semaphore(s))
    P = Sched(nc, sems, {"sync": [f"ds{i}" for i in range(12)],
                         "gpsimd": [f"dg{i}" for i in range(4)]})

    def sb(st, name, shape, dt):
        return st.enter_context(nc.sbuf_tensor(name, shape, dt))

    x_sb = sb(root, "x_sb", [128, NT, D], F32)
    stat = sb(root, "stat", [128, 16], F32)
    m2n = sb(root, "m2n", [128, 3], F32)
    gain_sb = sb(root, "gain_sb", [128, H], F32)
    mask_sb = sb(root, "mask_sb", [128, 384], F32)
    id_sb = sb(root, "id_sb", [128, 128], BF16)
    pm = [root.enter_context(nc.psum_tensor(f"pm{i}", [128, 512], F32))
          for i in range(4)]
    pt = [root.enter_context(nc.psum_tensor(f"pt{i}", [128, 1024], BF16))
          for i in range(2)]
    psc = [root.enter_context(nc.psum_tensor(f"psc{i}", [128, 512], F32))
           for i in range(2)]

    stA = contextlib.ExitStack()
    xnfm = sb(stA, "xnfm", [128, 8, T], BF16)
    wq_sb = sb(stA, "wq_sb", [128, 8, D], BF16)
    wk_sb = sb(stA, "wk_sb", [128, 8, D], BF16)
    wv_sb = sb(stA, "wv_sb", [128, 8, D], BF16)
    wo_sb = sb(stA, "wo_sb", [128, 8, D], BF16)
    cos_sb = sb(stA, "cos_sb", [128, NT, 32], BF16)
    sin_sb = sb(stA, "sin_sb", [128, NT, 32], BF16)

    def alloc_qds(st, F, tag="", npar=1, gb=False):
        return {
            "m2n": m2n,
            "bm": sb(st, f"qd_bm{tag}", [128, F // 16], F32),
            "rsc": sb(st, f"qd_rsc{tag}", [128, F // 16], F32),
            "a2": [sb(st, f"qd_a2{tag}{i}", [128, F], F32)
                   for i in range(npar)],
            "cb": [sb(st, f"qd_cb{tag}{i}", [128, F], BF16)
                   for i in range(npar)],
            "g0": [sb(st, f"qd_g0{tag}{i}", [128, F], BF16)
                   for i in range(npar)],
            "sc": [sb(st, f"qd_sc{tag}{i}", [128, F // 16], F32)
                   for i in range(npar)],
            "rsq": [sb(st, f"qd_rsq{tag}{i}", [128, F // 16], F32)
                    for i in range(npar)],
            "sgn": [sb(st, f"qd_sgn{tag}{i}", [128, F], BF16)
                    for i in range(npar)],
            "sA": sb(st, f"qd_sA{tag}", [128, F], BF16),
            "gA": sb(st, f"qd_gA{tag}", [128, F], BF16),
            "gB": sb(st, f"qd_gB{tag}", [128, F], BF16) if gb else None,
            "ev_gA_free": [None],
            "ev_gB_free": [None],
            "ev_g0cb_free": [None] * npar,
            "ev_sgn_free": [None] * npar,
            "ev_sc_free": [None] * npar,
            "ev_cmp_free": [None] * npar,
            "ev_sgn_a2": [None] * npar,
        }

    stB = contextlib.ExitStack()
    qds = alloc_qds(stB, 1024, npar=2)
    wsh = sb(stB, "wsh", [128, 1024], F32)
    qsh = sb(stB, "qsh", [128, 1024], BF16)
    fmo = sb(stB, "fmo", [128, 1024], BF16)
    xc = sb(stB, "xc", [128, D], F32)
    xnb = sb(stB, "xnb", [128, D], BF16)
    xnq = [sb(stB, f"xnq{i}", [128, D], BF16) for i in range(2)]
    xnqf = xnb

    P.phase()
    for _i in range(3):
        P.emit("vector", lambda eo, _i=_i: eo.memset(
            m2n[:, _i:_i + 1], -M2[4 + _i]), inc="e_v")
    ev_x = []
    for q in range(4):
        ev_x.append(P.emit_dma("sync", lambda eo, q=q: eo.dma_start(
            out=x_sb[:, 4 * q:4 * (q + 1), :],
            in_=x_d.rearrange("p (t d) -> p t d", d=D)[:, 4 * q:4 * (q + 1), :])))
    ev_cos = P.emit_dma("sync", lambda eo: eo.dma_start(
        out=cos_sb[:, :, :], in_=cos_d.rearrange("p (t j) -> p t j", j=32)))
    ev_sin = P.emit_dma("sync", lambda eo: eo.dma_start(
        out=sin_sb[:, :, :], in_=sin_d.rearrange("p (t j) -> p t j", j=32)))
    ev_gld = P.emit_dma("sync", lambda eo: eo.dma_start(out=gain_sb[:, :],
                                                        in_=gain_d))
    ev_mask = P.emit_dma("sync", lambda eo: eo.dma_start(out=mask_sb[:, :],
                                                         in_=mask_d))
    ev_id = P.emit_dma("sync", lambda eo: eo.dma_start(out=id_sb[:, :],
                                                       in_=id_d))
    ev_attnw = []
    attnw_pend = [(wqT_d, wq_sb), (wkT_d, wk_sb), (wvT_d, wv_sb),
                  (woT_d, wo_sb)]
    ev_gain = P.emit("vector", lambda eo: eo.tensor_scalar(
        gain_sb[:, :], gain_sb[:, :], 0.125, None, AL.mult),
        waits=[ev_gld], inc="e_v")

    st_wqd = {"evq": None, "esg": None, "pt": [None, None], "store": None,
              "ptsel": 0}

    def wqd_piece(src_ap, ntile, off, Wrow, col0):
        F = ntile * 128
        vsnap = ("e_v", P.cnt["e_v"])
        evd = P.emit_dma("sync", lambda eo, src_ap=src_ap, F=F: eo.dma_start(
            out=wsh[:, :F], in_=src_ap),
            waits=[st_wqd["evq"], st_wqd["esg"]])
        P.emit("vector", None, waits=[evd])
        e_sq, e_sg = emit_qd_head(P, wsh[:, :F], qds, F, par=0,
                                  in_ev=[evd, vsnap])
        st_wqd["evq"] = ("e_v", P.cnt["e_v"])
        st_wqd["esg"] = e_sg
        evq = emit_qd_tail(P, qsh[:, :F], qds, F, par=0, sq_ev=e_sq,
                           sign_ev=e_sg)
        bank = pt[0]
        evt = None
        for j in range(ntile):
            evt = P.emit("tensor", lambda eo, j=j, bank=bank: eo.transpose(
                bank[:, j * 128:(j + 1) * 128], qsh[:, j * 128:(j + 1) * 128],
                id_sb[:, :]), waits=[evq, ev_id, st_wqd["pt"][0]], inc="e_t")
        ev_ev = P.emit("vector", lambda eo, F=F, bank=bank: eo.tensor_copy(
            fmo[:, :F], bank[:, :F]), waits=[evt, st_wqd["store"]], inc="e_v")
        st_wqd["pt"][0] = ev_ev
        st_wqd["store"] = P.emit_dma("sync", lambda eo, off=off, Wrow=Wrow, F=F,
                                     col0=col0: eo.dma_start(
            out=ag_in[off:off + 128 * Wrow]
                .rearrange("(p i) -> p i", p=128)[:, col0:col0 + F],
            in_=fmo[:, :F]), waits=[ev_ev])
        return st_wqd["store"]

    ag_waits = []
    piece_args = []
    for (w_ap, roff) in ((wg_d, 0), (wu_d, G_SZ)):
        for (t0, ntile) in ((0, 8), (8, 8), (16, 6)):
            piece_args.append((
                w_ap[:, t0 * 128:(t0 + ntile) * 128],
                ntile, roff, 2816, t0 * 128))
    for pos in range(3):
        piece_args.append((
            wd_d[:, 1024 * pos:1024 * (pos + 1)],
            8, 2 * G_SZ + pos * DN_SZ, 1024, 0))

    ev_xnb_free = None
    ev_sg_prev = None
    ev_xnq_free = [None, None]
    ev_qf_free = None
    ln_pend = [None, None]

    def ln_flush():
        if ln_pend[1] is not None:
            ln_pend[1]()
            ln_pend[1] = None
        if ln_pend[0] is not None:
            ln_pend[1] = ln_pend[0]()
            ln_pend[0] = None

    for t in range(NT):
        par = t & 1
        if t >= 2 and t % 2 == 0 and attnw_pend:
            w_d, w_s = attnw_pend.pop(0)
            ev_attnw.append(P.emit_dma("gpsimd", lambda eo, w_d=w_d, w_s=w_s:
                                       eo.dma_start(
                out=w_s[:, :, :].rearrange("p c n -> p (c n)"), in_=w_d)))
        if piece_args:
            ln_flush()
            ag_waits.append(wqd_piece(*piece_args.pop(0)))
        ev_vsnap = ("e_v", P.cnt["e_v"])
        ev_cp = P.emit("scalar", lambda eo, t=t: eo.activation(
            xc[:, :], x_sb[:, t, :], AF.Copy, accum_out=stat[:, 0:1]),
            waits=[ev_x[t // 4], ev_vsnap], inc="e_s")
        ev_nm = P.emit("vector", lambda eo: eo.tensor_scalar(
            stat[:, 1:2], stat[:, 0:1], -1.0 / D, None, AL.mult),
            waits=[ev_cp], inc="e_v")
        P.emit("scalar", lambda eo: eo.activation(
            xc[:, :], xc[:, :], AF.Identity, bias=stat[:, 1:2]),
            waits=[ev_nm], inc="e_s")
        ev_acc = P.emit("scalar", lambda eo: eo.activation(
            qsh[:, :], xc[:, :], AF.Square, accum_out=stat[:, 2:3]),
            inc="e_s")
        ln_flush()
        ev = P.emit("vector", lambda eo: eo.tensor_scalar(
            stat[:, 3:4], stat[:, 2:3], 1.0 / D, 1e-5, AL.mult, AL.add),
            waits=[ev_acc], inc="e_v")
        ev = P.emit("scalar", lambda eo: eo.activation(
            stat[:, 4:5], stat[:, 3:4], AF.Ln), waits=[ev], inc="e_s")
        ev = P.emit("scalar", lambda eo: eo.activation(
            stat[:, 5:6], stat[:, 4:5], AF.Exp, scale=-0.5), inc="e_s")
        P.emit("vector", lambda eo: eo.tensor_scalar(
            stat[:, 6:7], stat[:, 5:6], 1.0 / 6.0, None, AL.mult),
            waits=[ev], inc="e_v")
        ev_xnb = P.emit("scalar", lambda eo: eo.activation(
            xnb[:, :], xc[:, :], AF.Copy, scale=stat[:, 5:6]),
            waits=[ev_xnb_free, ev_qf_free], inc="e_s")
        e_sq, e_sg = emit_qd_head(P, xc[:, :], qds, D, par=par,
                                  in_ev=ev_sg_prev, rstd=stat[:, 5:6],
                                  rstd6=stat[:, 6:7])
        ev_sg_prev = e_sg
        evt0 = None
        for c in range(8):
            evt0 = P.emit("tensor", lambda eo, c=c: eo.transpose(
                pt[0][:, c * 128:(c + 1) * 128], xnb[:, c * 128:(c + 1) * 128],
                id_sb[:, :]),
                waits=[ev_xnb, ev_id, st_wqd["pt"][0]], inc="e_t")
        ev_xnb_free = evt0

        def mk_ln_pend(t=t, par=par, evt0=evt0, e_sq=e_sq, e_sg=e_sg):
            def do():
                ev_fm = P.emit("vector", lambda eo, t=t: eo.tensor_copy(
                    xnfm[:, :, 128 * t:128 * (t + 1)],
                    pt[0][:, :].rearrange("p (c k) -> p c k", k=128)),
                    waits=[evt0], inc="e_v")
                st_wqd["pt"][0] = ev_fm
                evq = emit_qd_tail(P, xnq[par][:, :], qds, D, par=par,
                                   out_free=ev_xnq_free[par], sq_ev=e_sq,
                                   sign_ev=e_sg)
                evt1 = None
                for c in range(8):
                    evt1 = P.emit("tensor", lambda eo, c=c, par=par:
                                  eo.transpose(
                        pt[1][:, c * 128:(c + 1) * 128],
                        xnq[par][:, c * 128:(c + 1) * 128], id_sb[:, :]),
                        waits=[evq, st_wqd["pt"][1]], inc="e_t")
                ev_xnq_free[par] = evt1

                def do_evac(evt1=evt1, t=t):
                    nonlocal ev_qf_free
                    ev_qf = P.emit("vector", lambda eo: eo.tensor_copy(
                        xnqf[:, :].rearrange("p (c k) -> p c k", k=128),
                        pt[1][:, :].rearrange("p (c k) -> p c k", k=128)),
                        waits=[evt1, ev_qf_free, ev_xnb_free], inc="e_v")
                    st_wqd["pt"][1] = ev_qf
                    ev_qf_free = P.emit_dma("sync", lambda eo, t=t: eo.dma_start(
                        out=xnqfm_d.rearrange("p (c s) -> p c s", s=T)
                            [:, :, 128 * t:128 * (t + 1)],
                        in_=xnqf[:, :].rearrange("p (c k) -> p c k", k=128)),
                        waits=[ev_qf])
                return do_evac
            return do
        ln_pend[0] = mk_ln_pend()

    for _ in range(2):
        ln_flush()
    ev_wst = P.emit("gpsimd", lambda eo: eo.collective_compute(
        "AllGather", AL.bypass, replica_groups=[list(range(NCORES))],
        ins=[ag_in.opt()], outs=[wffnq_d.opt()]),
        waits=ag_waits, inc="e_g")

    P.barrier()
    P.run_block()
    stB.close()

    stC = contextlib.ExitStack()
    qk_sb = [sb(stC, f"qk{i}", [128, D], BF16) for i in range(2)]
    qsw = sb(stC, "qsw", [128, D], BF16)
    qm1 = sb(stC, "qm1", [128, D], BF16)
    sq3 = sb(stC, "sq3", [128, D], BF16)
    rstd_b = sb(stC, "rstd_b", [128, H], BF16)
    qn_b = sb(stC, "qn_b", [128, 2, D], BF16)
    kn_b = sb(stC, "kn_b", [128, 2, D], BF16)
    v_sb = sb(stC, "v_sb", [128, 2, D], BF16)
    qfm = sb(stC, "qfm", [128, 8, 256], BF16)
    kfm = sb(stC, "kfm", [128, 8, 256], BF16)
    yfm = sb(stC, "yfm", [128, 8, 256], BF16)
    sc_sb = [sb(stC, f"sc{i}", [128, 384], F32) for i in range(2)]
    e_sb = [sb(stC, f"e{i}", [128, 384], BF16) for i in range(3)]
    p_sb = [sb(stC, f"p{i}", [128, 384], BF16) for i in range(2)]
    pT_sb = [sb(stC, f"pT{i}", [128, 384], BF16) for i in range(2)]
    ast = sb(stC, "ast", [128, 32], F32)

    P.phase()
    pmf = [None, None, None, None]
    ptf = [None, None]
    pscf = [None, None]
    sc_free = [None, None]
    e_free = [None] * 3
    p_free = [None] * 2
    pT_free = [None] * 2
    ev_qk_free = [None, None]
    ev_dst_free = None
    ev_v_used = None
    ev_yfm_used = None
    ev_sq3_free = None

    v4 = lambda ap: ap.rearrange("p (h o j) -> p h o j", o=2, j=32)
    wo_pend = None
    for w in range(NWIN):
        if wo_pend is not None:
            wo_pend()
            wo_pend = None
        rounds = [("q", 0), ("q", 1), ("k", 0), ("k", 1), ("v", 0), ("v", 1)]
        rp = [None, None, None]

        def rflush():
            a = rp[0]() if rp[0] is not None else None
            b = rp[1]() if rp[1] is not None else None
            if rp[2] is not None:
                rp[2]()
            rp[0], rp[1], rp[2] = None, a, b

        def mk_back(kind, ti, par, outs, w=w):
            isq = kind == "q"
            dstb = qn_b if isq else kn_b
            fmdst = qfm if isq else kfm
            t = 2 * w + ti

            def back():
                nonlocal ev_sq3_free
                ev1 = P.emit("scalar", lambda eo, par=par: eo.activation(
                    sq3[:, :], qk_sb[par][:, :], AF.Square),
                    waits=outs + [ev_sq3_free], inc="e_s")
                ev2 = P.emit("vector", lambda eo: eo.tensor_reduce(
                    stat[:, 0:16], sq3[:, :].rearrange("p (h d) -> p h d", d=HD),
                    AX.X, AL.add), waits=[ev1], inc="e_v")
                ev_sq3_free = ev2
                ev3 = P.emit("vector", lambda eo: eo.tensor_scalar(
                    stat[:, 0:16], stat[:, 0:16], 1.0 / HD, RMS_EPS,
                    AL.mult, AL.add), inc="e_v")
                P.emit("scalar", lambda eo: eo.activation(
                    stat[:, 0:16], stat[:, 0:16], AF.Ln), waits=[ev3], inc="e_s")
                ev5 = P.emit("scalar", lambda eo: eo.activation(
                    rstd_b[:, :], stat[:, 0:16], AF.Exp, scale=-0.5), inc="e_s")
                if isq:
                    ev5 = P.emit("vector", lambda eo: eo.tensor_tensor(
                        rstd_b[:, :], rstd_b[:, :], gain_sb[:, :], AL.mult),
                        waits=[ev5, ev_gain], inc="e_v")
                P.emit("vector", lambda eo, t=t, par=par: eo.tensor_tensor(
                    v4(qsw[:, :])[:, :, 0, :], v4(qk_sb[par][:, :])[:, :, 1, :],
                    sin_sb[:, t, :].unsqueeze(1).broadcast_to([128, H, 32]),
                    AL.mult), waits=[ev_sin], inc="e_v")
                P.emit("vector", lambda eo, t=t, par=par:
                       eo.scalar_tensor_tensor(
                    v4(qsw[:, :])[:, :, 1, :], v4(qk_sb[par][:, :])[:, :, 0, :],
                    -1.0,
                    sin_sb[:, t, :].unsqueeze(1).broadcast_to([128, H, 32]),
                    AL.mult, AL.mult), inc="e_v")
                ev_qk_free[par] = P.emit("vector", lambda eo, t=t, par=par:
                                         eo.tensor_tensor(
                    v4(qm1[:, :]), v4(qk_sb[par][:, :]),
                    cos_sb[:, t, :].unsqueeze(1).unsqueeze(2)
                    .broadcast_to([128, H, 2, 32]), AL.mult),
                    waits=[ev_cos], inc="e_v")
                P.emit("vector", lambda eo: eo.tensor_tensor(
                    qm1[:, :], qm1[:, :], qsw[:, :], AL.add), inc="e_v")
                ev8 = P.emit("vector", lambda eo, dstb=dstb, ti=ti:
                             eo.tensor_tensor(
                    dstb[:, ti, :].rearrange("p (h d) -> p h d", d=HD),
                    qm1[:, :].rearrange("p (h d) -> p h d", d=HD),
                    rstd_b[:, :].unsqueeze(2).broadcast_to([128, H, HD]),
                    AL.mult), waits=[ev5, ev_dst_free], inc="e_v")

                def transp():
                    nonlocal ev_dst_free
                    evt = None
                    for c in range(8):
                        evt = P.emit("tensor", lambda eo, c=c, ti=ti, dstb=dstb,
                                     par=par: eo.transpose(
                                         pt[par][:, c * 128:(c + 1) * 128],
                                         dstb[:, ti, c * 128:(c + 1) * 128],
                                         id_sb[:, :]),
                                     waits=[ev8, ptf[par]], inc="e_t")
                    ev_dst_free = evt

                    def fmevac():
                        ptf[par] = P.emit("vector", lambda eo, ti=ti,
                                          fmdst=fmdst, par=par: eo.tensor_copy(
                            fmdst[:, :, 128 * ti:128 * (ti + 1)],
                            pt[par][:, :].rearrange("p (c k) -> p c k", k=128)),
                            waits=[evt], inc="e_v")
                    return fmevac
                return transp
            return back

        for s in range(len(rounds) + 3):
            if s < len(rounds):
                kind, ti = rounds[s]
                par = s & 1
                banks = (2, 3) if par == 0 else (0, 1)
                wsb = wq_sb if kind == "q" else (wk_sb if kind == "k" else wv_sb)
                evms = []
                for nn in range(2):
                    bank = banks[nn]

                    def mms(eo, ti=ti, nn=nn, w=w, wsb=wsb, bank=bank):
                        ins = None
                        for c in range(8):
                            ins = eo.matmul(
                                pm[bank][:, :],
                                xnfm[:, c, (2 * w + ti) * 128:(2 * w + ti + 1) * 128],
                                wsb[:, c, 512 * nn:512 * (nn + 1)],
                                start=(c == 0), stop=(c == 7))
                        return ins
                    evms.append(P.emit("tensor", mms,
                                       waits=[ev_attnw, pmf[bank]], inc="e_t"))
            rflush()
            if s < len(rounds):
                outs = []
                for nn in range(2):
                    bank = banks[nn]
                    if kind == "v":
                        ev = P.emit("scalar", lambda eo, ti=ti, nn=nn, bank=bank:
                                    eo.copy(v_sb[:, ti, 512 * nn:512 * (nn + 1)],
                                            pm[bank][:, :]),
                                    waits=[evms[nn], ev_v_used], inc="e_s")
                    else:
                        ev = P.emit("scalar", lambda eo, nn=nn, bank=bank,
                                    par=par: eo.copy(
                                        qk_sb[par][:, 512 * nn:512 * (nn + 1)],
                                        pm[bank][:, :]),
                                    waits=[evms[nn], ev_qk_free[par]], inc="e_s")
                    pmf[bank] = ev
                    outs.append(ev)
                if kind != "v":
                    rp[0] = mk_back(kind, ti, par, outs)
        ev_qkfm = ("e_v", P.cnt["e_v"])
        ev_v = ("e_s", P.cnt["e_s"])

        exp_evs = [None] * 4
        score_evs = [None, None]

        def stage_a(h):
            par = h & 1
            c, off = h >> 1, 64 * (h & 1)
            score_evs[par] = P.emit("tensor", lambda eo, c=c, off=off, par=par: (
                eo.matmul(psc[par][:, 0:128], qfm[off:off + 64, c, 0:128],
                          kfm[off:off + 64, c, 0:128], start=True, stop=True,
                          tile_position=(off, 0)),
                eo.matmul(psc[par][:, 128:384], qfm[off:off + 64, c, 128:256],
                          kfm[off:off + 64, c, 0:256], start=True, stop=True,
                          tile_position=(off, 0)))[-1],
                waits=[ev_qkfm, pscf[par]], inc="e_t")

        def stage_b(h):
            par = h & 1
            p4 = h & 3
            ev_s = score_evs[par]
            ev_msk = P.emit("vector", lambda eo, par=par: eo.tensor_tensor(
                sc_sb[par][:, :], psc[par][:, 0:384], mask_sb[:, :], AL.add),
                waits=[ev_s, ev_mask, sc_free[par]], inc="e_v")
            pscf[par] = ev_msk
            ev_m0 = P.emit("vector", lambda eo, par=par, p4=p4: eo.reduce_max(
                ast[:, 8 * p4:8 * p4 + 1], sc_sb[par][:, 0:128], AX.X,
                negate=True), inc="e_v")
            ev_m1 = P.emit("vector", lambda eo, par=par, p4=p4: eo.reduce_max(
                ast[:, 8 * p4 + 1:8 * p4 + 2], sc_sb[par][:, 128:384], AX.X,
                negate=True), inc="e_v")
            p3 = h % 3
            P.emit("scalar", lambda eo, par=par, p4=p4, p3=p3: eo.activation(
                e_sb[p3][:, 0:128], sc_sb[par][:, 0:128], AF.Exp,
                bias=ast[:, 8 * p4:8 * p4 + 1],
                accum_out=ast[:, 8 * p4 + 2:8 * p4 + 3]),
                waits=[ev_m0, e_free[p3]], inc="e_s")
            e1 = P.emit("scalar", lambda eo, par=par, p4=p4, p3=p3: eo.activation(
                e_sb[p3][:, 128:384], sc_sb[par][:, 128:384], AF.Exp,
                bias=ast[:, 8 * p4 + 1:8 * p4 + 2],
                accum_out=ast[:, 8 * p4 + 3:8 * p4 + 4]),
                waits=[ev_m1], inc="e_s")
            sc_free[par] = e1
            exp_evs[p4] = e1

        def stage_c(h):
            nonlocal ev_v_used, ev_yfm_used
            par = h & 1
            p4 = h & 3
            p3 = h % 3
            c, off = h >> 1, 64 * (h & 1)
            P.emit("vector", lambda eo, p4=p4: eo.reciprocal(
                ast[:, 8 * p4 + 4:8 * p4 + 6],
                ast[:, 8 * p4 + 2:8 * p4 + 4]),
                waits=[exp_evs[p4]], inc="e_v")
            P.emit("vector", lambda eo, p4=p4, p3=p3, par=par: eo.tensor_scalar(
                p_sb[par][:, 0:128], e_sb[p3][:, 0:128],
                ast[:, 8 * p4 + 4:8 * p4 + 5], None, AL.mult),
                waits=[p_free[par]], inc="e_v")
            ev_p1 = P.emit("vector", lambda eo, p4=p4, p3=p3, par=par:
                           eo.tensor_scalar(
                p_sb[par][:, 128:384], e_sb[p3][:, 128:384],
                ast[:, 8 * p4 + 5:8 * p4 + 6], None, AL.mult), inc="e_v")
            e_free[p3] = ev_p1
            evt = P.emit("tensor", lambda eo, par=par: (
                eo.transpose(pt[par][:, 0:128], p_sb[par][:, 0:128], id_sb[:, :]),
                eo.transpose(pt[par][:, 128:256], p_sb[par][:, 128:256],
                             id_sb[:, :]),
                eo.transpose(pt[par][:, 256:384], p_sb[par][:, 256:384],
                             id_sb[:, :]))[-1],
                waits=[ev_p1, ptf[par]], inc="e_t")
            p_free[par] = evt
            ev_pT = P.emit("scalar", lambda eo, par=par: eo.copy(
                pT_sb[par][:, 0:384], pt[par][:, 0:384]),
                waits=[evt, pT_free[par]], inc="e_s")
            ptf[par] = ev_pT

            def pv(eo, h=h, off=off, par=par):
                eo.matmul(pm[par][off:off + 64, 0:256],
                          v_sb[:, 0, 64 * h:64 * h + 64], pT_sb[par][:, 0:256],
                          start=True, stop=False, tile_position=(0, off))
                return eo.matmul(pm[par][off:off + 64, 128:256],
                                 v_sb[:, 1, 64 * h:64 * h + 64],
                                 pT_sb[par][:, 256:384],
                                 start=False, stop=True, tile_position=(0, off))
            ev_y = P.emit("tensor", pv, waits=[ev_pT, ev_v, pmf[par]],
                          inc="e_t")
            pT_free[par] = ev_y
            ev_v_used = ev_y
            y_evs[par] = ev_y

        def stage_d(h):
            nonlocal ev_yfm_used
            par = h & 1
            c, off = h >> 1, 64 * (h & 1)
            if par == 0:
                ev_ye = P.emit("vector", lambda eo, c=c, off=off, par=par:
                               eo.tensor_copy(
                    yfm[off:off + 64, c, :], pm[par][off:off + 64, 0:256]),
                    waits=[y_evs[par], ev_yfm_used], inc="e_v")
            else:
                ev_ye = P.emit("scalar", lambda eo, c=c, off=off, par=par:
                               eo.copy(
                    yfm[off:off + 64, c, :], pm[par][off:off + 64, 0:256]),
                    waits=[y_evs[par], ev_yfm_used], inc="e_s")
            pmf[par] = ev_ye
            yfm_evs[par] = ev_ye
            return ev_ye

        y_evs = [None, None]
        yfm_evs = [None, None]
        ev_yfm = None
        for s in range(H + 4):
            if s < H:
                stage_a(s)
            if 1 <= s <= H:
                stage_b(s - 1)
            if 3 <= s <= H + 2:
                stage_c(s - 3)
            if s >= 4:
                ev_yfm = stage_d(s - 4)

        def mk_wo(w=w, ev_yfm=list(yfm_evs)):
            def do_wo():
                nonlocal ev_yfm_used
                for ti in range(2):
                    for nn in range(2):
                        par = nn

                        def mms(eo, ti=ti, nn=nn, par=par):
                            ins = None
                            for c in range(8):
                                ins = eo.matmul(
                                    psc[par][:, :],
                                    yfm[:, c, 128 * ti:128 * (ti + 1)],
                                    wo_sb[:, c, 512 * nn:512 * (nn + 1)],
                                    start=(c == 0), stop=(c == 7))
                            return ins
                        evm = P.emit("tensor", mms,
                                     waits=[ev_yfm, pscf[par]], inc="e_t")
                        ev_yfm_used = evm
                        pscf[par] = P.emit("vector", lambda eo, ti=ti, nn=nn,
                                           par=par, w=w: eo.tensor_tensor(
                                               x_sb[:, 2 * w + ti,
                                                    512 * nn:512 * (nn + 1)],
                                               x_sb[:, 2 * w + ti,
                                                    512 * nn:512 * (nn + 1)],
                                               psc[par][:, :], AL.add),
                                           waits=[evm], inc="e_v")
            return do_wo
        wo_pend = mk_wo()

    wo_pend()
    P.barrier()
    P.run_block()
    stC.close()
    stA.close()

    stD = contextlib.ExitStack()
    qds4 = alloc_qds(stD, 1024, tag="4", npar=2)
    xnqfm = sb(stD, "xnqfm", [128, 8, T], BF16)
    wgq_it = [sb(stD, f"wgq{i}", [128, 8, 512], BF16) for i in range(2)]
    wuq_it = [sb(stD, f"wuq{i}", [128, 8, 512], BF16) for i in range(2)]
    wdq_sb = sb(stD, "wdq_sb", [128, 8, 512], BF16)
    hqfm = sb(stD, "hqfm", [128, 8, T], BF16)
    h_f = [sb(stD, f"h_f{i}", [128, 1024], F32) for i in range(2)]
    hq_b = [sb(stD, f"hq_b{i}", [128, 1024], BF16) for i in range(2)]

    P.phase()
    ev_xnq = P.emit_dma("sync", lambda eo: eo.dma_start(
        out=xnqfm[:, :, :].rearrange("p c s -> p (c s)"), in_=xnqfm_d))

    pmf = [None, None, None, None]
    ptf4 = [None, None]
    h_free = [None, None]
    hq_free = [None, None]
    thirds = [(0, 8, [512, 512]), (8, 8, [512, 512]), (16, 6, [512, 256])]
    buf = 0
    ev_w_used = [None, None]
    pend = [None, None, None]

    dn_jobs = []
    dn_state = {"loaded": -1, "wd_free": None, "pmf": [None, None],
                "ld_evs": {}, "lastmm": {}, "par": 0}

    def emit_dn_load(k, gbase, ncp):
        evs = []
        for cj in range(ncp):
            j = gbase + cj
            off = (j // 3) * SH + 2 * G_SZ + (j % 3) * DN_SZ
            for nn in range(2):
                evs.append(P.emit_dma("sync", lambda eo, cj=cj, nn=nn, off=off:
                                      eo.dma_start(
                    out=wdq_sb[:, 2 * cj + nn, :],
                    in_=wffnq_d[off:off + DN_SZ]
                    .rearrange("(p i) -> p i", p=128)
                    [:, 512 * nn:512 * (nn + 1)]),
                    waits=[ev_wst, dn_state["wd_free"]]))
        dn_state["ld_evs"][k] = evs
        dn_state["loaded"] = k

    def push_dn_jobs(k, cbase, ncp, tte, ev_hq):
        for ttj in (tte, tte + 1):
            for nn in range(2):
                def dj(k=k, cbase=cbase, ncp=ncp, ttj=ttj, nn=nn, ev_hq=ev_hq):
                    par = dn_state["par"]
                    dn_state["par"] ^= 1

                    def dmm(eo, par=par):
                        ins = None
                        for cj in range(ncp):
                            ins = eo.matmul(psc[par][:, :],
                                            hqfm[:, cbase + cj,
                                                 128 * ttj:128 * (ttj + 1)],
                                            wdq_sb[:, 2 * cj + nn, :],
                                            start=(cj == 0),
                                            stop=(cj == ncp - 1))
                        return ins
                    evm = P.emit("tensor", dmm,
                                 waits=[ev_hq, dn_state["ld_evs"][k],
                                        dn_state["pmf"][par]], inc="e_t")
                    dn_state["lastmm"][k] = evm
                    dn_state["wd_free"] = evm
                    dn_state["pmf"][par] = P.emit(
                        "vector", lambda eo, ttj=ttj, nn=nn, par=par:
                        eo.tensor_tensor(
                            x_sb[:, ttj, 512 * nn:512 * (nn + 1)],
                            x_sb[:, ttj, 512 * nn:512 * (nn + 1)],
                            psc[par][:, :], AL.add),
                        waits=[evm], inc="e_v")
                dn_jobs.append((k, dj))

    def pop_dn_jobs(limit):
        n = 0
        while dn_jobs and dn_jobs[0][0] <= dn_state["loaded"] and n < limit:
            dn_jobs.pop(0)[1]()
            n += 1

    pend_evac = [None]

    def flush_pend():
        tc = pend[0]() if pend[0] is not None else None
        ec = pend[1]() if pend[1] is not None else None
        pend_evac[0] = pend[2]
        pend[0], pend[1], pend[2] = None, tc, ec

    def flush_evac():
        if pend_evac[0] is not None:
            pend_evac[0]()
            pend_evac[0] = None

    def drain_pend():
        for _ in range(3):
            flush_pend()
            flush_evac()

    kglob = -1
    part_info = []
    for (ch0, nch, itiles) in thirds:
        ioff = 0
        for it, iw in enumerate(itiles):
            kglob += 1
            part_info.append((ch0 + ioff // 128, (ioff // 128) % 8, iw // 128))
            if kglob > 0:
                emit_dn_load(kglob - 1, part_info[kglob - 1][0],
                             part_info[kglob - 1][2])
            gbuf, ubuf = wgq_it[buf], wuq_it[buf]
            ev_lg = []
            goff = ch0 * 128 + ioff
            for c in range(8):
                base = c * SH
                ev_lg.append(P.emit_dma("sync", lambda eo, base=base, goff=goff,
                                        iw=iw, c=c, gbuf=gbuf: eo.dma_start(
                    out=gbuf[:, c, :iw],
                    in_=wffnq_d[base:base + G_SZ]
                    .rearrange("(p i) -> p i", p=128)[:, goff:goff + iw]),
                    waits=[ev_wst, ev_w_used[buf]]))
                ev_lg.append(P.emit_dma("sync", lambda eo, base=base, goff=goff,
                                        iw=iw, c=c, ubuf=ubuf: eo.dma_start(
                    out=ubuf[:, c, :iw],
                    in_=wffnq_d[base + G_SZ:base + 2 * G_SZ]
                    .rearrange("(p i) -> p i", p=128)[:, goff:goff + iw]),
                    waits=[ev_wst, ev_w_used[buf]]))
            ev_ldw = ev_lg
            chunkbase = ioff // 128
            nblk = iw // 128
            ev_gu_last = None
            for tt in range(NT):
                par = tt & 1
                pp = (tt >> 1) & 1
                pg, pu = pm[par], pm[2 + par]
                if (tt & 1) == 0:
                    pop_dn_jobs(4)
                    flush_pend()

                def gumm(eo, tt=tt, iw=iw, gbuf=gbuf, ubuf=ubuf, pg=pg, pu=pu):
                    ins = None
                    for c in range(8):
                        eo.matmul(pg[:, :iw],
                                  xnqfm[:, c, 128 * tt:128 * (tt + 1)],
                                  gbuf[:, c, :iw],
                                  start=(c == 0), stop=(c == 7))
                        ins = eo.matmul(pu[:, :iw],
                                        xnqfm[:, c, 128 * tt:128 * (tt + 1)],
                                        ubuf[:, c, :iw],
                                        start=(c == 0), stop=(c == 7))
                    return ins
                evg = P.emit("tensor", gumm,
                             waits=[ev_ldw, ev_xnq, pmf[par], pmf[2 + par]],
                             inc="e_t")
                evu = evg
                ev_gu_last = evu
                ev_sl = P.emit("scalar", lambda eo, iw=iw, pg=pg,
                               pp=pp, tt=tt: eo.activation(
                    h_f[pp][:, iw * (tt & 1):iw * ((tt & 1) + 1)],
                    pg[:, :iw], AF.Silu),
                    waits=[evg, h_free[pp] if (tt & 1) == 0 else None],
                    inc="e_s")
                pmf[par] = ev_sl
                ev_h = P.emit("vector", lambda eo, iw=iw, pu=pu,
                              pp=pp, tt=tt: eo.tensor_tensor(
                    h_f[pp][:, iw * (tt & 1):iw * ((tt & 1) + 1)],
                    h_f[pp][:, iw * (tt & 1):iw * ((tt & 1) + 1)],
                    pu[:, :iw], AL.mult),
                    waits=[ev_sl, evu], inc="e_v")
                pmf[2 + par] = ev_h
                if (tt & 1) == 0:
                    continue
                FP = 2 * iw
                e_sq, e_sg = emit_qd_head(P, h_f[pp][:, :FP], qds4, FP,
                                          par=pp, in_ev=ev_h)
                h_free[pp] = e_sg
                flush_evac()

                def mk_pend(pp=pp, tte=tt - 1, nblk=nblk, chunkbase=chunkbase,
                            FP=FP, e_sq=e_sq, e_sg=e_sg, kg=kglob):
                    def do_tail():
                        evq = emit_qd_tail(P, hq_b[pp][:, :FP], qds4, FP,
                                           par=pp, out_free=hq_free[pp],
                                           sq_ev=e_sq, sign_ev=e_sg)

                        def do_transp(evq=evq):
                            evt = None
                            for jb in range(2 * nblk):
                                evt = P.emit("tensor", lambda eo, jb=jb, pp=pp:
                                             eo.transpose(
                                                 pt[pp][:, jb * 128:(jb + 1) * 128],
                                                 hq_b[pp][:, jb * 128:(jb + 1) * 128],
                                                 id_sb[:, :]),
                                             waits=[evq, ptf4[pp],
                                                    dn_state["lastmm"]
                                                    .get(kg - 2)],
                                             inc="e_t")
                            hq_free[pp] = evt

                            def do_evac(evt=evt):
                                ptf4[pp] = P.emit("scalar", lambda eo, tte=tte,
                                                  chunkbase=chunkbase,
                                                  nblk=nblk,
                                                  pp=pp: eo.copy(
                                    hqfm[:, chunkbase:chunkbase + nblk,
                                         128 * tte:128 * (tte + 2)]
                                    .rearrange("p c (o k) -> p o c k", o=2),
                                    pt[pp][:, :2 * nblk * 128]
                                    .rearrange("p (o c k) -> p o c k",
                                               o=2, k=128)),
                                    waits=[evt], inc="e_s")
                                push_dn_jobs(kg, chunkbase, nblk, tte,
                                             ptf4[pp])
                            return do_evac
                        return do_transp
                    return do_tail
                pend[0] = mk_pend()
            ev_w_used[buf] = ev_gu_last
            buf ^= 1
            ioff += iw
    drain_pend()
    pop_dn_jobs(10 ** 9)
    emit_dn_load(kglob, part_info[kglob][0], part_info[kglob][2])
    for p8 in range(8):
        pop_dn_jobs(4)
        evp = ("e_v", P.cnt["e_v"])
        P.emit("sync", lambda eo, p8=p8: eo.dma_start(
            out=out_d.rearrange("p (t d) -> p t d", d=D)[:, 2 * p8:2 * p8 + 2, :],
            in_=x_sb[:, 2 * p8:2 * p8 + 2, :]),
            waits=[evp], inc="d_out", amt=16)
    pop_dn_jobs(10 ** 9)
    P.emit("sync", None, waits=[("d_out", 128)])
    P.barrier()
    P.run_block()
    stD.close()
    root.close()
    return nc


_NC_CACHE = {}


def kernel(L0_attn_x, L0_attn_wq, L0_attn_wk, L0_attn_wv, L0_attn_wo,
           L0_attn_q_gain, L0_ffn_w_gate, L0_ffn_w_up, L0_ffn_w_down):
    f32 = lambda a: np.ascontiguousarray(np.asarray(a, dtype=np.float32))
    x = f32(L0_attn_x).reshape(S, D)
    wqT = np.ascontiguousarray(f32(L0_attn_wq).T)
    wkT = np.ascontiguousarray(f32(L0_attn_wk).T)
    wvT = np.ascontiguousarray(f32(L0_attn_wv).T)
    woT = np.ascontiguousarray(f32(L0_attn_wo).T)
    wg = f32(L0_ffn_w_gate)
    wu = f32(L0_ffn_w_up)
    wd = f32(L0_ffn_w_down)
    wd_pad = np.concatenate(
        [wd, np.zeros((D, 3 * 128 * NCORES - I_FFN), dtype=np.float32)], axis=1)
    gain128 = np.ascontiguousarray(np.tile(f32(L0_attn_q_gain)[None, :],
                                           (128, 1)))

    inv_freq = (1.0 / 10000.0 ** (np.arange(0, HD, 2, dtype=np.float32)
                                  / np.float32(HD))).astype(np.float32)
    freqs = np.outer(np.arange(S, dtype=np.float32), inv_freq).astype(np.float32)
    cosb = np.cos(freqs.astype(np.float64)).astype(ml_dtypes.bfloat16)
    sinb = np.sin(freqs.astype(np.float64)).astype(ml_dtypes.bfloat16)
    tri = np.tril(np.ones((128, 128), dtype=bool))
    m0 = np.where(tri, np.float32(0), np.float32(FMIN)).astype(np.float32)
    mask384 = np.concatenate(
        [m0, np.zeros((128, 128), dtype=np.float32), m0], axis=1)
    ident = np.eye(128, dtype=ml_dtypes.bfloat16)

    if "nc" not in _NC_CACHE:
        _NC_CACHE["nc"] = build_program()
    nc = _NC_CACHE["nc"]

    def pmaj(a, tile):
        nt = a.shape[0] // 128
        return np.ascontiguousarray(
            a.reshape(nt, 128, tile).swapaxes(0, 1).reshape(128, nt * tile))

    wqTs = pmaj(wqT, D)
    wkTs = pmaj(wkT, D)
    wvTs = pmaj(wvT, D)
    woTs = pmaj(woT, D)
    in_maps = []
    for c in range(NCORES):
        sl = slice(c * T, (c + 1) * T)
        wdc = wd_pad[:, 384 * c:384 * (c + 1)]
        wds = np.ascontiguousarray(
            wdc.reshape(8, 128, 3, 128).transpose(1, 2, 0, 3)
            .reshape(128, 3 * 8 * 128))
        in_maps.append({
            "x": pmaj(x[sl], D),
            "wqT": wqTs, "wkT": wkTs, "wvT": wvTs, "woT": woTs,
            "wg": pmaj(np.ascontiguousarray(wg[:, 128 * c:128 * (c + 1)]), 128),
            "wu": pmaj(np.ascontiguousarray(wu[:, 128 * c:128 * (c + 1)]), 128),
            "wd": wds,
            "cosb": pmaj(cosb[sl], 32),
            "sinb": pmaj(sinb[sl], 32),
            "gain128": gain128, "mask0": mask384, "ident": ident,
        })
    res = run_bass_kernel_spmd(nc, in_maps, core_ids=list(range(NCORES)))
    out = np.concatenate(
        [r["out"].reshape(128, NT, D).swapaxes(0, 1).reshape(T, D)
         for r in res.results], axis=0)
    return out.reshape(1, S, D).astype(np.float32)


# version 39 (final)
# speedup vs baseline: 1.0534x; 1.0020x over previous
import contextlib
import os
import numpy as np
import ml_dtypes
import concourse.bass as bass
import concourse.mybir as mybir
from concourse.bass_utils import run_bass_kernel_spmd

CHAIN_WAITS = (os.environ.get("BASS_SIM_FP64") == "1"
               or os.environ.get("KERNEL_CHAIN_WAITS") == "1")

F32 = mybir.dt.float32
BF16 = mybir.dt.bfloat16
AL = mybir.AluOpType
AF = mybir.ActivationFunctionType
AX = mybir.AxisListType

NCORES = 8
S, D = 16384, 1024
T = S // NCORES
NT = T // 128
H, HD = 16, 64
NWIN = T // 256
I_FFN = 2816
FMIN = float(np.finfo(np.float32).min)
M2 = [0.0625, 0.5625, 1.5625, 3.0625, 6.25, 12.25, 25.0]
WTS = [0.5, 0.5, 0.5, 0.5, 1.0, 1.0, 2.0]
RMS_EPS = 1.1920928955078125e-07

G_SZ = 128 * 2816
DN_SZ = 128 * 1024
SH = 2 * G_SZ + 3 * DN_SZ

ENGS = ("sync", "gpsimd", "tensor", "vector", "scalar")


class Sched:

    CHAIN_SEM = {"vector": "e_v", "scalar": "e_s", "gpsimd": "e_g"}

    def __init__(self, nc, sems, dpool):
        self.nc = nc
        self.sem = sems
        self.cnt = {k: 0 for k in sems}
        self.waited = {e: {} for e in ENGS}
        self.dpool = dpool
        self.didx = {e: 0 for e in dpool}
        self.q = None

    def phase(self):
        self.q = {e: [] for e in ENGS}

    def emit(self, eng, fn, waits=(), inc=None, amt=1, chain=True):
        waits = list(waits)
        cs = self.CHAIN_SEM.get(eng)
        if fn is not None and cs is not None:
            if chain and self.cnt.get(cs, 0):
                waits.append((cs, self.cnt[cs]))
            if inc is None:
                inc = cs
        flat = []
        for ev in waits:
            if isinstance(ev, list):
                flat.extend(ev)
            else:
                flat.append(ev)
        wd = self.waited[eng]
        ws = []
        for ev in flat:
            if ev is None:
                continue
            s, v = ev
            if v is None or v <= 0 or wd.get(s, 0) >= v:
                continue
            wd[s] = v
            ws.append((s, v))
        ev = None
        if inc is not None:
            self.cnt[inc] += amt
            ev = (inc, self.cnt[inc])

        def thunk(eo, fn=fn, ws=ws, inc=inc, amt=amt):
            for s, v in ws:
                eo.wait_ge(self.sem[s], v)
            ins = fn(eo) if fn is not None else None
            if inc is not None:
                if ins is None:
                    ins = eo.nop()
                ins.then_inc(self.sem[inc], amt)

        self.q[eng].append(thunk)
        return ev

    def emit_dma(self, eng, fn, waits=()):
        pool = self.dpool[eng]
        i = self.didx[eng]
        self.didx[eng] += 1
        s = pool[i % len(pool)]
        rounds = i // len(pool)
        w = list(waits) + [(s, 16 * rounds)]
        return self.emit(eng, fn, waits=w, inc=s, amt=16)

    def barrier(self):
        for eng, pool in self.dpool.items():
            dws = [(s, self.cnt[s]) for s in pool if self.cnt[s]]
            if dws:
                self.emit(eng, None, waits=dws)
        evs = [self.emit(e, None, inc="b_" + e) for e in ENGS]
        for e in ENGS:
            self.emit(e, None, waits=evs)

    def run_block(self):
        with self.nc.Block() as blk:
            for e in ENGS:
                q = self.q[e]
                if not q:
                    continue

                def runner(eo, q=q):
                    for t in q:
                        t(eo)

                getattr(blk, e)(runner)
        self.q = None


def emit_qd_head_part(P, xin, scr, F, c0, c1, par=0, in_ev=None,
                      rstd=None, rstd6=None, last=True):
    nb = F // 16
    b0, b1 = c0 // 16, c1 // 16
    bm, rsc = scr["bm"], scr["rsc"]
    sc_, rsq = scr["sc"][par], scr["rsq"][par]
    a2, sgn = scr["a2"][par], scr["sgn"][par]
    P.emit("vector", lambda eo, xin=xin, b0=b0, b1=b1, c0=c0, c1=c1:
           eo.tensor_reduce(
        bm[:, b0:b1], xin[:, c0:c1].rearrange("p (b k) -> p b k", k=16),
        AX.X, AL.max, apply_absolute_value=True), inc="e_v")
    if last:
        s6 = rstd6 if rstd6 is not None else 1.0 / 6.0
        P.emit("vector", lambda eo, nb=nb, sc_=sc_, s6=s6: eo.tensor_scalar(
            sc_[:, :nb], bm[:, :nb], s6, 1e-12, AL.mult, AL.max),
            waits=[scr["ev_sc_free"][par]], inc="e_v")
        P.emit("vector", lambda eo, nb=nb, sc_=sc_: eo.reciprocal(
            rsc[:, :nb], sc_[:, :nb]), inc="e_v")
        P.emit("vector", lambda eo, nb=nb, rsq=rsq: eo.tensor_tensor(
            rsq[:, :nb], rsc[:, :nb], rsc[:, :nb], AL.mult), inc="e_v")
    if rstd is not None:
        e_sq = P.emit("scalar", lambda eo, xin=xin, c0=c0, c1=c1, a2=a2,
                      rstd=rstd: eo.activation(
            a2[:, c0:c1], xin[:, c0:c1], AF.Square, scale=rstd),
            waits=[in_ev, scr["ev_cmp_free"][par] if c0 == 0 else None],
            inc="e_s")
    else:
        e_sq = P.emit("scalar", lambda eo, xin=xin, c0=c0, c1=c1, a2=a2:
                      eo.activation(a2[:, c0:c1], xin[:, c0:c1], AF.Square),
            waits=[in_ev, scr["ev_cmp_free"][par] if c0 == 0 else None],
            inc="e_s")
    e_sg = P.emit("scalar", lambda eo, xin=xin, c0=c0, c1=c1, sgn=sgn: eo.sign(
        sgn[:, c0:c1], xin[:, c0:c1]),
        waits=[scr["ev_sgn_free"][par] if c0 == 0 else None], inc="e_s")
    return e_sq, e_sg


def emit_qd_head(P, xin, scr, F, par=0, in_ev=None, rstd=None, rstd6=None):
    return emit_qd_head_part(P, xin, scr, F, 0, F, par=par, in_ev=in_ev,
                             rstd=rstd, rstd6=rstd6, last=True)


def emit_qd_tail(P, qout, scr, F, par=0, out_free=None, sq_ev=None,
                 sign_ev=None):
    nb = F // 16
    cb, g0 = scr["cb"][par], scr["g0"][par]
    sc_, rsq = scr["sc"][par], scr["rsq"][par]
    a2, sgn = scr["a2"][par], scr["sgn"][par]
    sA, gA = scr["sA"], scr["gA"]
    ev_a2 = P.emit("vector", lambda eo, F=F, nb=nb, a2=a2, rsq=rsq:
                   eo.tensor_tensor(
        a2[:, :F].rearrange("p (b k) -> p b k", k=16),
        a2[:, :F].rearrange("p (b k) -> p b k", k=16),
        rsq[:, :nb].unsqueeze(2).broadcast_to([128, nb, 16]), AL.mult),
        waits=[sq_ev, scr["ev_sgn_a2"][par]], inc="e_v")
    for i in range(5):
        dst = g0 if i == 0 else cb
        ev_ts = P.emit("vector", lambda eo, dst=dst, F=F, i=i: eo.tensor_scalar(
            dst[:, :F], a2[:, :F], M2[i], WTS[i], AL.is_gt, AL.mult),
            waits=[scr["ev_g0cb_free"][par]] if i <= 1 else (), inc="e_v")
        if i > 0:
            P.emit("vector", lambda eo, F=F: eo.tensor_tensor(
                g0[:, :F], g0[:, :F], cb[:, :F], AL.add), inc="e_v")
    scr["ev_cmp_free"][par] = ev_ts
    P.emit("scalar", lambda eo, F=F: eo.activation(
        sA[:, :F], a2[:, :F], AF.Sign, bias=scr["m2n"][:, 1:2]),
        waits=[ev_a2], inc="e_s")
    eR5 = P.emit("scalar", lambda eo, F=F: eo.activation(
        gA[:, :F], sA[:, :F], AF.Relu),
        waits=[scr["ev_gA_free"][0]], inc="e_s")
    evA = P.emit("vector", lambda eo, F=F: eo.tensor_tensor(
        g0[:, :F], g0[:, :F], gA[:, :F], AL.add), waits=[eR5], inc="e_v")
    gB = scr["gB"] if scr["gB"] is not None else gA
    gBw = scr["ev_gB_free"][0] if scr["gB"] is not None else evA
    P.emit("scalar", lambda eo, F=F: eo.activation(
        sA[:, :F], a2[:, :F], AF.Sign, bias=scr["m2n"][:, 2:3]), inc="e_s")
    eR6 = P.emit("scalar", lambda eo, F=F, gB=gB: eo.activation(
        gB[:, :F], sA[:, :F], AF.Relu, scale=2.0), waits=[gBw], inc="e_s")
    evC = P.emit("vector", lambda eo, F=F, gB=gB: eo.tensor_tensor(
        g0[:, :F], g0[:, :F], gB[:, :F], AL.add), waits=[eR6], inc="e_v")
    scr["ev_gA_free"][0] = evA if scr["gB"] is not None else evC
    scr["ev_gB_free"][0] = evC
    scr["ev_sgn_a2"][par] = eR6
    ev_cbm = P.emit("gpsimd", lambda eo, F=F, sgn=sgn: eo.tensor_tensor(
        cb[:, :F], g0[:, :F], sgn[:, :F], AL.mult),
        waits=[sign_ev, evC], inc="e_g")
    scr["ev_sgn_free"][par] = ev_cbm
    evq = P.emit("gpsimd", lambda eo, qout=qout, F=F, nb=nb, sc_=sc_:
                 eo.tensor_tensor(
        qout.rearrange("p (b k) -> p b k", k=16),
        cb[:, :F].rearrange("p (b k) -> p b k", k=16),
        sc_[:, :nb].unsqueeze(2).broadcast_to([128, nb, 16]), AL.mult),
        waits=[out_free], inc="e_g")
    scr["ev_g0cb_free"][par] = evq
    scr["ev_sc_free"][par] = evq
    return evq


def emit_qd(P, xin, qout, scr, F, out_free=None, in_ev=None):
    e_sq, e_sg = emit_qd_head(P, xin, scr, F, par=0, in_ev=in_ev)
    evq = emit_qd_tail(P, qout, scr, F, par=0, out_free=out_free,
                       sq_ev=e_sq, sign_ev=e_sg)
    return evq, e_sg


def build_program():
    nc = bass.Bass("TRN2", target_bir_lowering=False, debug=False)

    x_d = nc.dram_tensor("x", [128, NT * D], F32, kind="ExternalInput").ap()
    wqT_d = nc.dram_tensor("wqT", [128, 8 * D], F32, kind="ExternalInput").ap()
    wkT_d = nc.dram_tensor("wkT", [128, 8 * D], F32, kind="ExternalInput").ap()
    wvT_d = nc.dram_tensor("wvT", [128, 8 * D], F32, kind="ExternalInput").ap()
    woT_d = nc.dram_tensor("woT", [128, 8 * D], F32, kind="ExternalInput").ap()
    wg_d = nc.dram_tensor("wg", [128, I_FFN], F32, kind="ExternalInput").ap()
    wu_d = nc.dram_tensor("wu", [128, I_FFN], F32, kind="ExternalInput").ap()
    wd_d = nc.dram_tensor("wd", [128, 3 * 8 * 128], F32, kind="ExternalInput").ap()
    cos_d = nc.dram_tensor("cosb", [128, NT * 32], BF16, kind="ExternalInput").ap()
    sin_d = nc.dram_tensor("sinb", [128, NT * 32], BF16, kind="ExternalInput").ap()
    gain_d = nc.dram_tensor("gain128", [128, H], F32, kind="ExternalInput").ap()
    mask_d = nc.dram_tensor("mask0", [128, 384], F32, kind="ExternalInput").ap()
    id_d = nc.dram_tensor("ident", [128, 128], BF16, kind="ExternalInput").ap()
    out_d = nc.dram_tensor("out", [128, NT * D], F32, kind="ExternalOutput").ap()
    ag_in = nc.dram_tensor("ag_in", [SH], BF16).ap()
    wffnq_d = nc.dram_tensor("wffnq", [NCORES * SH], BF16,
                             addr_space="Shared").ap()
    xnqfm_d = nc.dram_tensor("xnqfm_dr", [128, 8 * T], BF16).ap()

    root = contextlib.ExitStack()
    sems = {}
    snames = (["d_out", "e_t", "e_v", "e_s", "e_g",
               "b_sync", "b_gpsimd", "b_tensor", "b_vector", "b_scalar"]
              + [f"ds{i}" for i in range(12)] + [f"dg{i}" for i in range(4)])
    for s in snames:
        sems[s] = root.enter_context(nc.semaphore(s))
    P = Sched(nc, sems, {"sync": [f"ds{i}" for i in range(12)],
                         "gpsimd": [f"dg{i}" for i in range(4)]})

    def sb(st, name, shape, dt):
        return st.enter_context(nc.sbuf_tensor(name, shape, dt))

    x_sb = sb(root, "x_sb", [128, NT, D], F32)
    stat = sb(root, "stat", [128, 16], F32)
    m2n = sb(root, "m2n", [128, 3], F32)
    gain_sb = sb(root, "gain_sb", [128, H], F32)
    mask_sb = sb(root, "mask_sb", [128, 384], F32)
    id_sb = sb(root, "id_sb", [128, 128], BF16)
    pm = [root.enter_context(nc.psum_tensor(f"pm{i}", [128, 512], F32))
          for i in range(4)]
    pt = [root.enter_context(nc.psum_tensor(f"pt{i}", [128, 1024], BF16))
          for i in range(2)]
    psc = [root.enter_context(nc.psum_tensor(f"psc{i}", [128, 512], F32))
           for i in range(2)]

    stA = contextlib.ExitStack()
    xnfm = sb(stA, "xnfm", [128, 8, T], BF16)
    wq_sb = sb(stA, "wq_sb", [128, 8, D], BF16)
    wk_sb = sb(stA, "wk_sb", [128, 8, D], BF16)
    wv_sb = sb(stA, "wv_sb", [128, 8, D], BF16)
    wo_sb = sb(stA, "wo_sb", [128, 8, D], BF16)
    cos_sb = sb(stA, "cos_sb", [128, NT, 32], BF16)
    sin_sb = sb(stA, "sin_sb", [128, NT, 32], BF16)

    def alloc_qds(st, F, tag="", npar=1, gb=False):
        return {
            "m2n": m2n,
            "bm": sb(st, f"qd_bm{tag}", [128, F // 16], F32),
            "rsc": sb(st, f"qd_rsc{tag}", [128, F // 16], F32),
            "a2": [sb(st, f"qd_a2{tag}{i}", [128, F], F32)
                   for i in range(npar)],
            "cb": [sb(st, f"qd_cb{tag}{i}", [128, F], BF16)
                   for i in range(npar)],
            "g0": [sb(st, f"qd_g0{tag}{i}", [128, F], BF16)
                   for i in range(npar)],
            "sc": [sb(st, f"qd_sc{tag}{i}", [128, F // 16], F32)
                   for i in range(npar)],
            "rsq": [sb(st, f"qd_rsq{tag}{i}", [128, F // 16], F32)
                    for i in range(npar)],
            "sgn": [sb(st, f"qd_sgn{tag}{i}", [128, F], BF16)
                    for i in range(npar)],
            "sA": sb(st, f"qd_sA{tag}", [128, F], BF16),
            "gA": sb(st, f"qd_gA{tag}", [128, F], BF16),
            "gB": sb(st, f"qd_gB{tag}", [128, F], BF16) if gb else None,
            "ev_gA_free": [None],
            "ev_gB_free": [None],
            "ev_g0cb_free": [None] * npar,
            "ev_sgn_free": [None] * npar,
            "ev_sc_free": [None] * npar,
            "ev_cmp_free": [None] * npar,
            "ev_sgn_a2": [None] * npar,
        }

    stB = contextlib.ExitStack()
    qds = alloc_qds(stB, 1024, npar=2)
    wsh = sb(stB, "wsh", [128, 1024], F32)
    qsh = sb(stB, "qsh", [128, 1024], BF16)
    fmo = sb(stB, "fmo", [128, 1024], BF16)
    xc = sb(stB, "xc", [128, D], F32)
    xnb = sb(stB, "xnb", [128, D], BF16)
    xnq = [sb(stB, f"xnq{i}", [128, D], BF16) for i in range(2)]
    xnqf = xnb

    P.phase()
    for _i in range(3):
        P.emit("vector", lambda eo, _i=_i: eo.memset(
            m2n[:, _i:_i + 1], -M2[4 + _i]), inc="e_v")
    ev_x = []
    for q in range(4):
        ev_x.append(P.emit_dma("sync", lambda eo, q=q: eo.dma_start(
            out=x_sb[:, 4 * q:4 * (q + 1), :],
            in_=x_d.rearrange("p (t d) -> p t d", d=D)[:, 4 * q:4 * (q + 1), :])))
    ev_cos = P.emit_dma("sync", lambda eo: eo.dma_start(
        out=cos_sb[:, :, :], in_=cos_d.rearrange("p (t j) -> p t j", j=32)))
    ev_sin = P.emit_dma("sync", lambda eo: eo.dma_start(
        out=sin_sb[:, :, :], in_=sin_d.rearrange("p (t j) -> p t j", j=32)))
    ev_gld = P.emit_dma("sync", lambda eo: eo.dma_start(out=gain_sb[:, :],
                                                        in_=gain_d))
    ev_mask = P.emit_dma("sync", lambda eo: eo.dma_start(out=mask_sb[:, :],
                                                         in_=mask_d))
    ev_id = P.emit_dma("sync", lambda eo: eo.dma_start(out=id_sb[:, :],
                                                       in_=id_d))
    ev_attnw = []
    attnw_pend = [(wqT_d, wq_sb), (wkT_d, wk_sb), (wvT_d, wv_sb),
                  (woT_d, wo_sb)]
    ev_gain = P.emit("vector", lambda eo: eo.tensor_scalar(
        gain_sb[:, :], gain_sb[:, :], 0.125, None, AL.mult),
        waits=[ev_gld], inc="e_v")

    st_wqd = {"evq": None, "esg": None, "pt": [None, None], "store": None,
              "ptsel": 0, "tr": None}

    def wqd_piece(src_ap, ntile, off, Wrow, col0):
        F = ntile * 128
        par = st_wqd["ptsel"] & 1
        st_wqd["ptsel"] += 1
        vsnap = ("e_v", P.cnt["e_v"])
        evd = P.emit_dma("sync", lambda eo, src_ap=src_ap, F=F: eo.dma_start(
            out=wsh[:, :F], in_=src_ap),
            waits=[st_wqd["evq"], st_wqd["esg"]])
        P.emit("vector", None, waits=[evd])
        e_sq, e_sg = emit_qd_head(P, wsh[:, :F], qds, F, par=par,
                                  in_ev=[evd, vsnap])
        st_wqd["evq"] = ("e_v", P.cnt["e_v"])
        st_wqd["esg"] = e_sg
        evq = emit_qd_tail(P, qsh[:, :F], qds, F, par=par, sq_ev=e_sq,
                           sign_ev=e_sg, out_free=st_wqd["tr"])
        bank = pt[0]
        evt = None
        for j in range(ntile):
            evt = P.emit("tensor", lambda eo, j=j, bank=bank: eo.transpose(
                bank[:, j * 128:(j + 1) * 128], qsh[:, j * 128:(j + 1) * 128],
                id_sb[:, :]), waits=[evq, ev_id, st_wqd["pt"][0]], inc="e_t")
        st_wqd["tr"] = evt
        ev_ev = P.emit("vector", lambda eo, F=F, bank=bank: eo.tensor_copy(
            fmo[:, :F], bank[:, :F]), waits=[evt, st_wqd["store"]], inc="e_v")
        st_wqd["pt"][0] = ev_ev
        st_wqd["store"] = P.emit_dma("sync", lambda eo, off=off, Wrow=Wrow, F=F,
                                     col0=col0: eo.dma_start(
            out=ag_in[off:off + 128 * Wrow]
                .rearrange("(p i) -> p i", p=128)[:, col0:col0 + F],
            in_=fmo[:, :F]), waits=[ev_ev])
        return st_wqd["store"]

    ag_waits = []
    piece_args = []
    for (w_ap, roff) in ((wg_d, 0), (wu_d, G_SZ)):
        for (t0, ntile) in ((0, 8), (8, 8), (16, 6)):
            piece_args.append((
                w_ap[:, t0 * 128:(t0 + ntile) * 128],
                ntile, roff, 2816, t0 * 128))
    for pos in range(3):
        piece_args.append((
            wd_d[:, 1024 * pos:1024 * (pos + 1)],
            8, 2 * G_SZ + pos * DN_SZ, 1024, 0))

    ev_xnb_free = None
    ev_sg_prev = None
    ev_xnq_free = [None, None]
    ev_qf_free = None
    ln_pend = [None, None]

    def ln_flush():
        if ln_pend[1] is not None:
            ln_pend[1]()
            ln_pend[1] = None
        if ln_pend[0] is not None:
            ln_pend[1] = ln_pend[0]()
            ln_pend[0] = None

    for t in range(NT):
        par = t & 1
        if t >= 2 and t % 2 == 0 and attnw_pend:
            w_d, w_s = attnw_pend.pop(0)
            ev_attnw.append(P.emit_dma("gpsimd", lambda eo, w_d=w_d, w_s=w_s:
                                       eo.dma_start(
                out=w_s[:, :, :].rearrange("p c n -> p (c n)"), in_=w_d)))
        if piece_args:
            ln_flush()
            ag_waits.append(wqd_piece(*piece_args.pop(0)))
        ev_vsnap = ("e_v", P.cnt["e_v"])
        ev_cp = P.emit("scalar", lambda eo, t=t: eo.activation(
            xc[:, :], x_sb[:, t, :], AF.Copy, accum_out=stat[:, 0:1]),
            waits=[ev_x[t // 4], ev_vsnap], inc="e_s")
        ev_nm = P.emit("vector", lambda eo: eo.tensor_scalar(
            stat[:, 1:2], stat[:, 0:1], -1.0 / D, None, AL.mult),
            waits=[ev_cp], inc="e_v")
        P.emit("scalar", lambda eo: eo.activation(
            xc[:, :], xc[:, :], AF.Identity, bias=stat[:, 1:2]),
            waits=[ev_nm], inc="e_s")
        ev_acc = P.emit("scalar", lambda eo: eo.activation(
            qsh[:, :], xc[:, :], AF.Square, accum_out=stat[:, 2:3]),
            inc="e_s")
        ln_flush()
        ev = P.emit("vector", lambda eo: eo.tensor_scalar(
            stat[:, 3:4], stat[:, 2:3], 1.0 / D, 1e-5, AL.mult, AL.add),
            waits=[ev_acc], inc="e_v")
        ev = P.emit("scalar", lambda eo: eo.activation(
            stat[:, 4:5], stat[:, 3:4], AF.Ln), waits=[ev], inc="e_s")
        ev = P.emit("scalar", lambda eo: eo.activation(
            stat[:, 5:6], stat[:, 4:5], AF.Exp, scale=-0.5), inc="e_s")
        P.emit("vector", lambda eo: eo.tensor_scalar(
            stat[:, 6:7], stat[:, 5:6], 1.0 / 6.0, None, AL.mult),
            waits=[ev], inc="e_v")
        ev_xnb = P.emit("scalar", lambda eo: eo.activation(
            xnb[:, :], xc[:, :], AF.Copy, scale=stat[:, 5:6]),
            waits=[ev_xnb_free, ev_qf_free], inc="e_s")
        e_sq, e_sg = emit_qd_head(P, xc[:, :], qds, D, par=par,
                                  in_ev=ev_sg_prev, rstd=stat[:, 5:6],
                                  rstd6=stat[:, 6:7])
        ev_sg_prev = e_sg
        evt0 = None
        for c in range(8):
            evt0 = P.emit("tensor", lambda eo, c=c: eo.transpose(
                pt[0][:, c * 128:(c + 1) * 128], xnb[:, c * 128:(c + 1) * 128],
                id_sb[:, :]),
                waits=[ev_xnb, ev_id, st_wqd["pt"][0]], inc="e_t")
        ev_xnb_free = evt0

        def mk_ln_pend(t=t, par=par, evt0=evt0, e_sq=e_sq, e_sg=e_sg):
            def do():
                ev_fm = P.emit("vector", lambda eo, t=t: eo.tensor_copy(
                    xnfm[:, :, 128 * t:128 * (t + 1)],
                    pt[0][:, :].rearrange("p (c k) -> p c k", k=128)),
                    waits=[evt0], inc="e_v")
                st_wqd["pt"][0] = ev_fm
                evq = emit_qd_tail(P, xnq[par][:, :], qds, D, par=par,
                                   out_free=ev_xnq_free[par], sq_ev=e_sq,
                                   sign_ev=e_sg)
                evt1 = None
                for c in range(8):
                    evt1 = P.emit("tensor", lambda eo, c=c, par=par:
                                  eo.transpose(
                        pt[1][:, c * 128:(c + 1) * 128],
                        xnq[par][:, c * 128:(c + 1) * 128], id_sb[:, :]),
                        waits=[evq, st_wqd["pt"][1]], inc="e_t")
                ev_xnq_free[par] = evt1

                def do_evac(evt1=evt1, t=t):
                    nonlocal ev_qf_free
                    ev_qf = P.emit("vector", lambda eo: eo.tensor_copy(
                        xnqf[:, :].rearrange("p (c k) -> p c k", k=128),
                        pt[1][:, :].rearrange("p (c k) -> p c k", k=128)),
                        waits=[evt1, ev_qf_free, ev_xnb_free], inc="e_v")
                    st_wqd["pt"][1] = ev_qf
                    ev_qf_free = P.emit_dma("sync", lambda eo, t=t: eo.dma_start(
                        out=xnqfm_d.rearrange("p (c s) -> p c s", s=T)
                            [:, :, 128 * t:128 * (t + 1)],
                        in_=xnqf[:, :].rearrange("p (c k) -> p c k", k=128)),
                        waits=[ev_qf])
                return do_evac
            return do
        ln_pend[0] = mk_ln_pend()

    for _ in range(2):
        ln_flush()
    ev_wst = P.emit("gpsimd", lambda eo: eo.collective_compute(
        "AllGather", AL.bypass, replica_groups=[list(range(NCORES))],
        ins=[ag_in.opt()], outs=[wffnq_d.opt()]),
        waits=ag_waits, inc="e_g")

    P.barrier()
    P.run_block()
    stB.close()

    stC = contextlib.ExitStack()
    qk_sb = [sb(stC, f"qk{i}", [128, D], BF16) for i in range(2)]
    qsw = sb(stC, "qsw", [128, D], BF16)
    qm1 = sb(stC, "qm1", [128, D], BF16)
    sq3 = sb(stC, "sq3", [128, D], BF16)
    rstd_b = sb(stC, "rstd_b", [128, H], BF16)
    qn_b = sb(stC, "qn_b", [128, 2, D], BF16)
    kn_b = sb(stC, "kn_b", [128, 2, D], BF16)
    v_sb = sb(stC, "v_sb", [128, 2, D], BF16)
    qfm = sb(stC, "qfm", [128, 8, 256], BF16)
    kfm = sb(stC, "kfm", [128, 8, 256], BF16)
    yfm = sb(stC, "yfm", [128, 8, 256], BF16)
    sc_sb = [sb(stC, f"sc{i}", [128, 384], F32) for i in range(2)]
    e_sb = [sb(stC, f"e{i}", [128, 384], BF16) for i in range(3)]
    p_sb = [sb(stC, f"p{i}", [128, 384], BF16) for i in range(2)]
    pT_sb = [sb(stC, f"pT{i}", [128, 384], BF16) for i in range(2)]
    ast = sb(stC, "ast", [128, 32], F32)

    P.phase()
    pmf = [None, None, None, None]
    ptf = [None, None]
    pscf = [None, None]
    sc_free = [None, None]
    e_free = [None] * 3
    p_free = [None] * 2
    pT_free = [None] * 2
    ev_qk_free = [None, None]
    ev_dst_free = None
    ev_v_used = None
    ev_yfm_used = None
    ev_sq3_free = None

    v4 = lambda ap: ap.rearrange("p (h o j) -> p h o j", o=2, j=32)
    wo_pend = None
    for w in range(NWIN):
        if wo_pend is not None:
            wo_pend()
            wo_pend = None
        rounds = [("q", 0), ("q", 1), ("k", 0), ("k", 1), ("v", 0), ("v", 1)]
        rp = [None, None, None]

        def rflush():
            a = rp[0]() if rp[0] is not None else None
            b = rp[1]() if rp[1] is not None else None
            if rp[2] is not None:
                rp[2]()
            rp[0], rp[1], rp[2] = None, a, b

        def mk_back(kind, ti, par, outs, w=w):
            isq = kind == "q"
            dstb = qn_b if isq else kn_b
            fmdst = qfm if isq else kfm
            t = 2 * w + ti

            def back():
                nonlocal ev_sq3_free
                ev1 = P.emit("scalar", lambda eo, par=par: eo.activation(
                    sq3[:, :], qk_sb[par][:, :], AF.Square),
                    waits=outs + [ev_sq3_free], inc="e_s")
                ev2 = P.emit("vector", lambda eo: eo.tensor_reduce(
                    stat[:, 0:16], sq3[:, :].rearrange("p (h d) -> p h d", d=HD),
                    AX.X, AL.add), waits=[ev1], inc="e_v")
                ev_sq3_free = ev2
                ev3 = P.emit("vector", lambda eo: eo.tensor_scalar(
                    stat[:, 0:16], stat[:, 0:16], 1.0 / HD, RMS_EPS,
                    AL.mult, AL.add), inc="e_v")
                P.emit("scalar", lambda eo: eo.activation(
                    stat[:, 0:16], stat[:, 0:16], AF.Ln), waits=[ev3], inc="e_s")
                ev5 = P.emit("scalar", lambda eo: eo.activation(
                    rstd_b[:, :], stat[:, 0:16], AF.Exp, scale=-0.5), inc="e_s")
                if isq:
                    ev5 = P.emit("vector", lambda eo: eo.tensor_tensor(
                        rstd_b[:, :], rstd_b[:, :], gain_sb[:, :], AL.mult),
                        waits=[ev5, ev_gain], inc="e_v")
                P.emit("vector", lambda eo, t=t, par=par: eo.tensor_tensor(
                    v4(qsw[:, :])[:, :, 0, :], v4(qk_sb[par][:, :])[:, :, 1, :],
                    sin_sb[:, t, :].unsqueeze(1).broadcast_to([128, H, 32]),
                    AL.mult), waits=[ev_sin], inc="e_v")
                P.emit("vector", lambda eo, t=t, par=par:
                       eo.scalar_tensor_tensor(
                    v4(qsw[:, :])[:, :, 1, :], v4(qk_sb[par][:, :])[:, :, 0, :],
                    -1.0,
                    sin_sb[:, t, :].unsqueeze(1).broadcast_to([128, H, 32]),
                    AL.mult, AL.mult), inc="e_v")
                ev_qk_free[par] = P.emit("vector", lambda eo, t=t, par=par:
                                         eo.tensor_tensor(
                    v4(qm1[:, :]), v4(qk_sb[par][:, :]),
                    cos_sb[:, t, :].unsqueeze(1).unsqueeze(2)
                    .broadcast_to([128, H, 2, 32]), AL.mult),
                    waits=[ev_cos], inc="e_v")
                P.emit("vector", lambda eo: eo.tensor_tensor(
                    qm1[:, :], qm1[:, :], qsw[:, :], AL.add), inc="e_v")
                ev8 = P.emit("vector", lambda eo, dstb=dstb, ti=ti:
                             eo.tensor_tensor(
                    dstb[:, ti, :].rearrange("p (h d) -> p h d", d=HD),
                    qm1[:, :].rearrange("p (h d) -> p h d", d=HD),
                    rstd_b[:, :].unsqueeze(2).broadcast_to([128, H, HD]),
                    AL.mult), waits=[ev5, ev_dst_free], inc="e_v")

                def transp():
                    nonlocal ev_dst_free
                    evt = None
                    for c in range(8):
                        evt = P.emit("tensor", lambda eo, c=c, ti=ti, dstb=dstb,
                                     par=par: eo.transpose(
                                         pt[par][:, c * 128:(c + 1) * 128],
                                         dstb[:, ti, c * 128:(c + 1) * 128],
                                         id_sb[:, :]),
                                     waits=[ev8, ptf[par]], inc="e_t")
                    ev_dst_free = evt

                    def fmevac():
                        ptf[par] = P.emit("vector", lambda eo, ti=ti,
                                          fmdst=fmdst, par=par: eo.tensor_copy(
                            fmdst[:, :, 128 * ti:128 * (ti + 1)],
                            pt[par][:, :].rearrange("p (c k) -> p c k", k=128)),
                            waits=[evt], inc="e_v")
                    return fmevac
                return transp
            return back

        for s in range(len(rounds) + 3):
            if s < len(rounds):
                kind, ti = rounds[s]
                par = s & 1
                banks = (2, 3) if par == 0 else (0, 1)
                wsb = wq_sb if kind == "q" else (wk_sb if kind == "k" else wv_sb)
                evms = []
                for nn in range(2):
                    bank = banks[nn]

                    def mms(eo, ti=ti, nn=nn, w=w, wsb=wsb, bank=bank):
                        ins = None
                        for c in range(8):
                            ins = eo.matmul(
                                pm[bank][:, :],
                                xnfm[:, c, (2 * w + ti) * 128:(2 * w + ti + 1) * 128],
                                wsb[:, c, 512 * nn:512 * (nn + 1)],
                                start=(c == 0), stop=(c == 7))
                        return ins
                    evms.append(P.emit("tensor", mms,
                                       waits=[ev_attnw, pmf[bank]], inc="e_t"))
            rflush()
            if s < len(rounds):
                outs = []
                for nn in range(2):
                    bank = banks[nn]
                    if kind == "v":
                        ev = P.emit("scalar", lambda eo, ti=ti, nn=nn, bank=bank:
                                    eo.copy(v_sb[:, ti, 512 * nn:512 * (nn + 1)],
                                            pm[bank][:, :]),
                                    waits=[evms[nn], ev_v_used], inc="e_s")
                    else:
                        ev = P.emit("scalar", lambda eo, nn=nn, bank=bank,
                                    par=par: eo.copy(
                                        qk_sb[par][:, 512 * nn:512 * (nn + 1)],
                                        pm[bank][:, :]),
                                    waits=[evms[nn], ev_qk_free[par]], inc="e_s")
                    pmf[bank] = ev
                    outs.append(ev)
                if kind != "v":
                    rp[0] = mk_back(kind, ti, par, outs)
        ev_qkfm = ("e_v", P.cnt["e_v"])
        ev_v = ("e_s", P.cnt["e_s"])

        exp_evs = [None] * 4
        score_evs = [None, None]

        def stage_a(h):
            par = h & 1
            c, off = h >> 1, 64 * (h & 1)
            score_evs[par] = P.emit("tensor", lambda eo, c=c, off=off, par=par: (
                eo.matmul(psc[par][:, 0:128], qfm[off:off + 64, c, 0:128],
                          kfm[off:off + 64, c, 0:128], start=True, stop=True,
                          tile_position=(off, 0)),
                eo.matmul(psc[par][:, 128:384], qfm[off:off + 64, c, 128:256],
                          kfm[off:off + 64, c, 0:256], start=True, stop=True,
                          tile_position=(off, 0)))[-1],
                waits=[ev_qkfm, pscf[par]], inc="e_t")

        def stage_b(h):
            par = h & 1
            p4 = h & 3
            ev_s = score_evs[par]
            ev_msk = P.emit("vector", lambda eo, par=par: eo.tensor_tensor(
                sc_sb[par][:, :], psc[par][:, 0:384], mask_sb[:, :], AL.add),
                waits=[ev_s, ev_mask, sc_free[par]], inc="e_v")
            pscf[par] = ev_msk
            ev_m0 = P.emit("vector", lambda eo, par=par, p4=p4: eo.reduce_max(
                ast[:, 8 * p4:8 * p4 + 1], sc_sb[par][:, 0:128], AX.X,
                negate=True), inc="e_v")
            ev_m1 = P.emit("vector", lambda eo, par=par, p4=p4: eo.reduce_max(
                ast[:, 8 * p4 + 1:8 * p4 + 2], sc_sb[par][:, 128:384], AX.X,
                negate=True), inc="e_v")
            p3 = h % 3
            P.emit("scalar", lambda eo, par=par, p4=p4, p3=p3: eo.activation(
                e_sb[p3][:, 0:128], sc_sb[par][:, 0:128], AF.Exp,
                bias=ast[:, 8 * p4:8 * p4 + 1],
                accum_out=ast[:, 8 * p4 + 2:8 * p4 + 3]),
                waits=[ev_m0, e_free[p3]], inc="e_s")
            e1 = P.emit("scalar", lambda eo, par=par, p4=p4, p3=p3: eo.activation(
                e_sb[p3][:, 128:384], sc_sb[par][:, 128:384], AF.Exp,
                bias=ast[:, 8 * p4 + 1:8 * p4 + 2],
                accum_out=ast[:, 8 * p4 + 3:8 * p4 + 4]),
                waits=[ev_m1], inc="e_s")
            sc_free[par] = e1
            exp_evs[p4] = e1

        def stage_c(h):
            nonlocal ev_v_used, ev_yfm_used
            par = h & 1
            p4 = h & 3
            p3 = h % 3
            c, off = h >> 1, 64 * (h & 1)
            P.emit("vector", lambda eo, p4=p4: eo.reciprocal(
                ast[:, 8 * p4 + 4:8 * p4 + 6],
                ast[:, 8 * p4 + 2:8 * p4 + 4]),
                waits=[exp_evs[p4]], inc="e_v")
            P.emit("vector", lambda eo, p4=p4, p3=p3, par=par: eo.tensor_scalar(
                p_sb[par][:, 0:128], e_sb[p3][:, 0:128],
                ast[:, 8 * p4 + 4:8 * p4 + 5], None, AL.mult),
                waits=[p_free[par]], inc="e_v")
            ev_p1 = P.emit("vector", lambda eo, p4=p4, p3=p3, par=par:
                           eo.tensor_scalar(
                p_sb[par][:, 128:384], e_sb[p3][:, 128:384],
                ast[:, 8 * p4 + 5:8 * p4 + 6], None, AL.mult), inc="e_v")
            e_free[p3] = ev_p1
            evt = P.emit("tensor", lambda eo, par=par: (
                eo.transpose(pt[par][:, 0:128], p_sb[par][:, 0:128], id_sb[:, :]),
                eo.transpose(pt[par][:, 128:256], p_sb[par][:, 128:256],
                             id_sb[:, :]),
                eo.transpose(pt[par][:, 256:384], p_sb[par][:, 256:384],
                             id_sb[:, :]))[-1],
                waits=[ev_p1, ptf[par]], inc="e_t")
            p_free[par] = evt
            ev_pT = P.emit("scalar", lambda eo, par=par: eo.copy(
                pT_sb[par][:, 0:384], pt[par][:, 0:384]),
                waits=[evt, pT_free[par]], inc="e_s")
            ptf[par] = ev_pT

            def pv(eo, h=h, off=off, par=par):
                eo.matmul(pm[par][off:off + 64, 0:256],
                          v_sb[:, 0, 64 * h:64 * h + 64], pT_sb[par][:, 0:256],
                          start=True, stop=False, tile_position=(0, off))
                return eo.matmul(pm[par][off:off + 64, 128:256],
                                 v_sb[:, 1, 64 * h:64 * h + 64],
                                 pT_sb[par][:, 256:384],
                                 start=False, stop=True, tile_position=(0, off))
            ev_y = P.emit("tensor", pv, waits=[ev_pT, ev_v, pmf[par]],
                          inc="e_t")
            pT_free[par] = ev_y
            ev_v_used = ev_y
            y_evs[par] = ev_y

        def stage_d(h):
            nonlocal ev_yfm_used
            par = h & 1
            c, off = h >> 1, 64 * (h & 1)
            if par == 0:
                ev_ye = P.emit("vector", lambda eo, c=c, off=off, par=par:
                               eo.tensor_copy(
                    yfm[off:off + 64, c, :], pm[par][off:off + 64, 0:256]),
                    waits=[y_evs[par], ev_yfm_used], inc="e_v")
            else:
                ev_ye = P.emit("scalar", lambda eo, c=c, off=off, par=par:
                               eo.copy(
                    yfm[off:off + 64, c, :], pm[par][off:off + 64, 0:256]),
                    waits=[y_evs[par], ev_yfm_used], inc="e_s")
            pmf[par] = ev_ye
            yfm_evs[par] = ev_ye
            return ev_ye

        y_evs = [None, None]
        yfm_evs = [None, None]
        ev_yfm = None
        for s in range(H + 4):
            if s < H:
                stage_a(s)
            if 1 <= s <= H:
                stage_b(s - 1)
            if 3 <= s <= H + 2:
                stage_c(s - 3)
            if s >= 4:
                ev_yfm = stage_d(s - 4)

        def mk_wo(w=w, ev_yfm=list(yfm_evs)):
            def do_wo():
                nonlocal ev_yfm_used
                for ti in range(2):
                    for nn in range(2):
                        par = nn

                        def mms(eo, ti=ti, nn=nn, par=par):
                            ins = None
                            for c in range(8):
                                ins = eo.matmul(
                                    psc[par][:, :],
                                    yfm[:, c, 128 * ti:128 * (ti + 1)],
                                    wo_sb[:, c, 512 * nn:512 * (nn + 1)],
                                    start=(c == 0), stop=(c == 7))
                            return ins
                        evm = P.emit("tensor", mms,
                                     waits=[ev_yfm, pscf[par]], inc="e_t")
                        ev_yfm_used = evm
                        pscf[par] = P.emit("vector", lambda eo, ti=ti, nn=nn,
                                           par=par, w=w: eo.tensor_tensor(
                                               x_sb[:, 2 * w + ti,
                                                    512 * nn:512 * (nn + 1)],
                                               x_sb[:, 2 * w + ti,
                                                    512 * nn:512 * (nn + 1)],
                                               psc[par][:, :], AL.add),
                                           waits=[evm], inc="e_v")
            return do_wo
        wo_pend = mk_wo()

    wo_pend()
    P.barrier()
    P.run_block()
    stC.close()
    stA.close()

    stD = contextlib.ExitStack()
    qds4 = alloc_qds(stD, 1024, tag="4", npar=2)
    xnqfm = sb(stD, "xnqfm", [128, 8, T], BF16)
    wgq_it = [sb(stD, f"wgq{i}", [128, 8, 512], BF16) for i in range(2)]
    wuq_it = [sb(stD, f"wuq{i}", [128, 8, 512], BF16) for i in range(2)]
    wdq_sb = sb(stD, "wdq_sb", [128, 8, 512], BF16)
    hqfm = sb(stD, "hqfm", [128, 8, T], BF16)
    h_f = [sb(stD, f"h_f{i}", [128, 1024], F32) for i in range(2)]
    hq_b = [sb(stD, f"hq_b{i}", [128, 1024], BF16) for i in range(2)]

    P.phase()
    ev_xnq = P.emit_dma("sync", lambda eo: eo.dma_start(
        out=xnqfm[:, :, :].rearrange("p c s -> p (c s)"), in_=xnqfm_d))

    pmf = [None, None, None, None]
    ptf4 = [None, None]
    h_free = [None, None]
    hq_free = [None, None]
    thirds = [(0, 8, [512, 512]), (8, 8, [512, 512]), (16, 6, [512, 256])]
    buf = 0
    ev_w_used = [None, None]
    pend = [None, None, None]

    dn_jobs = []
    dn_state = {"loaded": -1, "wd_free": None, "pmf": [None, None],
                "ld_evs": {}, "lastmm": {}, "par": 0}

    def emit_dn_load(k, gbase, ncp):
        evs = []
        for cj in range(ncp):
            j = gbase + cj
            off = (j // 3) * SH + 2 * G_SZ + (j % 3) * DN_SZ
            for nn in range(2):
                evs.append(P.emit_dma("sync", lambda eo, cj=cj, nn=nn, off=off:
                                      eo.dma_start(
                    out=wdq_sb[:, 2 * cj + nn, :],
                    in_=wffnq_d[off:off + DN_SZ]
                    .rearrange("(p i) -> p i", p=128)
                    [:, 512 * nn:512 * (nn + 1)]),
                    waits=[ev_wst, dn_state["wd_free"]]))
        dn_state["ld_evs"][k] = evs
        dn_state["loaded"] = k

    def push_dn_jobs(k, cbase, ncp, tte, ev_hq):
        for ttj in (tte, tte + 1):
            for nn in range(2):
                def dj(k=k, cbase=cbase, ncp=ncp, ttj=ttj, nn=nn, ev_hq=ev_hq):
                    par = dn_state["par"]
                    dn_state["par"] ^= 1

                    def dmm(eo, par=par):
                        ins = None
                        for cj in range(ncp):
                            ins = eo.matmul(psc[par][:, :],
                                            hqfm[:, cbase + cj,
                                                 128 * ttj:128 * (ttj + 1)],
                                            wdq_sb[:, 2 * cj + nn, :],
                                            start=(cj == 0),
                                            stop=(cj == ncp - 1))
                        return ins
                    evm = P.emit("tensor", dmm,
                                 waits=[ev_hq, dn_state["ld_evs"][k],
                                        dn_state["pmf"][par]], inc="e_t")
                    dn_state["lastmm"][k] = evm
                    dn_state["wd_free"] = evm
                    dn_state["pmf"][par] = P.emit(
                        "vector", lambda eo, ttj=ttj, nn=nn, par=par:
                        eo.tensor_tensor(
                            x_sb[:, ttj, 512 * nn:512 * (nn + 1)],
                            x_sb[:, ttj, 512 * nn:512 * (nn + 1)],
                            psc[par][:, :], AL.add),
                        waits=[evm], inc="e_v")
                dn_jobs.append((k, dj))

    def pop_dn_jobs(limit):
        n = 0
        while dn_jobs and dn_jobs[0][0] <= dn_state["loaded"] and n < limit:
            dn_jobs.pop(0)[1]()
            n += 1

    pend_evac = [None]

    def flush_pend():
        tc = pend[0]() if pend[0] is not None else None
        ec = pend[1]() if pend[1] is not None else None
        pend_evac[0] = pend[2]
        pend[0], pend[1], pend[2] = None, tc, ec

    def flush_evac():
        if pend_evac[0] is not None:
            pend_evac[0]()
            pend_evac[0] = None

    def drain_pend():
        for _ in range(3):
            flush_pend()
            flush_evac()

    kglob = -1
    part_info = []
    for (ch0, nch, itiles) in thirds:
        ioff = 0
        for it, iw in enumerate(itiles):
            kglob += 1
            part_info.append((ch0 + ioff // 128, (ioff // 128) % 8, iw // 128))
            if kglob > 0:
                emit_dn_load(kglob - 1, part_info[kglob - 1][0],
                             part_info[kglob - 1][2])
            gbuf, ubuf = wgq_it[buf], wuq_it[buf]
            ev_lg = []
            goff = ch0 * 128 + ioff
            for c in range(8):
                base = c * SH
                ev_lg.append(P.emit_dma("sync", lambda eo, base=base, goff=goff,
                                        iw=iw, c=c, gbuf=gbuf: eo.dma_start(
                    out=gbuf[:, c, :iw],
                    in_=wffnq_d[base:base + G_SZ]
                    .rearrange("(p i) -> p i", p=128)[:, goff:goff + iw]),
                    waits=[ev_wst, ev_w_used[buf]]))
                ev_lg.append(P.emit_dma("sync", lambda eo, base=base, goff=goff,
                                        iw=iw, c=c, ubuf=ubuf: eo.dma_start(
                    out=ubuf[:, c, :iw],
                    in_=wffnq_d[base + G_SZ:base + 2 * G_SZ]
                    .rearrange("(p i) -> p i", p=128)[:, goff:goff + iw]),
                    waits=[ev_wst, ev_w_used[buf]]))
            ev_ldw = ev_lg
            chunkbase = ioff // 128
            nblk = iw // 128
            ev_gu_last = None
            for tt in range(NT):
                par = tt & 1
                pp = (tt >> 1) & 1
                pg, pu = pm[par], pm[2 + par]
                if (tt & 1) == 0:
                    pop_dn_jobs(4)
                    flush_pend()

                def gumm(eo, tt=tt, iw=iw, gbuf=gbuf, ubuf=ubuf, pg=pg, pu=pu):
                    ins = None
                    for c in range(8):
                        eo.matmul(pg[:, :iw],
                                  xnqfm[:, c, 128 * tt:128 * (tt + 1)],
                                  gbuf[:, c, :iw],
                                  start=(c == 0), stop=(c == 7))
                        ins = eo.matmul(pu[:, :iw],
                                        xnqfm[:, c, 128 * tt:128 * (tt + 1)],
                                        ubuf[:, c, :iw],
                                        start=(c == 0), stop=(c == 7))
                    return ins
                evg = P.emit("tensor", gumm,
                             waits=[ev_ldw, ev_xnq, pmf[par], pmf[2 + par]],
                             inc="e_t")
                evu = evg
                ev_gu_last = evu
                ev_sl = P.emit("scalar", lambda eo, iw=iw, pg=pg,
                               pp=pp, tt=tt: eo.activation(
                    h_f[pp][:, iw * (tt & 1):iw * ((tt & 1) + 1)],
                    pg[:, :iw], AF.Silu),
                    waits=[evg, h_free[pp] if (tt & 1) == 0 else None],
                    inc="e_s")
                pmf[par] = ev_sl
                ev_h = P.emit("vector", lambda eo, iw=iw, pu=pu,
                              pp=pp, tt=tt: eo.tensor_tensor(
                    h_f[pp][:, iw * (tt & 1):iw * ((tt & 1) + 1)],
                    h_f[pp][:, iw * (tt & 1):iw * ((tt & 1) + 1)],
                    pu[:, :iw], AL.mult),
                    waits=[ev_sl, evu], inc="e_v")
                pmf[2 + par] = ev_h
                if (tt & 1) == 0:
                    continue
                FP = 2 * iw
                e_sq, e_sg = emit_qd_head(P, h_f[pp][:, :FP], qds4, FP,
                                          par=pp, in_ev=ev_h)
                h_free[pp] = e_sg
                flush_evac()

                def mk_pend(pp=pp, tte=tt - 1, nblk=nblk, chunkbase=chunkbase,
                            FP=FP, e_sq=e_sq, e_sg=e_sg, kg=kglob):
                    def do_tail():
                        evq = emit_qd_tail(P, hq_b[pp][:, :FP], qds4, FP,
                                           par=pp, out_free=hq_free[pp],
                                           sq_ev=e_sq, sign_ev=e_sg)

                        def do_transp(evq=evq):
                            evt = None
                            for jb in range(2 * nblk):
                                evt = P.emit("tensor", lambda eo, jb=jb, pp=pp:
                                             eo.transpose(
                                                 pt[pp][:, jb * 128:(jb + 1) * 128],
                                                 hq_b[pp][:, jb * 128:(jb + 1) * 128],
                                                 id_sb[:, :]),
                                             waits=[evq, ptf4[pp],
                                                    dn_state["lastmm"]
                                                    .get(kg - 2)],
                                             inc="e_t")
                            hq_free[pp] = evt

                            def do_evac(evt=evt):
                                ptf4[pp] = P.emit("scalar", lambda eo, tte=tte,
                                                  chunkbase=chunkbase,
                                                  nblk=nblk,
                                                  pp=pp: eo.copy(
                                    hqfm[:, chunkbase:chunkbase + nblk,
                                         128 * tte:128 * (tte + 2)]
                                    .rearrange("p c (o k) -> p o c k", o=2),
                                    pt[pp][:, :2 * nblk * 128]
                                    .rearrange("p (o c k) -> p o c k",
                                               o=2, k=128)),
                                    waits=[evt], inc="e_s")
                                push_dn_jobs(kg, chunkbase, nblk, tte,
                                             ptf4[pp])
                            return do_evac
                        return do_transp
                    return do_tail
                pend[0] = mk_pend()
            ev_w_used[buf] = ev_gu_last
            buf ^= 1
            ioff += iw
    drain_pend()
    pop_dn_jobs(10 ** 9)
    emit_dn_load(kglob, part_info[kglob][0], part_info[kglob][2])
    for p8 in range(8):
        pop_dn_jobs(4)
        evp = ("e_v", P.cnt["e_v"])
        P.emit("sync", lambda eo, p8=p8: eo.dma_start(
            out=out_d.rearrange("p (t d) -> p t d", d=D)[:, 2 * p8:2 * p8 + 2, :],
            in_=x_sb[:, 2 * p8:2 * p8 + 2, :]),
            waits=[evp], inc="d_out", amt=16)
    pop_dn_jobs(10 ** 9)
    P.emit("sync", None, waits=[("d_out", 128)])
    P.barrier()
    P.run_block()
    stD.close()
    root.close()
    return nc


_NC_CACHE = {}


def kernel(L0_attn_x, L0_attn_wq, L0_attn_wk, L0_attn_wv, L0_attn_wo,
           L0_attn_q_gain, L0_ffn_w_gate, L0_ffn_w_up, L0_ffn_w_down):
    f32 = lambda a: np.ascontiguousarray(np.asarray(a, dtype=np.float32))
    x = f32(L0_attn_x).reshape(S, D)
    wqT = np.ascontiguousarray(f32(L0_attn_wq).T)
    wkT = np.ascontiguousarray(f32(L0_attn_wk).T)
    wvT = np.ascontiguousarray(f32(L0_attn_wv).T)
    woT = np.ascontiguousarray(f32(L0_attn_wo).T)
    wg = f32(L0_ffn_w_gate)
    wu = f32(L0_ffn_w_up)
    wd = f32(L0_ffn_w_down)
    wd_pad = np.concatenate(
        [wd, np.zeros((D, 3 * 128 * NCORES - I_FFN), dtype=np.float32)], axis=1)
    gain128 = np.ascontiguousarray(np.tile(f32(L0_attn_q_gain)[None, :],
                                           (128, 1)))

    inv_freq = (1.0 / 10000.0 ** (np.arange(0, HD, 2, dtype=np.float32)
                                  / np.float32(HD))).astype(np.float32)
    freqs = np.outer(np.arange(S, dtype=np.float32), inv_freq).astype(np.float32)
    cosb = np.cos(freqs.astype(np.float64)).astype(ml_dtypes.bfloat16)
    sinb = np.sin(freqs.astype(np.float64)).astype(ml_dtypes.bfloat16)
    tri = np.tril(np.ones((128, 128), dtype=bool))
    m0 = np.where(tri, np.float32(0), np.float32(FMIN)).astype(np.float32)
    mask384 = np.concatenate(
        [m0, np.zeros((128, 128), dtype=np.float32), m0], axis=1)
    ident = np.eye(128, dtype=ml_dtypes.bfloat16)

    if "nc" not in _NC_CACHE:
        _NC_CACHE["nc"] = build_program()
    nc = _NC_CACHE["nc"]

    def pmaj(a, tile):
        nt = a.shape[0] // 128
        return np.ascontiguousarray(
            a.reshape(nt, 128, tile).swapaxes(0, 1).reshape(128, nt * tile))

    wqTs = pmaj(wqT, D)
    wkTs = pmaj(wkT, D)
    wvTs = pmaj(wvT, D)
    woTs = pmaj(woT, D)
    in_maps = []
    for c in range(NCORES):
        sl = slice(c * T, (c + 1) * T)
        wdc = wd_pad[:, 384 * c:384 * (c + 1)]
        wds = np.ascontiguousarray(
            wdc.reshape(8, 128, 3, 128).transpose(1, 2, 0, 3)
            .reshape(128, 3 * 8 * 128))
        in_maps.append({
            "x": pmaj(x[sl], D),
            "wqT": wqTs, "wkT": wkTs, "wvT": wvTs, "woT": woTs,
            "wg": pmaj(np.ascontiguousarray(wg[:, 128 * c:128 * (c + 1)]), 128),
            "wu": pmaj(np.ascontiguousarray(wu[:, 128 * c:128 * (c + 1)]), 128),
            "wd": wds,
            "cosb": pmaj(cosb[sl], 32),
            "sinb": pmaj(sinb[sl], 32),
            "gain128": gain128, "mask0": mask384, "ident": ident,
        })
    res = run_bass_kernel_spmd(nc, in_maps, core_ids=list(range(NCORES)))
    out = np.concatenate(
        [r["out"].reshape(128, NT, D).swapaxes(0, 1).reshape(T, D)
         for r in res.results], axis=0)
    return out.reshape(1, S, D).astype(np.float32)


# version 40
# speedup vs baseline: 1.0541x; 1.0007x over previous
import contextlib
import os
import numpy as np
import ml_dtypes
import concourse.bass as bass
import concourse.mybir as mybir
from concourse.bass_utils import run_bass_kernel_spmd

CHAIN_WAITS = (os.environ.get("BASS_SIM_FP64") == "1"
               or os.environ.get("KERNEL_CHAIN_WAITS") == "1")

F32 = mybir.dt.float32
BF16 = mybir.dt.bfloat16
AL = mybir.AluOpType
AF = mybir.ActivationFunctionType
AX = mybir.AxisListType

NCORES = 8
S, D = 16384, 1024
T = S // NCORES
NT = T // 128
H, HD = 16, 64
NWIN = T // 256
I_FFN = 2816
FMIN = float(np.finfo(np.float32).min)
M2 = [0.0625, 0.5625, 1.5625, 3.0625, 6.25, 12.25, 25.0]
WTS = [0.5, 0.5, 0.5, 0.5, 1.0, 1.0, 2.0]
RMS_EPS = 1.1920928955078125e-07

G_SZ = 128 * 2816
DN_SZ = 128 * 1024
SH = 2 * G_SZ + 3 * DN_SZ

ENGS = ("sync", "gpsimd", "tensor", "vector", "scalar")


class Sched:

    CHAIN_SEM = {"vector": "e_v", "scalar": "e_s", "gpsimd": "e_g"}

    def __init__(self, nc, sems, dpool):
        self.nc = nc
        self.sem = sems
        self.cnt = {k: 0 for k in sems}
        self.waited = {e: {} for e in ENGS}
        self.dpool = dpool
        self.didx = {e: 0 for e in dpool}
        self.q = None

    def phase(self):
        self.q = {e: [] for e in ENGS}

    def emit(self, eng, fn, waits=(), inc=None, amt=1, chain=True):
        waits = list(waits)
        cs = self.CHAIN_SEM.get(eng)
        if fn is not None and cs is not None:
            if chain and self.cnt.get(cs, 0):
                waits.append((cs, self.cnt[cs]))
            if inc is None:
                inc = cs
        flat = []
        for ev in waits:
            if isinstance(ev, list):
                flat.extend(ev)
            else:
                flat.append(ev)
        wd = self.waited[eng]
        ws = []
        for ev in flat:
            if ev is None:
                continue
            s, v = ev
            if v is None or v <= 0 or wd.get(s, 0) >= v:
                continue
            wd[s] = v
            ws.append((s, v))
        ev = None
        if inc is not None:
            self.cnt[inc] += amt
            ev = (inc, self.cnt[inc])

        def thunk(eo, fn=fn, ws=ws, inc=inc, amt=amt):
            for s, v in ws:
                eo.wait_ge(self.sem[s], v)
            ins = fn(eo) if fn is not None else None
            if inc is not None:
                if ins is None:
                    ins = eo.nop()
                ins.then_inc(self.sem[inc], amt)

        self.q[eng].append(thunk)
        return ev

    def emit_dma(self, eng, fn, waits=()):
        pool = self.dpool[eng]
        i = self.didx[eng]
        self.didx[eng] += 1
        s = pool[i % len(pool)]
        rounds = i // len(pool)
        w = list(waits) + [(s, 16 * rounds)]
        return self.emit(eng, fn, waits=w, inc=s, amt=16)

    def barrier(self):
        for eng, pool in self.dpool.items():
            dws = [(s, self.cnt[s]) for s in pool if self.cnt[s]]
            if dws:
                self.emit(eng, None, waits=dws)
        evs = [self.emit(e, None, inc="b_" + e) for e in ENGS]
        for e in ENGS:
            self.emit(e, None, waits=evs)

    def run_block(self):
        with self.nc.Block() as blk:
            for e in ENGS:
                q = self.q[e]
                if not q:
                    continue

                def runner(eo, q=q):
                    for t in q:
                        t(eo)

                getattr(blk, e)(runner)
        self.q = None


def emit_qd_head_part(P, xin, scr, F, c0, c1, par=0, in_ev=None,
                      rstd=None, rstd6=None, last=True):
    nb = F // 16
    b0, b1 = c0 // 16, c1 // 16
    bm, rsc = scr["bm"], scr["rsc"]
    sc_, rsq = scr["sc"][par], scr["rsq"][par]
    a2, sgn = scr["a2"][par], scr["sgn"][par]
    P.emit("vector", lambda eo, xin=xin, b0=b0, b1=b1, c0=c0, c1=c1:
           eo.tensor_reduce(
        bm[:, b0:b1], xin[:, c0:c1].rearrange("p (b k) -> p b k", k=16),
        AX.X, AL.max, apply_absolute_value=True), inc="e_v")
    if last:
        s6 = rstd6 if rstd6 is not None else 1.0 / 6.0
        P.emit("vector", lambda eo, nb=nb, sc_=sc_, s6=s6: eo.tensor_scalar(
            sc_[:, :nb], bm[:, :nb], s6, 1e-12, AL.mult, AL.max),
            waits=[scr["ev_sc_free"][par]], inc="e_v")
        P.emit("vector", lambda eo, nb=nb, sc_=sc_: eo.reciprocal(
            rsc[:, :nb], sc_[:, :nb]), inc="e_v")
        P.emit("vector", lambda eo, nb=nb, rsq=rsq: eo.tensor_tensor(
            rsq[:, :nb], rsc[:, :nb], rsc[:, :nb], AL.mult), inc="e_v")
    if rstd is not None:
        e_sq = P.emit("scalar", lambda eo, xin=xin, c0=c0, c1=c1, a2=a2,
                      rstd=rstd: eo.activation(
            a2[:, c0:c1], xin[:, c0:c1], AF.Square, scale=rstd),
            waits=[in_ev, scr["ev_cmp_free"][par] if c0 == 0 else None],
            inc="e_s")
    else:
        e_sq = P.emit("scalar", lambda eo, xin=xin, c0=c0, c1=c1, a2=a2:
                      eo.activation(a2[:, c0:c1], xin[:, c0:c1], AF.Square),
            waits=[in_ev, scr["ev_cmp_free"][par] if c0 == 0 else None],
            inc="e_s")
    e_sg = P.emit("scalar", lambda eo, xin=xin, c0=c0, c1=c1, sgn=sgn: eo.sign(
        sgn[:, c0:c1], xin[:, c0:c1]),
        waits=[scr["ev_sgn_free"][par] if c0 == 0 else None], inc="e_s")
    return e_sq, e_sg


def emit_qd_head(P, xin, scr, F, par=0, in_ev=None, rstd=None, rstd6=None):
    return emit_qd_head_part(P, xin, scr, F, 0, F, par=par, in_ev=in_ev,
                             rstd=rstd, rstd6=rstd6, last=True)


def emit_qd_tail(P, qout, scr, F, par=0, out_free=None, sq_ev=None,
                 sign_ev=None):
    nb = F // 16
    cb, g0 = scr["cb"][par], scr["g0"][par]
    sc_, rsq = scr["sc"][par], scr["rsq"][par]
    a2, sgn = scr["a2"][par], scr["sgn"][par]
    sA, gA = scr["sA"], scr["gA"]
    ev_a2 = P.emit("vector", lambda eo, F=F, nb=nb, a2=a2, rsq=rsq:
                   eo.tensor_tensor(
        a2[:, :F].rearrange("p (b k) -> p b k", k=16),
        a2[:, :F].rearrange("p (b k) -> p b k", k=16),
        rsq[:, :nb].unsqueeze(2).broadcast_to([128, nb, 16]), AL.mult),
        waits=[sq_ev, scr["ev_sgn_a2"][par]], inc="e_v")
    for i in range(5):
        dst = g0 if i == 0 else cb
        ev_ts = P.emit("vector", lambda eo, dst=dst, F=F, i=i: eo.tensor_scalar(
            dst[:, :F], a2[:, :F], M2[i], WTS[i], AL.is_gt, AL.mult),
            waits=[scr["ev_g0cb_free"][par]] if i <= 1 else (), inc="e_v")
        if i > 0:
            P.emit("vector", lambda eo, F=F: eo.tensor_tensor(
                g0[:, :F], g0[:, :F], cb[:, :F], AL.add), inc="e_v")
    scr["ev_cmp_free"][par] = ev_ts
    P.emit("scalar", lambda eo, F=F: eo.activation(
        sA[:, :F], a2[:, :F], AF.Sign, bias=scr["m2n"][:, 1:2]),
        waits=[ev_a2], inc="e_s")
    eR5 = P.emit("scalar", lambda eo, F=F: eo.activation(
        gA[:, :F], sA[:, :F], AF.Relu),
        waits=[scr["ev_gA_free"][0]], inc="e_s")
    evA = P.emit("vector", lambda eo, F=F: eo.tensor_tensor(
        g0[:, :F], g0[:, :F], gA[:, :F], AL.add), waits=[eR5], inc="e_v")
    gB = scr["gB"] if scr["gB"] is not None else gA
    gBw = scr["ev_gB_free"][0] if scr["gB"] is not None else evA
    P.emit("scalar", lambda eo, F=F: eo.activation(
        sA[:, :F], a2[:, :F], AF.Sign, bias=scr["m2n"][:, 2:3]), inc="e_s")
    eR6 = P.emit("scalar", lambda eo, F=F, gB=gB: eo.activation(
        gB[:, :F], sA[:, :F], AF.Relu, scale=2.0), waits=[gBw], inc="e_s")
    evC = P.emit("vector", lambda eo, F=F, gB=gB: eo.tensor_tensor(
        g0[:, :F], g0[:, :F], gB[:, :F], AL.add), waits=[eR6], inc="e_v")
    scr["ev_gA_free"][0] = evA if scr["gB"] is not None else evC
    scr["ev_gB_free"][0] = evC
    scr["ev_sgn_a2"][par] = eR6
    ev_cbm = P.emit("gpsimd", lambda eo, F=F, sgn=sgn: eo.tensor_tensor(
        cb[:, :F], g0[:, :F], sgn[:, :F], AL.mult),
        waits=[sign_ev, evC], inc="e_g")
    scr["ev_sgn_free"][par] = ev_cbm
    evq = P.emit("gpsimd", lambda eo, qout=qout, F=F, nb=nb, sc_=sc_:
                 eo.tensor_tensor(
        qout.rearrange("p (b k) -> p b k", k=16),
        cb[:, :F].rearrange("p (b k) -> p b k", k=16),
        sc_[:, :nb].unsqueeze(2).broadcast_to([128, nb, 16]), AL.mult),
        waits=[out_free], inc="e_g")
    scr["ev_g0cb_free"][par] = evq
    scr["ev_sc_free"][par] = evq
    return evq


def emit_qd(P, xin, qout, scr, F, out_free=None, in_ev=None):
    e_sq, e_sg = emit_qd_head(P, xin, scr, F, par=0, in_ev=in_ev)
    evq = emit_qd_tail(P, qout, scr, F, par=0, out_free=out_free,
                       sq_ev=e_sq, sign_ev=e_sg)
    return evq, e_sg


def build_program():
    nc = bass.Bass("TRN2", target_bir_lowering=False, debug=False)

    x_d = nc.dram_tensor("x", [128, NT * D], F32, kind="ExternalInput").ap()
    wqT_d = nc.dram_tensor("wqT", [128, 8 * D], F32, kind="ExternalInput").ap()
    wkT_d = nc.dram_tensor("wkT", [128, 8 * D], F32, kind="ExternalInput").ap()
    wvT_d = nc.dram_tensor("wvT", [128, 8 * D], F32, kind="ExternalInput").ap()
    woT_d = nc.dram_tensor("woT", [128, 8 * D], F32, kind="ExternalInput").ap()
    wg_d = nc.dram_tensor("wg", [128, I_FFN], F32, kind="ExternalInput").ap()
    wu_d = nc.dram_tensor("wu", [128, I_FFN], F32, kind="ExternalInput").ap()
    wd_d = nc.dram_tensor("wd", [128, 3 * 8 * 128], F32, kind="ExternalInput").ap()
    cos_d = nc.dram_tensor("cosb", [128, NT * 32], BF16, kind="ExternalInput").ap()
    sin_d = nc.dram_tensor("sinb", [128, NT * 32], BF16, kind="ExternalInput").ap()
    gain_d = nc.dram_tensor("gain128", [128, H], F32, kind="ExternalInput").ap()
    mask_d = nc.dram_tensor("mask0", [128, 384], F32, kind="ExternalInput").ap()
    id_d = nc.dram_tensor("ident", [128, 128], BF16, kind="ExternalInput").ap()
    out_d = nc.dram_tensor("out", [128, NT * D], F32, kind="ExternalOutput").ap()
    ag_in = nc.dram_tensor("ag_in", [SH], BF16).ap()
    wffnq_d = nc.dram_tensor("wffnq", [NCORES * SH], BF16,
                             addr_space="Shared").ap()
    xnqfm_d = nc.dram_tensor("xnqfm_dr", [128, 8 * T], BF16).ap()

    root = contextlib.ExitStack()
    sems = {}
    snames = (["d_out", "e_t", "e_v", "e_s", "e_g",
               "b_sync", "b_gpsimd", "b_tensor", "b_vector", "b_scalar"]
              + [f"ds{i}" for i in range(12)] + [f"dg{i}" for i in range(4)])
    for s in snames:
        sems[s] = root.enter_context(nc.semaphore(s))
    P = Sched(nc, sems, {"sync": [f"ds{i}" for i in range(12)],
                         "gpsimd": [f"dg{i}" for i in range(4)]})

    def sb(st, name, shape, dt):
        return st.enter_context(nc.sbuf_tensor(name, shape, dt))

    x_sb = sb(root, "x_sb", [128, NT, D], F32)
    stat = sb(root, "stat", [128, 16], F32)
    m2n = sb(root, "m2n", [128, 3], F32)
    gain_sb = sb(root, "gain_sb", [128, H], F32)
    mask_sb = sb(root, "mask_sb", [128, 384], F32)
    id_sb = sb(root, "id_sb", [128, 128], BF16)
    pm = [root.enter_context(nc.psum_tensor(f"pm{i}", [128, 512], F32))
          for i in range(4)]
    pt = [root.enter_context(nc.psum_tensor(f"pt{i}", [128, 1024], BF16))
          for i in range(2)]
    psc = [root.enter_context(nc.psum_tensor(f"psc{i}", [128, 512], F32))
           for i in range(2)]

    stA = contextlib.ExitStack()
    xnfm = sb(stA, "xnfm", [128, 8, T], BF16)
    wq_sb = sb(stA, "wq_sb", [128, 8, D], BF16)
    wk_sb = sb(stA, "wk_sb", [128, 8, D], BF16)
    wv_sb = sb(stA, "wv_sb", [128, 8, D], BF16)
    wo_sb = sb(stA, "wo_sb", [128, 8, D], BF16)
    cos_sb = sb(stA, "cos_sb", [128, NT, 32], BF16)
    sin_sb = sb(stA, "sin_sb", [128, NT, 32], BF16)

    def alloc_qds(st, F, tag="", npar=1, gb=False):
        return {
            "m2n": m2n,
            "bm": sb(st, f"qd_bm{tag}", [128, F // 16], F32),
            "rsc": sb(st, f"qd_rsc{tag}", [128, F // 16], F32),
            "a2": [sb(st, f"qd_a2{tag}{i}", [128, F], F32)
                   for i in range(npar)],
            "cb": [sb(st, f"qd_cb{tag}{i}", [128, F], BF16)
                   for i in range(npar)],
            "g0": [sb(st, f"qd_g0{tag}{i}", [128, F], BF16)
                   for i in range(npar)],
            "sc": [sb(st, f"qd_sc{tag}{i}", [128, F // 16], F32)
                   for i in range(npar)],
            "rsq": [sb(st, f"qd_rsq{tag}{i}", [128, F // 16], F32)
                    for i in range(npar)],
            "sgn": [sb(st, f"qd_sgn{tag}{i}", [128, F], BF16)
                    for i in range(npar)],
            "sA": sb(st, f"qd_sA{tag}", [128, F], BF16),
            "gA": sb(st, f"qd_gA{tag}", [128, F], BF16),
            "gB": sb(st, f"qd_gB{tag}", [128, F], BF16) if gb else None,
            "ev_gA_free": [None],
            "ev_gB_free": [None],
            "ev_g0cb_free": [None] * npar,
            "ev_sgn_free": [None] * npar,
            "ev_sc_free": [None] * npar,
            "ev_cmp_free": [None] * npar,
            "ev_sgn_a2": [None] * npar,
        }

    stB = contextlib.ExitStack()
    qds = alloc_qds(stB, 1024, npar=2)
    wsh = sb(stB, "wsh", [128, 1024], F32)
    qsh = sb(stB, "qsh", [128, 1024], BF16)
    fmo = sb(stB, "fmo", [128, 1024], BF16)
    xc = sb(stB, "xc", [128, D], F32)
    xnb = sb(stB, "xnb", [128, D], BF16)
    xnq = [sb(stB, f"xnq{i}", [128, D], BF16) for i in range(2)]
    xnqf = xnb

    P.phase()
    for _i in range(3):
        P.emit("vector", lambda eo, _i=_i: eo.memset(
            m2n[:, _i:_i + 1], -M2[4 + _i]), inc="e_v")
    ev_x = []
    for q in range(4):
        ev_x.append(P.emit_dma("sync", lambda eo, q=q: eo.dma_start(
            out=x_sb[:, 4 * q:4 * (q + 1), :],
            in_=x_d.rearrange("p (t d) -> p t d", d=D)[:, 4 * q:4 * (q + 1), :])))
    ev_cos = P.emit_dma("sync", lambda eo: eo.dma_start(
        out=cos_sb[:, :, :], in_=cos_d.rearrange("p (t j) -> p t j", j=32)))
    ev_sin = P.emit_dma("sync", lambda eo: eo.dma_start(
        out=sin_sb[:, :, :], in_=sin_d.rearrange("p (t j) -> p t j", j=32)))
    ev_gld = P.emit_dma("sync", lambda eo: eo.dma_start(out=gain_sb[:, :],
                                                        in_=gain_d))
    ev_mask = P.emit_dma("sync", lambda eo: eo.dma_start(out=mask_sb[:, :],
                                                         in_=mask_d))
    ev_id = P.emit_dma("sync", lambda eo: eo.dma_start(out=id_sb[:, :],
                                                       in_=id_d))
    ev_attnw = []
    attnw_pend = [(wqT_d, wq_sb), (wkT_d, wk_sb), (wvT_d, wv_sb),
                  (woT_d, wo_sb)]
    ev_gain = P.emit("vector", lambda eo: eo.tensor_scalar(
        gain_sb[:, :], gain_sb[:, :], 0.125, None, AL.mult),
        waits=[ev_gld], inc="e_v")

    st_wqd = {"evq": None, "esg": None, "pt": [None, None], "store": None,
              "ptsel": 0, "tr": None}

    def wqd_piece(src_ap, ntile, off, Wrow, col0):
        F = ntile * 128
        par = st_wqd["ptsel"] & 1
        st_wqd["ptsel"] += 1
        vsnap = ("e_v", P.cnt["e_v"])
        evd = P.emit_dma("sync", lambda eo, src_ap=src_ap, F=F: eo.dma_start(
            out=wsh[:, :F], in_=src_ap),
            waits=[st_wqd["evq"], st_wqd["esg"]])
        P.emit("vector", None, waits=[evd])
        e_sq, e_sg = emit_qd_head(P, wsh[:, :F], qds, F, par=par,
                                  in_ev=[evd, vsnap])
        st_wqd["evq"] = ("e_v", P.cnt["e_v"])
        st_wqd["esg"] = e_sg
        evq = emit_qd_tail(P, qsh[:, :F], qds, F, par=par, sq_ev=e_sq,
                           sign_ev=e_sg, out_free=st_wqd["tr"])
        bank = pt[0]
        evt = None
        for j in range(ntile):
            evt = P.emit("tensor", lambda eo, j=j, bank=bank: eo.transpose(
                bank[:, j * 128:(j + 1) * 128], qsh[:, j * 128:(j + 1) * 128],
                id_sb[:, :]), waits=[evq, ev_id, st_wqd["pt"][0]], inc="e_t")
        st_wqd["tr"] = evt
        ev_ev = P.emit("vector", lambda eo, F=F, bank=bank: eo.tensor_copy(
            fmo[:, :F], bank[:, :F]), waits=[evt, st_wqd["store"]], inc="e_v")
        st_wqd["pt"][0] = ev_ev
        st_wqd["store"] = P.emit_dma("sync", lambda eo, off=off, Wrow=Wrow, F=F,
                                     col0=col0: eo.dma_start(
            out=ag_in[off:off + 128 * Wrow]
                .rearrange("(p i) -> p i", p=128)[:, col0:col0 + F],
            in_=fmo[:, :F]), waits=[ev_ev])
        return st_wqd["store"]

    ag_waits = []
    piece_args = []
    for (w_ap, roff) in ((wg_d, 0), (wu_d, G_SZ)):
        for (t0, ntile) in ((0, 8), (8, 8), (16, 6)):
            piece_args.append((
                w_ap[:, t0 * 128:(t0 + ntile) * 128],
                ntile, roff, 2816, t0 * 128))
    for pos in range(3):
        piece_args.append((
            wd_d[:, 1024 * pos:1024 * (pos + 1)],
            8, 2 * G_SZ + pos * DN_SZ, 1024, 0))

    ev_xnb_free = None
    ev_sg_prev = None
    ev_xnq_free = [None, None]
    ev_qf_free = None
    ln_pend = [None, None]

    def ln_flush():
        if ln_pend[1] is not None:
            ln_pend[1]()
            ln_pend[1] = None
        if ln_pend[0] is not None:
            ln_pend[1] = ln_pend[0]()
            ln_pend[0] = None

    for t in range(NT):
        par = t & 1
        if t >= 2 and t % 2 == 0 and attnw_pend:
            w_d, w_s = attnw_pend.pop(0)
            ev_attnw.append(P.emit_dma("gpsimd", lambda eo, w_d=w_d, w_s=w_s:
                                       eo.dma_start(
                out=w_s[:, :, :].rearrange("p c n -> p (c n)"), in_=w_d)))
        if piece_args:
            ln_flush()
            ag_waits.append(wqd_piece(*piece_args.pop(0)))
        ev_vsnap = ("e_v", P.cnt["e_v"])
        ev_cp = P.emit("scalar", lambda eo, t=t: eo.activation(
            xc[:, :], x_sb[:, t, :], AF.Copy, accum_out=stat[:, 0:1]),
            waits=[ev_x[t // 4], ev_vsnap], inc="e_s")
        ev_nm = P.emit("vector", lambda eo: eo.tensor_scalar(
            stat[:, 1:2], stat[:, 0:1], -1.0 / D, None, AL.mult),
            waits=[ev_cp], inc="e_v")
        P.emit("scalar", lambda eo: eo.activation(
            xc[:, :], xc[:, :], AF.Identity, bias=stat[:, 1:2]),
            waits=[ev_nm], inc="e_s")
        ev_acc = P.emit("scalar", lambda eo: eo.activation(
            qsh[:, :], xc[:, :], AF.Square, accum_out=stat[:, 2:3]),
            inc="e_s")
        ln_flush()
        ev = P.emit("vector", lambda eo: eo.tensor_scalar(
            stat[:, 3:4], stat[:, 2:3], 1.0 / D, 1e-5, AL.mult, AL.add),
            waits=[ev_acc], inc="e_v")
        ev = P.emit("scalar", lambda eo: eo.activation(
            stat[:, 4:5], stat[:, 3:4], AF.Ln), waits=[ev], inc="e_s")
        ev = P.emit("scalar", lambda eo: eo.activation(
            stat[:, 5:6], stat[:, 4:5], AF.Exp, scale=-0.5), inc="e_s")
        P.emit("vector", lambda eo: eo.tensor_scalar(
            stat[:, 6:7], stat[:, 5:6], 1.0 / 6.0, None, AL.mult),
            waits=[ev], inc="e_v")
        ev_xnb = P.emit("scalar", lambda eo: eo.activation(
            xnb[:, :], xc[:, :], AF.Copy, scale=stat[:, 5:6]),
            waits=[ev_xnb_free, ev_qf_free], inc="e_s")
        e_sq, e_sg = emit_qd_head(P, xc[:, :], qds, D, par=par,
                                  in_ev=ev_sg_prev, rstd=stat[:, 5:6],
                                  rstd6=stat[:, 6:7])
        ev_sg_prev = e_sg
        evt0 = None
        for c in range(8):
            evt0 = P.emit("tensor", lambda eo, c=c: eo.transpose(
                pt[0][:, c * 128:(c + 1) * 128], xnb[:, c * 128:(c + 1) * 128],
                id_sb[:, :]),
                waits=[ev_xnb, ev_id, st_wqd["pt"][0]], inc="e_t")
        ev_xnb_free = evt0

        def mk_ln_pend(t=t, par=par, evt0=evt0, e_sq=e_sq, e_sg=e_sg):
            def do():
                ev_fm = P.emit("scalar", lambda eo, t=t: eo.copy(
                    xnfm[:, :, 128 * t:128 * (t + 1)],
                    pt[0][:, :].rearrange("p (c k) -> p c k", k=128)),
                    waits=[evt0], inc="e_s")
                st_wqd["pt"][0] = ev_fm
                evq = emit_qd_tail(P, xnq[par][:, :], qds, D, par=par,
                                   out_free=ev_xnq_free[par], sq_ev=e_sq,
                                   sign_ev=e_sg)
                evt1 = None
                for c in range(8):
                    evt1 = P.emit("tensor", lambda eo, c=c, par=par:
                                  eo.transpose(
                        pt[1][:, c * 128:(c + 1) * 128],
                        xnq[par][:, c * 128:(c + 1) * 128], id_sb[:, :]),
                        waits=[evq, st_wqd["pt"][1]], inc="e_t")
                ev_xnq_free[par] = evt1

                def do_evac(evt1=evt1, t=t):
                    nonlocal ev_qf_free
                    ev_qf = P.emit("scalar", lambda eo: eo.copy(
                        xnqf[:, :].rearrange("p (c k) -> p c k", k=128),
                        pt[1][:, :].rearrange("p (c k) -> p c k", k=128)),
                        waits=[evt1, ev_qf_free, ev_xnb_free], inc="e_s")
                    st_wqd["pt"][1] = ev_qf
                    ev_qf_free = P.emit_dma("sync", lambda eo, t=t: eo.dma_start(
                        out=xnqfm_d.rearrange("p (c s) -> p c s", s=T)
                            [:, :, 128 * t:128 * (t + 1)],
                        in_=xnqf[:, :].rearrange("p (c k) -> p c k", k=128)),
                        waits=[ev_qf])
                return do_evac
            return do
        ln_pend[0] = mk_ln_pend()

    for _ in range(2):
        ln_flush()
    ev_wst = P.emit("gpsimd", lambda eo: eo.collective_compute(
        "AllGather", AL.bypass, replica_groups=[list(range(NCORES))],
        ins=[ag_in.opt()], outs=[wffnq_d.opt()]),
        waits=ag_waits, inc="e_g")

    P.barrier()
    P.run_block()
    stB.close()

    stC = contextlib.ExitStack()
    qk_sb = [sb(stC, f"qk{i}", [128, D], BF16) for i in range(2)]
    qsw = sb(stC, "qsw", [128, D], BF16)
    qm1 = sb(stC, "qm1", [128, D], BF16)
    sq3 = sb(stC, "sq3", [128, D], BF16)
    rstd_b = sb(stC, "rstd_b", [128, H], BF16)
    qn_b = sb(stC, "qn_b", [128, 2, D], BF16)
    kn_b = sb(stC, "kn_b", [128, 2, D], BF16)
    v_sb = sb(stC, "v_sb", [128, 2, D], BF16)
    qfm = sb(stC, "qfm", [128, 8, 256], BF16)
    kfm = sb(stC, "kfm", [128, 8, 256], BF16)
    yfm = sb(stC, "yfm", [128, 8, 256], BF16)
    sc_sb = [sb(stC, f"sc{i}", [128, 384], F32) for i in range(2)]
    e_sb = [sb(stC, f"e{i}", [128, 384], BF16) for i in range(3)]
    p_sb = [sb(stC, f"p{i}", [128, 384], BF16) for i in range(2)]
    pT_sb = [sb(stC, f"pT{i}", [128, 384], BF16) for i in range(2)]
    ast = sb(stC, "ast", [128, 32], F32)

    P.phase()
    pmf = [None, None, None, None]
    ptf = [None, None]
    pscf = [None, None]
    sc_free = [None, None]
    e_free = [None] * 3
    p_free = [None] * 2
    pT_free = [None] * 2
    ev_qk_free = [None, None]
    ev_dst_free = None
    ev_v_used = None
    ev_yfm_used = None
    ev_sq3_free = None

    v4 = lambda ap: ap.rearrange("p (h o j) -> p h o j", o=2, j=32)
    wo_pend = None
    for w in range(NWIN):
        if wo_pend is not None:
            wo_pend()
            wo_pend = None
        rounds = [("q", 0), ("q", 1), ("k", 0), ("k", 1), ("v", 0), ("v", 1)]
        rp = [None, None, None]

        def rflush():
            a = rp[0]() if rp[0] is not None else None
            b = rp[1]() if rp[1] is not None else None
            if rp[2] is not None:
                rp[2]()
            rp[0], rp[1], rp[2] = None, a, b

        def mk_back(kind, ti, par, outs, w=w):
            isq = kind == "q"
            dstb = qn_b if isq else kn_b
            fmdst = qfm if isq else kfm
            t = 2 * w + ti

            def back():
                nonlocal ev_sq3_free
                ev1 = P.emit("scalar", lambda eo, par=par: eo.activation(
                    sq3[:, :], qk_sb[par][:, :], AF.Square),
                    waits=outs + [ev_sq3_free], inc="e_s")
                ev2 = P.emit("vector", lambda eo: eo.tensor_reduce(
                    stat[:, 0:16], sq3[:, :].rearrange("p (h d) -> p h d", d=HD),
                    AX.X, AL.add), waits=[ev1], inc="e_v")
                ev_sq3_free = ev2
                ev3 = P.emit("vector", lambda eo: eo.tensor_scalar(
                    stat[:, 0:16], stat[:, 0:16], 1.0 / HD, RMS_EPS,
                    AL.mult, AL.add), inc="e_v")
                P.emit("scalar", lambda eo: eo.activation(
                    stat[:, 0:16], stat[:, 0:16], AF.Ln), waits=[ev3], inc="e_s")
                ev5 = P.emit("scalar", lambda eo: eo.activation(
                    rstd_b[:, :], stat[:, 0:16], AF.Exp, scale=-0.5), inc="e_s")
                if isq:
                    ev5 = P.emit("vector", lambda eo: eo.tensor_tensor(
                        rstd_b[:, :], rstd_b[:, :], gain_sb[:, :], AL.mult),
                        waits=[ev5, ev_gain], inc="e_v")
                P.emit("vector", lambda eo, t=t, par=par: eo.tensor_tensor(
                    v4(qsw[:, :])[:, :, 0, :], v4(qk_sb[par][:, :])[:, :, 1, :],
                    sin_sb[:, t, :].unsqueeze(1).broadcast_to([128, H, 32]),
                    AL.mult), waits=[ev_sin], inc="e_v")
                P.emit("vector", lambda eo, t=t, par=par:
                       eo.scalar_tensor_tensor(
                    v4(qsw[:, :])[:, :, 1, :], v4(qk_sb[par][:, :])[:, :, 0, :],
                    -1.0,
                    sin_sb[:, t, :].unsqueeze(1).broadcast_to([128, H, 32]),
                    AL.mult, AL.mult), inc="e_v")
                ev_qk_free[par] = P.emit("vector", lambda eo, t=t, par=par:
                                         eo.tensor_tensor(
                    v4(qm1[:, :]), v4(qk_sb[par][:, :]),
                    cos_sb[:, t, :].unsqueeze(1).unsqueeze(2)
                    .broadcast_to([128, H, 2, 32]), AL.mult),
                    waits=[ev_cos], inc="e_v")
                P.emit("vector", lambda eo: eo.tensor_tensor(
                    qm1[:, :], qm1[:, :], qsw[:, :], AL.add), inc="e_v")
                ev8 = P.emit("vector", lambda eo, dstb=dstb, ti=ti:
                             eo.tensor_tensor(
                    dstb[:, ti, :].rearrange("p (h d) -> p h d", d=HD),
                    qm1[:, :].rearrange("p (h d) -> p h d", d=HD),
                    rstd_b[:, :].unsqueeze(2).broadcast_to([128, H, HD]),
                    AL.mult), waits=[ev5, ev_dst_free], inc="e_v")

                def transp():
                    nonlocal ev_dst_free
                    evt = None
                    for c in range(8):
                        evt = P.emit("tensor", lambda eo, c=c, ti=ti, dstb=dstb,
                                     par=par: eo.transpose(
                                         pt[par][:, c * 128:(c + 1) * 128],
                                         dstb[:, ti, c * 128:(c + 1) * 128],
                                         id_sb[:, :]),
                                     waits=[ev8, ptf[par]], inc="e_t")
                    ev_dst_free = evt

                    def fmevac():
                        ptf[par] = P.emit("vector", lambda eo, ti=ti,
                                          fmdst=fmdst, par=par: eo.tensor_copy(
                            fmdst[:, :, 128 * ti:128 * (ti + 1)],
                            pt[par][:, :].rearrange("p (c k) -> p c k", k=128)),
                            waits=[evt], inc="e_v")
                    return fmevac
                return transp
            return back

        for s in range(len(rounds) + 3):
            if s < len(rounds):
                kind, ti = rounds[s]
                par = s & 1
                banks = (2, 3) if par == 0 else (0, 1)
                wsb = wq_sb if kind == "q" else (wk_sb if kind == "k" else wv_sb)
                evms = []
                for nn in range(2):
                    bank = banks[nn]

                    def mms(eo, ti=ti, nn=nn, w=w, wsb=wsb, bank=bank):
                        ins = None
                        for c in range(8):
                            ins = eo.matmul(
                                pm[bank][:, :],
                                xnfm[:, c, (2 * w + ti) * 128:(2 * w + ti + 1) * 128],
                                wsb[:, c, 512 * nn:512 * (nn + 1)],
                                start=(c == 0), stop=(c == 7))
                        return ins
                    evms.append(P.emit("tensor", mms,
                                       waits=[ev_attnw, pmf[bank]], inc="e_t"))
            rflush()
            if s < len(rounds):
                outs = []
                for nn in range(2):
                    bank = banks[nn]
                    if kind == "v":
                        ev = P.emit("scalar", lambda eo, ti=ti, nn=nn, bank=bank:
                                    eo.copy(v_sb[:, ti, 512 * nn:512 * (nn + 1)],
                                            pm[bank][:, :]),
                                    waits=[evms[nn], ev_v_used], inc="e_s")
                    else:
                        ev = P.emit("scalar", lambda eo, nn=nn, bank=bank,
                                    par=par: eo.copy(
                                        qk_sb[par][:, 512 * nn:512 * (nn + 1)],
                                        pm[bank][:, :]),
                                    waits=[evms[nn], ev_qk_free[par]], inc="e_s")
                    pmf[bank] = ev
                    outs.append(ev)
                if kind != "v":
                    rp[0] = mk_back(kind, ti, par, outs)
        ev_qkfm = ("e_v", P.cnt["e_v"])
        ev_v = ("e_s", P.cnt["e_s"])

        exp_evs = [None] * 4
        score_evs = [None, None]

        def stage_a(h):
            par = h & 1
            c, off = h >> 1, 64 * (h & 1)
            score_evs[par] = P.emit("tensor", lambda eo, c=c, off=off, par=par: (
                eo.matmul(psc[par][:, 0:128], qfm[off:off + 64, c, 0:128],
                          kfm[off:off + 64, c, 0:128], start=True, stop=True,
                          tile_position=(off, 0)),
                eo.matmul(psc[par][:, 128:384], qfm[off:off + 64, c, 128:256],
                          kfm[off:off + 64, c, 0:256], start=True, stop=True,
                          tile_position=(off, 0)))[-1],
                waits=[ev_qkfm, pscf[par]], inc="e_t")

        def stage_b(h):
            par = h & 1
            p4 = h & 3
            ev_s = score_evs[par]
            ev_msk = P.emit("vector", lambda eo, par=par: eo.tensor_tensor(
                sc_sb[par][:, :], psc[par][:, 0:384], mask_sb[:, :], AL.add),
                waits=[ev_s, ev_mask, sc_free[par]], inc="e_v")
            pscf[par] = ev_msk
            ev_m0 = P.emit("vector", lambda eo, par=par, p4=p4: eo.reduce_max(
                ast[:, 8 * p4:8 * p4 + 1], sc_sb[par][:, 0:128], AX.X,
                negate=True), inc="e_v")
            ev_m1 = P.emit("vector", lambda eo, par=par, p4=p4: eo.reduce_max(
                ast[:, 8 * p4 + 1:8 * p4 + 2], sc_sb[par][:, 128:384], AX.X,
                negate=True), inc="e_v")
            p3 = h % 3
            P.emit("scalar", lambda eo, par=par, p4=p4, p3=p3: eo.activation(
                e_sb[p3][:, 0:128], sc_sb[par][:, 0:128], AF.Exp,
                bias=ast[:, 8 * p4:8 * p4 + 1],
                accum_out=ast[:, 8 * p4 + 2:8 * p4 + 3]),
                waits=[ev_m0, e_free[p3]], inc="e_s")
            e1 = P.emit("scalar", lambda eo, par=par, p4=p4, p3=p3: eo.activation(
                e_sb[p3][:, 128:384], sc_sb[par][:, 128:384], AF.Exp,
                bias=ast[:, 8 * p4 + 1:8 * p4 + 2],
                accum_out=ast[:, 8 * p4 + 3:8 * p4 + 4]),
                waits=[ev_m1], inc="e_s")
            sc_free[par] = e1
            exp_evs[p4] = e1

        def stage_c(h):
            nonlocal ev_v_used, ev_yfm_used
            par = h & 1
            p4 = h & 3
            p3 = h % 3
            c, off = h >> 1, 64 * (h & 1)
            P.emit("vector", lambda eo, p4=p4: eo.reciprocal(
                ast[:, 8 * p4 + 4:8 * p4 + 6],
                ast[:, 8 * p4 + 2:8 * p4 + 4]),
                waits=[exp_evs[p4]], inc="e_v")
            P.emit("vector", lambda eo, p4=p4, p3=p3, par=par: eo.tensor_scalar(
                p_sb[par][:, 0:128], e_sb[p3][:, 0:128],
                ast[:, 8 * p4 + 4:8 * p4 + 5], None, AL.mult),
                waits=[p_free[par]], inc="e_v")
            ev_p1 = P.emit("vector", lambda eo, p4=p4, p3=p3, par=par:
                           eo.tensor_scalar(
                p_sb[par][:, 128:384], e_sb[p3][:, 128:384],
                ast[:, 8 * p4 + 5:8 * p4 + 6], None, AL.mult), inc="e_v")
            e_free[p3] = ev_p1
            evt = P.emit("tensor", lambda eo, par=par: (
                eo.transpose(pt[par][:, 0:128], p_sb[par][:, 0:128], id_sb[:, :]),
                eo.transpose(pt[par][:, 128:256], p_sb[par][:, 128:256],
                             id_sb[:, :]),
                eo.transpose(pt[par][:, 256:384], p_sb[par][:, 256:384],
                             id_sb[:, :]))[-1],
                waits=[ev_p1, ptf[par]], inc="e_t")
            p_free[par] = evt
            ev_pT = P.emit("scalar", lambda eo, par=par: eo.copy(
                pT_sb[par][:, 0:384], pt[par][:, 0:384]),
                waits=[evt, pT_free[par]], inc="e_s")
            ptf[par] = ev_pT

            def pv(eo, h=h, off=off, par=par):
                eo.matmul(pm[par][off:off + 64, 0:256],
                          v_sb[:, 0, 64 * h:64 * h + 64], pT_sb[par][:, 0:256],
                          start=True, stop=False, tile_position=(0, off))
                return eo.matmul(pm[par][off:off + 64, 128:256],
                                 v_sb[:, 1, 64 * h:64 * h + 64],
                                 pT_sb[par][:, 256:384],
                                 start=False, stop=True, tile_position=(0, off))
            ev_y = P.emit("tensor", pv, waits=[ev_pT, ev_v, pmf[par]],
                          inc="e_t")
            pT_free[par] = ev_y
            ev_v_used = ev_y
            y_evs[par] = ev_y

        def stage_d(h):
            nonlocal ev_yfm_used
            par = h & 1
            c, off = h >> 1, 64 * (h & 1)
            if par == 0:
                ev_ye = P.emit("vector", lambda eo, c=c, off=off, par=par:
                               eo.tensor_copy(
                    yfm[off:off + 64, c, :], pm[par][off:off + 64, 0:256]),
                    waits=[y_evs[par], ev_yfm_used], inc="e_v")
            else:
                ev_ye = P.emit("scalar", lambda eo, c=c, off=off, par=par:
                               eo.copy(
                    yfm[off:off + 64, c, :], pm[par][off:off + 64, 0:256]),
                    waits=[y_evs[par], ev_yfm_used], inc="e_s")
            pmf[par] = ev_ye
            yfm_evs[par] = ev_ye
            return ev_ye

        y_evs = [None, None]
        yfm_evs = [None, None]
        ev_yfm = None
        for s in range(H + 4):
            if s < H:
                stage_a(s)
            if 1 <= s <= H:
                stage_b(s - 1)
            if 3 <= s <= H + 2:
                stage_c(s - 3)
            if s >= 4:
                ev_yfm = stage_d(s - 4)

        def mk_wo(w=w, ev_yfm=list(yfm_evs)):
            def do_wo():
                nonlocal ev_yfm_used
                for ti in range(2):
                    for nn in range(2):
                        par = nn

                        def mms(eo, ti=ti, nn=nn, par=par):
                            ins = None
                            for c in range(8):
                                ins = eo.matmul(
                                    psc[par][:, :],
                                    yfm[:, c, 128 * ti:128 * (ti + 1)],
                                    wo_sb[:, c, 512 * nn:512 * (nn + 1)],
                                    start=(c == 0), stop=(c == 7))
                            return ins
                        evm = P.emit("tensor", mms,
                                     waits=[ev_yfm, pscf[par]], inc="e_t")
                        ev_yfm_used = evm
                        pscf[par] = P.emit("vector", lambda eo, ti=ti, nn=nn,
                                           par=par, w=w: eo.tensor_tensor(
                                               x_sb[:, 2 * w + ti,
                                                    512 * nn:512 * (nn + 1)],
                                               x_sb[:, 2 * w + ti,
                                                    512 * nn:512 * (nn + 1)],
                                               psc[par][:, :], AL.add),
                                           waits=[evm], inc="e_v")
            return do_wo
        wo_pend = mk_wo()

    wo_pend()
    P.barrier()
    P.run_block()
    stC.close()
    stA.close()

    stD = contextlib.ExitStack()
    qds4 = alloc_qds(stD, 1024, tag="4", npar=2)
    xnqfm = sb(stD, "xnqfm", [128, 8, T], BF16)
    wgq_it = [sb(stD, f"wgq{i}", [128, 8, 512], BF16) for i in range(2)]
    wuq_it = [sb(stD, f"wuq{i}", [128, 8, 512], BF16) for i in range(2)]
    wdq_sb = sb(stD, "wdq_sb", [128, 8, 512], BF16)
    hqfm = sb(stD, "hqfm", [128, 8, T], BF16)
    h_f = [sb(stD, f"h_f{i}", [128, 1024], F32) for i in range(2)]
    hq_b = [sb(stD, f"hq_b{i}", [128, 1024], BF16) for i in range(2)]

    P.phase()
    ev_xnq = P.emit_dma("sync", lambda eo: eo.dma_start(
        out=xnqfm[:, :, :].rearrange("p c s -> p (c s)"), in_=xnqfm_d))

    pmf = [None, None, None, None]
    ptf4 = [None, None]
    h_free = [None, None]
    hq_free = [None, None]
    thirds = [(0, 8, [512, 512]), (8, 8, [512, 512]), (16, 6, [512, 256])]
    buf = 0
    ev_w_used = [None, None]
    pend = [None, None, None]

    dn_jobs = []
    dn_state = {"loaded": -1, "wd_free": None, "pmf": [None, None],
                "ld_evs": {}, "lastmm": {}, "par": 0}

    def emit_dn_load(k, gbase, ncp):
        evs = []
        for cj in range(ncp):
            j = gbase + cj
            off = (j // 3) * SH + 2 * G_SZ + (j % 3) * DN_SZ
            for nn in range(2):
                evs.append(P.emit_dma("sync", lambda eo, cj=cj, nn=nn, off=off:
                                      eo.dma_start(
                    out=wdq_sb[:, 2 * cj + nn, :],
                    in_=wffnq_d[off:off + DN_SZ]
                    .rearrange("(p i) -> p i", p=128)
                    [:, 512 * nn:512 * (nn + 1)]),
                    waits=[ev_wst, dn_state["wd_free"]]))
        dn_state["ld_evs"][k] = evs
        dn_state["loaded"] = k

    def push_dn_jobs(k, cbase, ncp, tte, ev_hq):
        for ttj in (tte, tte + 1):
            for nn in range(2):
                def dj(k=k, cbase=cbase, ncp=ncp, ttj=ttj, nn=nn, ev_hq=ev_hq):
                    par = dn_state["par"]
                    dn_state["par"] ^= 1

                    def dmm(eo, par=par):
                        ins = None
                        for cj in range(ncp):
                            ins = eo.matmul(psc[par][:, :],
                                            hqfm[:, cbase + cj,
                                                 128 * ttj:128 * (ttj + 1)],
                                            wdq_sb[:, 2 * cj + nn, :],
                                            start=(cj == 0),
                                            stop=(cj == ncp - 1))
                        return ins
                    evm = P.emit("tensor", dmm,
                                 waits=[ev_hq, dn_state["ld_evs"][k],
                                        dn_state["pmf"][par]], inc="e_t")
                    dn_state["lastmm"][k] = evm
                    dn_state["wd_free"] = evm
                    dn_state["pmf"][par] = P.emit(
                        "vector", lambda eo, ttj=ttj, nn=nn, par=par:
                        eo.tensor_tensor(
                            x_sb[:, ttj, 512 * nn:512 * (nn + 1)],
                            x_sb[:, ttj, 512 * nn:512 * (nn + 1)],
                            psc[par][:, :], AL.add),
                        waits=[evm], inc="e_v")
                dn_jobs.append((k, dj))

    def pop_dn_jobs(limit):
        n = 0
        while dn_jobs and dn_jobs[0][0] <= dn_state["loaded"] and n < limit:
            dn_jobs.pop(0)[1]()
            n += 1

    pend_evac = [None]

    def flush_pend():
        tc = pend[0]() if pend[0] is not None else None
        ec = pend[1]() if pend[1] is not None else None
        pend_evac[0] = pend[2]
        pend[0], pend[1], pend[2] = None, tc, ec

    def flush_evac():
        if pend_evac[0] is not None:
            pend_evac[0]()
            pend_evac[0] = None

    def drain_pend():
        for _ in range(3):
            flush_pend()
            flush_evac()

    kglob = -1
    part_info = []
    for (ch0, nch, itiles) in thirds:
        ioff = 0
        for it, iw in enumerate(itiles):
            kglob += 1
            part_info.append((ch0 + ioff // 128, (ioff // 128) % 8, iw // 128))
            if kglob > 0:
                emit_dn_load(kglob - 1, part_info[kglob - 1][0],
                             part_info[kglob - 1][2])
            gbuf, ubuf = wgq_it[buf], wuq_it[buf]
            ev_lg = []
            goff = ch0 * 128 + ioff
            for c in range(8):
                base = c * SH
                ev_lg.append(P.emit_dma("sync", lambda eo, base=base, goff=goff,
                                        iw=iw, c=c, gbuf=gbuf: eo.dma_start(
                    out=gbuf[:, c, :iw],
                    in_=wffnq_d[base:base + G_SZ]
                    .rearrange("(p i) -> p i", p=128)[:, goff:goff + iw]),
                    waits=[ev_wst, ev_w_used[buf]]))
                ev_lg.append(P.emit_dma("sync", lambda eo, base=base, goff=goff,
                                        iw=iw, c=c, ubuf=ubuf: eo.dma_start(
                    out=ubuf[:, c, :iw],
                    in_=wffnq_d[base + G_SZ:base + 2 * G_SZ]
                    .rearrange("(p i) -> p i", p=128)[:, goff:goff + iw]),
                    waits=[ev_wst, ev_w_used[buf]]))
            ev_ldw = ev_lg
            chunkbase = ioff // 128
            nblk = iw // 128
            ev_gu_last = None
            for tt in range(NT):
                par = tt & 1
                pp = (tt >> 1) & 1
                pg, pu = pm[par], pm[2 + par]
                if (tt & 1) == 0:
                    pop_dn_jobs(4)
                    flush_pend()

                def gumm(eo, tt=tt, iw=iw, gbuf=gbuf, ubuf=ubuf, pg=pg, pu=pu):
                    ins = None
                    for c in range(8):
                        eo.matmul(pg[:, :iw],
                                  xnqfm[:, c, 128 * tt:128 * (tt + 1)],
                                  gbuf[:, c, :iw],
                                  start=(c == 0), stop=(c == 7))
                        ins = eo.matmul(pu[:, :iw],
                                        xnqfm[:, c, 128 * tt:128 * (tt + 1)],
                                        ubuf[:, c, :iw],
                                        start=(c == 0), stop=(c == 7))
                    return ins
                evg = P.emit("tensor", gumm,
                             waits=[ev_ldw, ev_xnq, pmf[par], pmf[2 + par]],
                             inc="e_t")
                evu = evg
                ev_gu_last = evu
                ev_sl = P.emit("scalar", lambda eo, iw=iw, pg=pg,
                               pp=pp, tt=tt: eo.activation(
                    h_f[pp][:, iw * (tt & 1):iw * ((tt & 1) + 1)],
                    pg[:, :iw], AF.Silu),
                    waits=[evg, h_free[pp] if (tt & 1) == 0 else None],
                    inc="e_s")
                pmf[par] = ev_sl
                ev_h = P.emit("vector", lambda eo, iw=iw, pu=pu,
                              pp=pp, tt=tt: eo.tensor_tensor(
                    h_f[pp][:, iw * (tt & 1):iw * ((tt & 1) + 1)],
                    h_f[pp][:, iw * (tt & 1):iw * ((tt & 1) + 1)],
                    pu[:, :iw], AL.mult),
                    waits=[ev_sl, evu], inc="e_v")
                pmf[2 + par] = ev_h
                if (tt & 1) == 0:
                    continue
                FP = 2 * iw
                e_sq, e_sg = emit_qd_head(P, h_f[pp][:, :FP], qds4, FP,
                                          par=pp, in_ev=ev_h)
                h_free[pp] = e_sg
                flush_evac()

                def mk_pend(pp=pp, tte=tt - 1, nblk=nblk, chunkbase=chunkbase,
                            FP=FP, e_sq=e_sq, e_sg=e_sg, kg=kglob):
                    def do_tail():
                        evq = emit_qd_tail(P, hq_b[pp][:, :FP], qds4, FP,
                                           par=pp, out_free=hq_free[pp],
                                           sq_ev=e_sq, sign_ev=e_sg)

                        def do_transp(evq=evq):
                            evt = None
                            for jb in range(2 * nblk):
                                evt = P.emit("tensor", lambda eo, jb=jb, pp=pp:
                                             eo.transpose(
                                                 pt[pp][:, jb * 128:(jb + 1) * 128],
                                                 hq_b[pp][:, jb * 128:(jb + 1) * 128],
                                                 id_sb[:, :]),
                                             waits=[evq, ptf4[pp],
                                                    dn_state["lastmm"]
                                                    .get(kg - 2)],
                                             inc="e_t")
                            hq_free[pp] = evt

                            def do_evac(evt=evt):
                                ptf4[pp] = P.emit("scalar", lambda eo, tte=tte,
                                                  chunkbase=chunkbase,
                                                  nblk=nblk,
                                                  pp=pp: eo.copy(
                                    hqfm[:, chunkbase:chunkbase + nblk,
                                         128 * tte:128 * (tte + 2)]
                                    .rearrange("p c (o k) -> p o c k", o=2),
                                    pt[pp][:, :2 * nblk * 128]
                                    .rearrange("p (o c k) -> p o c k",
                                               o=2, k=128)),
                                    waits=[evt], inc="e_s")
                                push_dn_jobs(kg, chunkbase, nblk, tte,
                                             ptf4[pp])
                            return do_evac
                        return do_transp
                    return do_tail
                pend[0] = mk_pend()
            ev_w_used[buf] = ev_gu_last
            buf ^= 1
            ioff += iw
    drain_pend()
    pop_dn_jobs(10 ** 9)
    emit_dn_load(kglob, part_info[kglob][0], part_info[kglob][2])
    for p8 in range(8):
        pop_dn_jobs(4)
        evp = ("e_v", P.cnt["e_v"])
        P.emit("sync", lambda eo, p8=p8: eo.dma_start(
            out=out_d.rearrange("p (t d) -> p t d", d=D)[:, 2 * p8:2 * p8 + 2, :],
            in_=x_sb[:, 2 * p8:2 * p8 + 2, :]),
            waits=[evp], inc="d_out", amt=16)
    pop_dn_jobs(10 ** 9)
    P.emit("sync", None, waits=[("d_out", 128)])
    P.barrier()
    P.run_block()
    stD.close()
    root.close()
    return nc


_NC_CACHE = {}


def kernel(L0_attn_x, L0_attn_wq, L0_attn_wk, L0_attn_wv, L0_attn_wo,
           L0_attn_q_gain, L0_ffn_w_gate, L0_ffn_w_up, L0_ffn_w_down):
    f32 = lambda a: np.ascontiguousarray(np.asarray(a, dtype=np.float32))
    x = f32(L0_attn_x).reshape(S, D)
    wqT = np.ascontiguousarray(f32(L0_attn_wq).T)
    wkT = np.ascontiguousarray(f32(L0_attn_wk).T)
    wvT = np.ascontiguousarray(f32(L0_attn_wv).T)
    woT = np.ascontiguousarray(f32(L0_attn_wo).T)
    wg = f32(L0_ffn_w_gate)
    wu = f32(L0_ffn_w_up)
    wd = f32(L0_ffn_w_down)
    wd_pad = np.concatenate(
        [wd, np.zeros((D, 3 * 128 * NCORES - I_FFN), dtype=np.float32)], axis=1)
    gain128 = np.ascontiguousarray(np.tile(f32(L0_attn_q_gain)[None, :],
                                           (128, 1)))

    inv_freq = (1.0 / 10000.0 ** (np.arange(0, HD, 2, dtype=np.float32)
                                  / np.float32(HD))).astype(np.float32)
    freqs = np.outer(np.arange(S, dtype=np.float32), inv_freq).astype(np.float32)
    cosb = np.cos(freqs.astype(np.float64)).astype(ml_dtypes.bfloat16)
    sinb = np.sin(freqs.astype(np.float64)).astype(ml_dtypes.bfloat16)
    tri = np.tril(np.ones((128, 128), dtype=bool))
    m0 = np.where(tri, np.float32(0), np.float32(FMIN)).astype(np.float32)
    mask384 = np.concatenate(
        [m0, np.zeros((128, 128), dtype=np.float32), m0], axis=1)
    ident = np.eye(128, dtype=ml_dtypes.bfloat16)

    if "nc" not in _NC_CACHE:
        _NC_CACHE["nc"] = build_program()
    nc = _NC_CACHE["nc"]

    def pmaj(a, tile):
        nt = a.shape[0] // 128
        return np.ascontiguousarray(
            a.reshape(nt, 128, tile).swapaxes(0, 1).reshape(128, nt * tile))

    wqTs = pmaj(wqT, D)
    wkTs = pmaj(wkT, D)
    wvTs = pmaj(wvT, D)
    woTs = pmaj(woT, D)
    in_maps = []
    for c in range(NCORES):
        sl = slice(c * T, (c + 1) * T)
        wdc = wd_pad[:, 384 * c:384 * (c + 1)]
        wds = np.ascontiguousarray(
            wdc.reshape(8, 128, 3, 128).transpose(1, 2, 0, 3)
            .reshape(128, 3 * 8 * 128))
        in_maps.append({
            "x": pmaj(x[sl], D),
            "wqT": wqTs, "wkT": wkTs, "wvT": wvTs, "woT": woTs,
            "wg": pmaj(np.ascontiguousarray(wg[:, 128 * c:128 * (c + 1)]), 128),
            "wu": pmaj(np.ascontiguousarray(wu[:, 128 * c:128 * (c + 1)]), 128),
            "wd": wds,
            "cosb": pmaj(cosb[sl], 32),
            "sinb": pmaj(sinb[sl], 32),
            "gain128": gain128, "mask0": mask384, "ident": ident,
        })
    res = run_bass_kernel_spmd(nc, in_maps, core_ids=list(range(NCORES)))
    out = np.concatenate(
        [r["out"].reshape(128, NT, D).swapaxes(0, 1).reshape(T, D)
         for r in res.results], axis=0)
    return out.reshape(1, S, D).astype(np.float32)


# version 41
# speedup vs baseline: 1.0561x; 1.0019x over previous
import contextlib
import os
import numpy as np
import ml_dtypes
import concourse.bass as bass
import concourse.mybir as mybir
from concourse.bass_utils import run_bass_kernel_spmd

CHAIN_WAITS = (os.environ.get("BASS_SIM_FP64") == "1"
               or os.environ.get("KERNEL_CHAIN_WAITS") == "1")

F32 = mybir.dt.float32
BF16 = mybir.dt.bfloat16
AL = mybir.AluOpType
AF = mybir.ActivationFunctionType
AX = mybir.AxisListType

NCORES = 8
S, D = 16384, 1024
T = S // NCORES
NT = T // 128
H, HD = 16, 64
NWIN = T // 256
I_FFN = 2816
FMIN = float(np.finfo(np.float32).min)
M2 = [0.0625, 0.5625, 1.5625, 3.0625, 6.25, 12.25, 25.0]
WTS = [0.5, 0.5, 0.5, 0.5, 1.0, 1.0, 2.0]
RMS_EPS = 1.1920928955078125e-07

G_SZ = 128 * 2816
DN_SZ = 128 * 1024
SH = 2 * G_SZ + 3 * DN_SZ

ENGS = ("sync", "gpsimd", "tensor", "vector", "scalar")


class Sched:

    CHAIN_SEM = {"vector": "e_v", "scalar": "e_s", "gpsimd": "e_g"}

    def __init__(self, nc, sems, dpool):
        self.nc = nc
        self.sem = sems
        self.cnt = {k: 0 for k in sems}
        self.waited = {e: {} for e in ENGS}
        self.dpool = dpool
        self.didx = {e: 0 for e in dpool}
        self.q = None

    def phase(self):
        self.q = {e: [] for e in ENGS}

    def emit(self, eng, fn, waits=(), inc=None, amt=1, chain=True):
        waits = list(waits)
        cs = self.CHAIN_SEM.get(eng)
        if fn is not None and cs is not None:
            if chain and self.cnt.get(cs, 0):
                waits.append((cs, self.cnt[cs]))
            if inc is None:
                inc = cs
        flat = []
        for ev in waits:
            if isinstance(ev, list):
                flat.extend(ev)
            else:
                flat.append(ev)
        wd = self.waited[eng]
        ws = []
        for ev in flat:
            if ev is None:
                continue
            s, v = ev
            if v is None or v <= 0 or wd.get(s, 0) >= v:
                continue
            wd[s] = v
            ws.append((s, v))
        ev = None
        if inc is not None:
            self.cnt[inc] += amt
            ev = (inc, self.cnt[inc])

        def thunk(eo, fn=fn, ws=ws, inc=inc, amt=amt):
            for s, v in ws:
                eo.wait_ge(self.sem[s], v)
            ins = fn(eo) if fn is not None else None
            if inc is not None:
                if ins is None:
                    ins = eo.nop()
                ins.then_inc(self.sem[inc], amt)

        self.q[eng].append(thunk)
        return ev

    def emit_dma(self, eng, fn, waits=()):
        pool = self.dpool[eng]
        i = self.didx[eng]
        self.didx[eng] += 1
        s = pool[i % len(pool)]
        rounds = i // len(pool)
        w = list(waits) + [(s, 16 * rounds)]
        return self.emit(eng, fn, waits=w, inc=s, amt=16)

    def barrier(self):
        for eng, pool in self.dpool.items():
            dws = [(s, self.cnt[s]) for s in pool if self.cnt[s]]
            if dws:
                self.emit(eng, None, waits=dws)
        evs = [self.emit(e, None, inc="b_" + e) for e in ENGS]
        for e in ENGS:
            self.emit(e, None, waits=evs)

    def run_block(self):
        with self.nc.Block() as blk:
            for e in ENGS:
                q = self.q[e]
                if not q:
                    continue

                def runner(eo, q=q):
                    for t in q:
                        t(eo)

                getattr(blk, e)(runner)
        self.q = None


def emit_qd_head_part(P, xin, scr, F, c0, c1, par=0, in_ev=None,
                      rstd=None, rstd6=None, last=True):
    nb = F // 16
    b0, b1 = c0 // 16, c1 // 16
    bm, rsc = scr["bm"], scr["rsc"]
    sc_, rsq = scr["sc"][par], scr["rsq"][par]
    a2, sgn = scr["a2"][par], scr["sgn"][par]
    P.emit("vector", lambda eo, xin=xin, b0=b0, b1=b1, c0=c0, c1=c1:
           eo.tensor_reduce(
        bm[:, b0:b1], xin[:, c0:c1].rearrange("p (b k) -> p b k", k=16),
        AX.X, AL.max, apply_absolute_value=True), inc="e_v")
    if last:
        s6 = rstd6 if rstd6 is not None else 1.0 / 6.0
        P.emit("vector", lambda eo, nb=nb, sc_=sc_, s6=s6: eo.tensor_scalar(
            sc_[:, :nb], bm[:, :nb], s6, 1e-12, AL.mult, AL.max),
            waits=[scr["ev_sc_free"][par]], inc="e_v")
        P.emit("vector", lambda eo, nb=nb, sc_=sc_: eo.reciprocal(
            rsc[:, :nb], sc_[:, :nb]), inc="e_v")
        P.emit("vector", lambda eo, nb=nb, rsq=rsq: eo.tensor_tensor(
            rsq[:, :nb], rsc[:, :nb], rsc[:, :nb], AL.mult), inc="e_v")
    if rstd is not None:
        e_sq = P.emit("scalar", lambda eo, xin=xin, c0=c0, c1=c1, a2=a2,
                      rstd=rstd: eo.activation(
            a2[:, c0:c1], xin[:, c0:c1], AF.Square, scale=rstd),
            waits=[in_ev, scr["ev_cmp_free"][par] if c0 == 0 else None],
            inc="e_s")
    else:
        e_sq = P.emit("scalar", lambda eo, xin=xin, c0=c0, c1=c1, a2=a2:
                      eo.activation(a2[:, c0:c1], xin[:, c0:c1], AF.Square),
            waits=[in_ev, scr["ev_cmp_free"][par] if c0 == 0 else None],
            inc="e_s")
    e_sg = P.emit("scalar", lambda eo, xin=xin, c0=c0, c1=c1, sgn=sgn: eo.sign(
        sgn[:, c0:c1], xin[:, c0:c1]),
        waits=[scr["ev_sgn_free"][par] if c0 == 0 else None], inc="e_s")
    return e_sq, e_sg


def emit_qd_head(P, xin, scr, F, par=0, in_ev=None, rstd=None, rstd6=None):
    return emit_qd_head_part(P, xin, scr, F, 0, F, par=par, in_ev=in_ev,
                             rstd=rstd, rstd6=rstd6, last=True)


def emit_qd_tail(P, qout, scr, F, par=0, out_free=None, sq_ev=None,
                 sign_ev=None):
    nb = F // 16
    cb, g0 = scr["cb"][par], scr["g0"][par]
    sc_, rsq = scr["sc"][par], scr["rsq"][par]
    a2, sgn = scr["a2"][par], scr["sgn"][par]
    sA, gA = scr["sA"], scr["gA"]
    ev_a2 = P.emit("vector", lambda eo, F=F, nb=nb, a2=a2, rsq=rsq:
                   eo.tensor_tensor(
        a2[:, :F].rearrange("p (b k) -> p b k", k=16),
        a2[:, :F].rearrange("p (b k) -> p b k", k=16),
        rsq[:, :nb].unsqueeze(2).broadcast_to([128, nb, 16]), AL.mult),
        waits=[sq_ev, scr["ev_sgn_a2"][par]], inc="e_v")
    for i in range(5):
        dst = g0 if i == 0 else cb
        ev_ts = P.emit("vector", lambda eo, dst=dst, F=F, i=i: eo.tensor_scalar(
            dst[:, :F], a2[:, :F], M2[i], WTS[i], AL.is_gt, AL.mult),
            waits=[scr["ev_g0cb_free"][par]] if i <= 1 else (), inc="e_v")
        if i > 0:
            P.emit("vector", lambda eo, F=F: eo.tensor_tensor(
                g0[:, :F], g0[:, :F], cb[:, :F], AL.add), inc="e_v")
    scr["ev_cmp_free"][par] = ev_ts
    P.emit("scalar", lambda eo, F=F: eo.activation(
        sA[:, :F], a2[:, :F], AF.Sign, bias=scr["m2n"][:, 1:2]),
        waits=[ev_a2], inc="e_s")
    eR5 = P.emit("scalar", lambda eo, F=F: eo.activation(
        gA[:, :F], sA[:, :F], AF.Relu),
        waits=[scr["ev_gA_free"][0]], inc="e_s")
    evA = P.emit("vector", lambda eo, F=F: eo.tensor_tensor(
        g0[:, :F], g0[:, :F], gA[:, :F], AL.add), waits=[eR5], inc="e_v")
    gB = scr["gB"] if scr["gB"] is not None else gA
    gBw = scr["ev_gB_free"][0] if scr["gB"] is not None else evA
    P.emit("scalar", lambda eo, F=F: eo.activation(
        sA[:, :F], a2[:, :F], AF.Sign, bias=scr["m2n"][:, 2:3]), inc="e_s")
    eR6 = P.emit("scalar", lambda eo, F=F, gB=gB: eo.activation(
        gB[:, :F], sA[:, :F], AF.Relu, scale=2.0), waits=[gBw], inc="e_s")
    evC = P.emit("vector", lambda eo, F=F, gB=gB: eo.tensor_tensor(
        g0[:, :F], g0[:, :F], gB[:, :F], AL.add), waits=[eR6], inc="e_v")
    scr["ev_gA_free"][0] = evA if scr["gB"] is not None else evC
    scr["ev_gB_free"][0] = evC
    scr["ev_sgn_a2"][par] = eR6
    ev_cbm = P.emit("gpsimd", lambda eo, F=F, sgn=sgn: eo.tensor_tensor(
        cb[:, :F], g0[:, :F], sgn[:, :F], AL.mult),
        waits=[sign_ev, evC], inc="e_g")
    scr["ev_sgn_free"][par] = ev_cbm
    evq = P.emit("gpsimd", lambda eo, qout=qout, F=F, nb=nb, sc_=sc_:
                 eo.tensor_tensor(
        qout.rearrange("p (b k) -> p b k", k=16),
        cb[:, :F].rearrange("p (b k) -> p b k", k=16),
        sc_[:, :nb].unsqueeze(2).broadcast_to([128, nb, 16]), AL.mult),
        waits=[out_free], inc="e_g")
    scr["ev_g0cb_free"][par] = evq
    scr["ev_sc_free"][par] = evq
    return evq


def emit_qd(P, xin, qout, scr, F, out_free=None, in_ev=None):
    e_sq, e_sg = emit_qd_head(P, xin, scr, F, par=0, in_ev=in_ev)
    evq = emit_qd_tail(P, qout, scr, F, par=0, out_free=out_free,
                       sq_ev=e_sq, sign_ev=e_sg)
    return evq, e_sg


def build_program():
    nc = bass.Bass("TRN2", target_bir_lowering=False, debug=False)

    x_d = nc.dram_tensor("x", [128, NT * D], F32, kind="ExternalInput").ap()
    wqT_d = nc.dram_tensor("wqT", [128, 8 * D], F32, kind="ExternalInput").ap()
    wkT_d = nc.dram_tensor("wkT", [128, 8 * D], F32, kind="ExternalInput").ap()
    wvT_d = nc.dram_tensor("wvT", [128, 8 * D], F32, kind="ExternalInput").ap()
    woT_d = nc.dram_tensor("woT", [128, 8 * D], F32, kind="ExternalInput").ap()
    wg_d = nc.dram_tensor("wg", [128, I_FFN], F32, kind="ExternalInput").ap()
    wu_d = nc.dram_tensor("wu", [128, I_FFN], F32, kind="ExternalInput").ap()
    wd_d = nc.dram_tensor("wd", [128, 3 * 8 * 128], F32, kind="ExternalInput").ap()
    cos_d = nc.dram_tensor("cosb", [128, NT * 32], BF16, kind="ExternalInput").ap()
    sin_d = nc.dram_tensor("sinb", [128, NT * 32], BF16, kind="ExternalInput").ap()
    gain_d = nc.dram_tensor("gain128", [128, H], F32, kind="ExternalInput").ap()
    mask_d = nc.dram_tensor("mask0", [128, 384], F32, kind="ExternalInput").ap()
    id_d = nc.dram_tensor("ident", [128, 128], BF16, kind="ExternalInput").ap()
    out_d = nc.dram_tensor("out", [128, NT * D], F32, kind="ExternalOutput").ap()
    ag_in = nc.dram_tensor("ag_in", [SH], BF16).ap()
    wffnq_d = nc.dram_tensor("wffnq", [NCORES * SH], BF16,
                             addr_space="Shared").ap()
    xnqfm_d = nc.dram_tensor("xnqfm_dr", [128, 8 * T], BF16).ap()

    root = contextlib.ExitStack()
    sems = {}
    snames = (["d_out", "e_t", "e_v", "e_s", "e_g",
               "b_sync", "b_gpsimd", "b_tensor", "b_vector", "b_scalar"]
              + [f"ds{i}" for i in range(12)] + [f"dg{i}" for i in range(4)])
    for s in snames:
        sems[s] = root.enter_context(nc.semaphore(s))
    P = Sched(nc, sems, {"sync": [f"ds{i}" for i in range(12)],
                         "gpsimd": [f"dg{i}" for i in range(4)]})

    def sb(st, name, shape, dt):
        return st.enter_context(nc.sbuf_tensor(name, shape, dt))

    x_sb = sb(root, "x_sb", [128, NT, D], F32)
    stat = sb(root, "stat", [128, 16], F32)
    m2n = sb(root, "m2n", [128, 5], F32)
    gain_sb = sb(root, "gain_sb", [128, H], F32)
    mask_sb = sb(root, "mask_sb", [128, 384], F32)
    id_sb = sb(root, "id_sb", [128, 128], BF16)
    pm = [root.enter_context(nc.psum_tensor(f"pm{i}", [128, 512], F32))
          for i in range(4)]
    pt = [root.enter_context(nc.psum_tensor(f"pt{i}", [128, 1024], BF16))
          for i in range(2)]
    psc = [root.enter_context(nc.psum_tensor(f"psc{i}", [128, 512], F32))
           for i in range(2)]

    stA = contextlib.ExitStack()
    xnfm = sb(stA, "xnfm", [128, 8, T], BF16)
    wq_sb = sb(stA, "wq_sb", [128, 8, D], BF16)
    wk_sb = sb(stA, "wk_sb", [128, 8, D], BF16)
    wv_sb = sb(stA, "wv_sb", [128, 8, D], BF16)
    wo_sb = sb(stA, "wo_sb", [128, 8, D], BF16)
    cos_sb = sb(stA, "cos_sb", [128, NT, 32], BF16)
    sin_sb = sb(stA, "sin_sb", [128, NT, 32], BF16)

    def alloc_qds(st, F, tag="", npar=1, gb=False):
        return {
            "m2n": m2n,
            "bm": sb(st, f"qd_bm{tag}", [128, F // 16], F32),
            "rsc": sb(st, f"qd_rsc{tag}", [128, F // 16], F32),
            "a2": [sb(st, f"qd_a2{tag}{i}", [128, F], F32)
                   for i in range(npar)],
            "cb": [sb(st, f"qd_cb{tag}{i}", [128, F], BF16)
                   for i in range(npar)],
            "g0": [sb(st, f"qd_g0{tag}{i}", [128, F], BF16)
                   for i in range(npar)],
            "sc": [sb(st, f"qd_sc{tag}{i}", [128, F // 16], F32)
                   for i in range(npar)],
            "rsq": [sb(st, f"qd_rsq{tag}{i}", [128, F // 16], F32)
                    for i in range(npar)],
            "sgn": [sb(st, f"qd_sgn{tag}{i}", [128, F], BF16)
                    for i in range(npar)],
            "sA": sb(st, f"qd_sA{tag}", [128, F], BF16),
            "gA": sb(st, f"qd_gA{tag}", [128, F], BF16),
            "gB": sb(st, f"qd_gB{tag}", [128, F], BF16) if gb else None,
            "ev_gA_free": [None],
            "ev_gB_free": [None],
            "ev_g0cb_free": [None] * npar,
            "ev_sgn_free": [None] * npar,
            "ev_sc_free": [None] * npar,
            "ev_cmp_free": [None] * npar,
            "ev_sgn_a2": [None] * npar,
        }

    stB = contextlib.ExitStack()
    qds = alloc_qds(stB, 1024, npar=2)
    wsh = sb(stB, "wsh", [128, 1024], F32)
    qsh = sb(stB, "qsh", [128, 1024], BF16)
    fmo = sb(stB, "fmo", [128, 1024], BF16)
    xc = sb(stB, "xc", [128, D], F32)
    xnb = sb(stB, "xnb", [128, D], BF16)
    xnq = [sb(stB, f"xnq{i}", [128, D], BF16) for i in range(2)]
    xnqf = xnb

    P.phase()
    for _i, _v in enumerate([-M2[4], -M2[5], -M2[6], 1e-5, RMS_EPS]):
        P.emit("vector", lambda eo, _i=_i, _v=_v: eo.memset(
            m2n[:, _i:_i + 1], _v), inc="e_v")
    ev_x = []
    for q in range(4):
        ev_x.append(P.emit_dma("sync", lambda eo, q=q: eo.dma_start(
            out=x_sb[:, 4 * q:4 * (q + 1), :],
            in_=x_d.rearrange("p (t d) -> p t d", d=D)[:, 4 * q:4 * (q + 1), :])))
    ev_cos = P.emit_dma("sync", lambda eo: eo.dma_start(
        out=cos_sb[:, :, :], in_=cos_d.rearrange("p (t j) -> p t j", j=32)))
    ev_sin = P.emit_dma("sync", lambda eo: eo.dma_start(
        out=sin_sb[:, :, :], in_=sin_d.rearrange("p (t j) -> p t j", j=32)))
    ev_gld = P.emit_dma("sync", lambda eo: eo.dma_start(out=gain_sb[:, :],
                                                        in_=gain_d))
    ev_mask = P.emit_dma("sync", lambda eo: eo.dma_start(out=mask_sb[:, :],
                                                         in_=mask_d))
    ev_id = P.emit_dma("sync", lambda eo: eo.dma_start(out=id_sb[:, :],
                                                       in_=id_d))
    ev_attnw = []
    attnw_pend = [(wqT_d, wq_sb), (wkT_d, wk_sb), (wvT_d, wv_sb),
                  (woT_d, wo_sb)]
    ev_gain = P.emit("vector", lambda eo: eo.tensor_scalar(
        gain_sb[:, :], gain_sb[:, :], 0.125, None, AL.mult),
        waits=[ev_gld], inc="e_v")

    st_wqd = {"evq": None, "esg": None, "pt": [None, None], "store": None,
              "ptsel": 0, "tr": None}

    def wqd_piece(src_ap, ntile, off, Wrow, col0):
        F = ntile * 128
        par = st_wqd["ptsel"] & 1
        st_wqd["ptsel"] += 1
        vsnap = ("e_v", P.cnt["e_v"])
        evd = P.emit_dma("sync", lambda eo, src_ap=src_ap, F=F: eo.dma_start(
            out=wsh[:, :F], in_=src_ap),
            waits=[st_wqd["evq"], st_wqd["esg"]])
        P.emit("vector", None, waits=[evd])
        e_sq, e_sg = emit_qd_head(P, wsh[:, :F], qds, F, par=par,
                                  in_ev=[evd, vsnap])
        st_wqd["evq"] = ("e_v", P.cnt["e_v"])
        st_wqd["esg"] = e_sg
        evq = emit_qd_tail(P, qsh[:, :F], qds, F, par=par, sq_ev=e_sq,
                           sign_ev=e_sg, out_free=st_wqd["tr"])
        bank = pt[0]
        evt = None
        for j in range(ntile):
            evt = P.emit("tensor", lambda eo, j=j, bank=bank: eo.transpose(
                bank[:, j * 128:(j + 1) * 128], qsh[:, j * 128:(j + 1) * 128],
                id_sb[:, :]), waits=[evq, ev_id, st_wqd["pt"][0]], inc="e_t")
        st_wqd["tr"] = evt
        ev_ev = P.emit("vector", lambda eo, F=F, bank=bank: eo.tensor_copy(
            fmo[:, :F], bank[:, :F]), waits=[evt, st_wqd["store"]], inc="e_v")
        st_wqd["pt"][0] = ev_ev
        st_wqd["store"] = P.emit_dma("sync", lambda eo, off=off, Wrow=Wrow, F=F,
                                     col0=col0: eo.dma_start(
            out=ag_in[off:off + 128 * Wrow]
                .rearrange("(p i) -> p i", p=128)[:, col0:col0 + F],
            in_=fmo[:, :F]), waits=[ev_ev])
        return st_wqd["store"]

    ag_waits = []
    piece_args = []
    for (w_ap, roff) in ((wg_d, 0), (wu_d, G_SZ)):
        for (t0, ntile) in ((0, 8), (8, 8), (16, 6)):
            piece_args.append((
                w_ap[:, t0 * 128:(t0 + ntile) * 128],
                ntile, roff, 2816, t0 * 128))
    for pos in range(3):
        piece_args.append((
            wd_d[:, 1024 * pos:1024 * (pos + 1)],
            8, 2 * G_SZ + pos * DN_SZ, 1024, 0))

    ev_xnb_free = None
    ev_sg_prev = None
    ev_xnq_free = [None, None]
    ev_qf_free = None
    ln_pend = [None, None]

    def ln_flush():
        if ln_pend[1] is not None:
            ln_pend[1]()
            ln_pend[1] = None
        if ln_pend[0] is not None:
            ln_pend[1] = ln_pend[0]()
            ln_pend[0] = None

    for t in range(NT):
        par = t & 1
        if t >= 2 and t % 2 == 0 and attnw_pend:
            w_d, w_s = attnw_pend.pop(0)
            ev_attnw.append(P.emit_dma("gpsimd", lambda eo, w_d=w_d, w_s=w_s:
                                       eo.dma_start(
                out=w_s[:, :, :].rearrange("p c n -> p (c n)"), in_=w_d)))
        if piece_args:
            ln_flush()
            ag_waits.append(wqd_piece(*piece_args.pop(0)))
        ev_vsnap = ("e_v", P.cnt["e_v"])
        ev_cp = P.emit("scalar", lambda eo, t=t: eo.activation(
            xc[:, :], x_sb[:, t, :], AF.Copy, accum_out=stat[:, 0:1]),
            waits=[ev_x[t // 4], ev_vsnap], inc="e_s")
        ev_nm = P.emit("vector", lambda eo: eo.tensor_scalar(
            stat[:, 1:2], stat[:, 0:1], -1.0 / D, None, AL.mult),
            waits=[ev_cp], inc="e_v")
        P.emit("scalar", lambda eo: eo.activation(
            xc[:, :], xc[:, :], AF.Identity, bias=stat[:, 1:2]),
            waits=[ev_nm], inc="e_s")
        ev_acc = P.emit("scalar", lambda eo: eo.activation(
            qsh[:, :], xc[:, :], AF.Square, accum_out=stat[:, 2:3]),
            inc="e_s")
        ln_flush()
        ev = P.emit("scalar", lambda eo: eo.activation(
            stat[:, 4:5], stat[:, 2:3], AF.Ln, scale=1.0 / D,
            bias=m2n[:, 3:4]), inc="e_s")
        ev = P.emit("scalar", lambda eo: eo.activation(
            stat[:, 5:6], stat[:, 4:5], AF.Exp, scale=-0.5), inc="e_s")
        P.emit("vector", lambda eo: eo.tensor_scalar(
            stat[:, 6:7], stat[:, 5:6], 1.0 / 6.0, None, AL.mult),
            waits=[ev], inc="e_v")
        ev_xnb = P.emit("scalar", lambda eo: eo.activation(
            xnb[:, :], xc[:, :], AF.Copy, scale=stat[:, 5:6]),
            waits=[ev_xnb_free, ev_qf_free], inc="e_s")
        e_sq, e_sg = emit_qd_head(P, xc[:, :], qds, D, par=par,
                                  in_ev=ev_sg_prev, rstd=stat[:, 5:6],
                                  rstd6=stat[:, 6:7])
        ev_sg_prev = e_sg
        evt0 = None
        for c in range(8):
            evt0 = P.emit("tensor", lambda eo, c=c: eo.transpose(
                pt[0][:, c * 128:(c + 1) * 128], xnb[:, c * 128:(c + 1) * 128],
                id_sb[:, :]),
                waits=[ev_xnb, ev_id, st_wqd["pt"][0]], inc="e_t")
        ev_xnb_free = evt0

        def mk_ln_pend(t=t, par=par, evt0=evt0, e_sq=e_sq, e_sg=e_sg):
            def do():
                ev_fm = P.emit("vector", lambda eo, t=t: eo.tensor_copy(
                    xnfm[:, :, 128 * t:128 * (t + 1)],
                    pt[0][:, :].rearrange("p (c k) -> p c k", k=128)),
                    waits=[evt0], inc="e_v")
                st_wqd["pt"][0] = ev_fm
                evq = emit_qd_tail(P, xnq[par][:, :], qds, D, par=par,
                                   out_free=ev_xnq_free[par], sq_ev=e_sq,
                                   sign_ev=e_sg)
                evt1 = None
                for c in range(8):
                    evt1 = P.emit("tensor", lambda eo, c=c, par=par:
                                  eo.transpose(
                        pt[1][:, c * 128:(c + 1) * 128],
                        xnq[par][:, c * 128:(c + 1) * 128], id_sb[:, :]),
                        waits=[evq, st_wqd["pt"][1]], inc="e_t")
                ev_xnq_free[par] = evt1

                def do_evac(evt1=evt1, t=t):
                    nonlocal ev_qf_free
                    ev_qf = P.emit("vector", lambda eo: eo.tensor_copy(
                        xnqf[:, :].rearrange("p (c k) -> p c k", k=128),
                        pt[1][:, :].rearrange("p (c k) -> p c k", k=128)),
                        waits=[evt1, ev_qf_free, ev_xnb_free], inc="e_v")
                    st_wqd["pt"][1] = ev_qf
                    ev_qf_free = P.emit_dma("sync", lambda eo, t=t: eo.dma_start(
                        out=xnqfm_d.rearrange("p (c s) -> p c s", s=T)
                            [:, :, 128 * t:128 * (t + 1)],
                        in_=xnqf[:, :].rearrange("p (c k) -> p c k", k=128)),
                        waits=[ev_qf])
                return do_evac
            return do
        ln_pend[0] = mk_ln_pend()

    for _ in range(2):
        ln_flush()
    ev_wst = P.emit("gpsimd", lambda eo: eo.collective_compute(
        "AllGather", AL.bypass, replica_groups=[list(range(NCORES))],
        ins=[ag_in.opt()], outs=[wffnq_d.opt()]),
        waits=ag_waits, inc="e_g")

    P.barrier()
    P.run_block()
    stB.close()

    stC = contextlib.ExitStack()
    qk_sb = [sb(stC, f"qk{i}", [128, D], BF16) for i in range(2)]
    qsw = sb(stC, "qsw", [128, D], BF16)
    qm1 = sb(stC, "qm1", [128, D], BF16)
    sq3 = sb(stC, "sq3", [128, D], BF16)
    rstd_b = sb(stC, "rstd_b", [128, H], BF16)
    qn_b = sb(stC, "qn_b", [128, 2, D], BF16)
    kn_b = sb(stC, "kn_b", [128, 2, D], BF16)
    v_sb = sb(stC, "v_sb", [128, 2, D], BF16)
    qfm = sb(stC, "qfm", [128, 8, 256], BF16)
    kfm = sb(stC, "kfm", [128, 8, 256], BF16)
    yfm = sb(stC, "yfm", [128, 8, 256], BF16)
    sc_sb = [sb(stC, f"sc{i}", [128, 384], F32) for i in range(2)]
    e_sb = [sb(stC, f"e{i}", [128, 384], BF16) for i in range(3)]
    p_sb = [sb(stC, f"p{i}", [128, 384], BF16) for i in range(2)]
    pT_sb = [sb(stC, f"pT{i}", [128, 384], BF16) for i in range(2)]
    ast = sb(stC, "ast", [128, 32], F32)

    P.phase()
    pmf = [None, None, None, None]
    ptf = [None, None]
    pscf = [None, None]
    sc_free = [None, None]
    e_free = [None] * 3
    p_free = [None] * 2
    pT_free = [None] * 2
    ev_qk_free = [None, None]
    ev_dst_free = None
    ev_v_used = None
    ev_yfm_used = None
    ev_sq3_free = None

    v4 = lambda ap: ap.rearrange("p (h o j) -> p h o j", o=2, j=32)
    wo_pend = None
    for w in range(NWIN):
        if wo_pend is not None:
            wo_pend()
            wo_pend = None
        rounds = [("q", 0), ("q", 1), ("k", 0), ("k", 1), ("v", 0), ("v", 1)]
        rp = [None, None, None]

        def rflush():
            a = rp[0]() if rp[0] is not None else None
            b = rp[1]() if rp[1] is not None else None
            if rp[2] is not None:
                rp[2]()
            rp[0], rp[1], rp[2] = None, a, b

        def mk_back(kind, ti, par, outs, w=w):
            isq = kind == "q"
            dstb = qn_b if isq else kn_b
            fmdst = qfm if isq else kfm
            t = 2 * w + ti

            def back():
                nonlocal ev_sq3_free
                ev1 = P.emit("scalar", lambda eo, par=par: eo.activation(
                    sq3[:, :], qk_sb[par][:, :], AF.Square),
                    waits=outs + [ev_sq3_free], inc="e_s")
                ev2 = P.emit("vector", lambda eo: eo.tensor_reduce(
                    stat[:, 0:16], sq3[:, :].rearrange("p (h d) -> p h d", d=HD),
                    AX.X, AL.add), waits=[ev1], inc="e_v")
                ev_sq3_free = ev2
                P.emit("scalar", lambda eo: eo.activation(
                    stat[:, 0:16], stat[:, 0:16], AF.Ln, scale=1.0 / HD,
                    bias=m2n[:, 4:5]), waits=[ev2], inc="e_s")
                ev5 = P.emit("scalar", lambda eo: eo.activation(
                    rstd_b[:, :], stat[:, 0:16], AF.Exp, scale=-0.5), inc="e_s")
                if isq:
                    ev5 = P.emit("vector", lambda eo: eo.tensor_tensor(
                        rstd_b[:, :], rstd_b[:, :], gain_sb[:, :], AL.mult),
                        waits=[ev5, ev_gain], inc="e_v")
                P.emit("vector", lambda eo, t=t, par=par: eo.tensor_tensor(
                    v4(qsw[:, :])[:, :, 0, :], v4(qk_sb[par][:, :])[:, :, 1, :],
                    sin_sb[:, t, :].unsqueeze(1).broadcast_to([128, H, 32]),
                    AL.mult), waits=[ev_sin], inc="e_v")
                P.emit("vector", lambda eo, t=t, par=par:
                       eo.scalar_tensor_tensor(
                    v4(qsw[:, :])[:, :, 1, :], v4(qk_sb[par][:, :])[:, :, 0, :],
                    -1.0,
                    sin_sb[:, t, :].unsqueeze(1).broadcast_to([128, H, 32]),
                    AL.mult, AL.mult), inc="e_v")
                ev_qk_free[par] = P.emit("vector", lambda eo, t=t, par=par:
                                         eo.tensor_tensor(
                    v4(qm1[:, :]), v4(qk_sb[par][:, :]),
                    cos_sb[:, t, :].unsqueeze(1).unsqueeze(2)
                    .broadcast_to([128, H, 2, 32]), AL.mult),
                    waits=[ev_cos], inc="e_v")
                P.emit("vector", lambda eo: eo.tensor_tensor(
                    qm1[:, :], qm1[:, :], qsw[:, :], AL.add), inc="e_v")
                ev8 = P.emit("vector", lambda eo, dstb=dstb, ti=ti:
                             eo.tensor_tensor(
                    dstb[:, ti, :].rearrange("p (h d) -> p h d", d=HD),
                    qm1[:, :].rearrange("p (h d) -> p h d", d=HD),
                    rstd_b[:, :].unsqueeze(2).broadcast_to([128, H, HD]),
                    AL.mult), waits=[ev5, ev_dst_free], inc="e_v")

                def transp():
                    nonlocal ev_dst_free
                    evt = None
                    for c in range(8):
                        evt = P.emit("tensor", lambda eo, c=c, ti=ti, dstb=dstb,
                                     par=par: eo.transpose(
                                         pt[par][:, c * 128:(c + 1) * 128],
                                         dstb[:, ti, c * 128:(c + 1) * 128],
                                         id_sb[:, :]),
                                     waits=[ev8, ptf[par]], inc="e_t")
                    ev_dst_free = evt

                    def fmevac():
                        ptf[par] = P.emit("vector", lambda eo, ti=ti,
                                          fmdst=fmdst, par=par: eo.tensor_copy(
                            fmdst[:, :, 128 * ti:128 * (ti + 1)],
                            pt[par][:, :].rearrange("p (c k) -> p c k", k=128)),
                            waits=[evt], inc="e_v")
                    return fmevac
                return transp
            return back

        for s in range(len(rounds) + 3):
            if s < len(rounds):
                kind, ti = rounds[s]
                par = s & 1
                banks = (2, 3) if par == 0 else (0, 1)
                wsb = wq_sb if kind == "q" else (wk_sb if kind == "k" else wv_sb)
                evms = []
                for nn in range(2):
                    bank = banks[nn]

                    def mms(eo, ti=ti, nn=nn, w=w, wsb=wsb, bank=bank):
                        ins = None
                        for c in range(8):
                            ins = eo.matmul(
                                pm[bank][:, :],
                                xnfm[:, c, (2 * w + ti) * 128:(2 * w + ti + 1) * 128],
                                wsb[:, c, 512 * nn:512 * (nn + 1)],
                                start=(c == 0), stop=(c == 7))
                        return ins
                    evms.append(P.emit("tensor", mms,
                                       waits=[ev_attnw, pmf[bank]], inc="e_t"))
            rflush()
            if s < len(rounds):
                outs = []
                for nn in range(2):
                    bank = banks[nn]
                    if kind == "v":
                        ev = P.emit("scalar", lambda eo, ti=ti, nn=nn, bank=bank:
                                    eo.copy(v_sb[:, ti, 512 * nn:512 * (nn + 1)],
                                            pm[bank][:, :]),
                                    waits=[evms[nn], ev_v_used], inc="e_s")
                    else:
                        ev = P.emit("scalar", lambda eo, nn=nn, bank=bank,
                                    par=par: eo.copy(
                                        qk_sb[par][:, 512 * nn:512 * (nn + 1)],
                                        pm[bank][:, :]),
                                    waits=[evms[nn], ev_qk_free[par]], inc="e_s")
                    pmf[bank] = ev
                    outs.append(ev)
                if kind != "v":
                    rp[0] = mk_back(kind, ti, par, outs)
        ev_qkfm = ("e_v", P.cnt["e_v"])
        ev_v = ("e_s", P.cnt["e_s"])

        exp_evs = [None] * 4
        score_evs = [None, None]

        def stage_a(h):
            par = h & 1
            c, off = h >> 1, 64 * (h & 1)
            score_evs[par] = P.emit("tensor", lambda eo, c=c, off=off, par=par: (
                eo.matmul(psc[par][:, 0:128], qfm[off:off + 64, c, 0:128],
                          kfm[off:off + 64, c, 0:128], start=True, stop=True,
                          tile_position=(off, 0)),
                eo.matmul(psc[par][:, 128:384], qfm[off:off + 64, c, 128:256],
                          kfm[off:off + 64, c, 0:256], start=True, stop=True,
                          tile_position=(off, 0)))[-1],
                waits=[ev_qkfm, pscf[par]], inc="e_t")

        def stage_b(h):
            par = h & 1
            p4 = h & 3
            ev_s = score_evs[par]
            ev_msk = P.emit("vector", lambda eo, par=par: eo.tensor_tensor(
                sc_sb[par][:, :], psc[par][:, 0:384], mask_sb[:, :], AL.add),
                waits=[ev_s, ev_mask, sc_free[par]], inc="e_v")
            pscf[par] = ev_msk
            ev_m0 = P.emit("vector", lambda eo, par=par, p4=p4: eo.reduce_max(
                ast[:, 8 * p4:8 * p4 + 1], sc_sb[par][:, 0:128], AX.X,
                negate=True), inc="e_v")
            ev_m1 = P.emit("vector", lambda eo, par=par, p4=p4: eo.reduce_max(
                ast[:, 8 * p4 + 1:8 * p4 + 2], sc_sb[par][:, 128:384], AX.X,
                negate=True), inc="e_v")
            p3 = h % 3
            P.emit("scalar", lambda eo, par=par, p4=p4, p3=p3: eo.activation(
                e_sb[p3][:, 0:128], sc_sb[par][:, 0:128], AF.Exp,
                bias=ast[:, 8 * p4:8 * p4 + 1],
                accum_out=ast[:, 8 * p4 + 2:8 * p4 + 3]),
                waits=[ev_m0, e_free[p3]], inc="e_s")
            e1 = P.emit("scalar", lambda eo, par=par, p4=p4, p3=p3: eo.activation(
                e_sb[p3][:, 128:384], sc_sb[par][:, 128:384], AF.Exp,
                bias=ast[:, 8 * p4 + 1:8 * p4 + 2],
                accum_out=ast[:, 8 * p4 + 3:8 * p4 + 4]),
                waits=[ev_m1], inc="e_s")
            sc_free[par] = e1
            exp_evs[p4] = e1

        def stage_c(h):
            nonlocal ev_v_used, ev_yfm_used
            par = h & 1
            p4 = h & 3
            p3 = h % 3
            c, off = h >> 1, 64 * (h & 1)
            P.emit("vector", lambda eo, p4=p4: eo.reciprocal(
                ast[:, 8 * p4 + 4:8 * p4 + 6],
                ast[:, 8 * p4 + 2:8 * p4 + 4]),
                waits=[exp_evs[p4]], inc="e_v")
            P.emit("vector", lambda eo, p4=p4, p3=p3, par=par: eo.tensor_scalar(
                p_sb[par][:, 0:128], e_sb[p3][:, 0:128],
                ast[:, 8 * p4 + 4:8 * p4 + 5], None, AL.mult),
                waits=[p_free[par]], inc="e_v")
            ev_p1 = P.emit("vector", lambda eo, p4=p4, p3=p3, par=par:
                           eo.tensor_scalar(
                p_sb[par][:, 128:384], e_sb[p3][:, 128:384],
                ast[:, 8 * p4 + 5:8 * p4 + 6], None, AL.mult), inc="e_v")
            e_free[p3] = ev_p1
            evt = P.emit("tensor", lambda eo, par=par: (
                eo.transpose(pt[par][:, 0:128], p_sb[par][:, 0:128], id_sb[:, :]),
                eo.transpose(pt[par][:, 128:256], p_sb[par][:, 128:256],
                             id_sb[:, :]),
                eo.transpose(pt[par][:, 256:384], p_sb[par][:, 256:384],
                             id_sb[:, :]))[-1],
                waits=[ev_p1, ptf[par]], inc="e_t")
            p_free[par] = evt
            ev_pT = P.emit("scalar", lambda eo, par=par: eo.copy(
                pT_sb[par][:, 0:384], pt[par][:, 0:384]),
                waits=[evt, pT_free[par]], inc="e_s")
            ptf[par] = ev_pT

            def pv(eo, h=h, off=off, par=par):
                eo.matmul(pm[par][off:off + 64, 0:256],
                          v_sb[:, 0, 64 * h:64 * h + 64], pT_sb[par][:, 0:256],
                          start=True, stop=False, tile_position=(0, off))
                return eo.matmul(pm[par][off:off + 64, 128:256],
                                 v_sb[:, 1, 64 * h:64 * h + 64],
                                 pT_sb[par][:, 256:384],
                                 start=False, stop=True, tile_position=(0, off))
            ev_y = P.emit("tensor", pv, waits=[ev_pT, ev_v, pmf[par]],
                          inc="e_t")
            pT_free[par] = ev_y
            ev_v_used = ev_y
            y_evs[par] = ev_y

        def stage_d(h):
            nonlocal ev_yfm_used
            par = h & 1
            c, off = h >> 1, 64 * (h & 1)
            if par == 0:
                ev_ye = P.emit("vector", lambda eo, c=c, off=off, par=par:
                               eo.tensor_copy(
                    yfm[off:off + 64, c, :], pm[par][off:off + 64, 0:256]),
                    waits=[y_evs[par], ev_yfm_used], inc="e_v")
            else:
                ev_ye = P.emit("scalar", lambda eo, c=c, off=off, par=par:
                               eo.copy(
                    yfm[off:off + 64, c, :], pm[par][off:off + 64, 0:256]),
                    waits=[y_evs[par], ev_yfm_used], inc="e_s")
            pmf[par] = ev_ye
            yfm_evs[par] = ev_ye
            return ev_ye

        y_evs = [None, None]
        yfm_evs = [None, None]
        ev_yfm = None
        for s in range(H + 4):
            if s < H:
                stage_a(s)
            if 1 <= s <= H:
                stage_b(s - 1)
            if 3 <= s <= H + 2:
                stage_c(s - 3)
            if s >= 4:
                ev_yfm = stage_d(s - 4)

        def mk_wo(w=w, ev_yfm=list(yfm_evs)):
            def do_wo():
                nonlocal ev_yfm_used
                for ti in range(2):
                    for nn in range(2):
                        par = nn

                        def mms(eo, ti=ti, nn=nn, par=par):
                            ins = None
                            for c in range(8):
                                ins = eo.matmul(
                                    psc[par][:, :],
                                    yfm[:, c, 128 * ti:128 * (ti + 1)],
                                    wo_sb[:, c, 512 * nn:512 * (nn + 1)],
                                    start=(c == 0), stop=(c == 7))
                            return ins
                        evm = P.emit("tensor", mms,
                                     waits=[ev_yfm, pscf[par]], inc="e_t")
                        ev_yfm_used = evm
                        pscf[par] = P.emit("vector", lambda eo, ti=ti, nn=nn,
                                           par=par, w=w: eo.tensor_tensor(
                                               x_sb[:, 2 * w + ti,
                                                    512 * nn:512 * (nn + 1)],
                                               x_sb[:, 2 * w + ti,
                                                    512 * nn:512 * (nn + 1)],
                                               psc[par][:, :], AL.add),
                                           waits=[evm], inc="e_v")
            return do_wo
        wo_pend = mk_wo()

    wo_pend()
    P.barrier()
    P.run_block()
    stC.close()
    stA.close()

    stD = contextlib.ExitStack()
    qds4 = alloc_qds(stD, 1024, tag="4", npar=2)
    xnqfm = sb(stD, "xnqfm", [128, 8, T], BF16)
    wgq_it = [sb(stD, f"wgq{i}", [128, 8, 512], BF16) for i in range(2)]
    wuq_it = [sb(stD, f"wuq{i}", [128, 8, 512], BF16) for i in range(2)]
    wdq_sb = sb(stD, "wdq_sb", [128, 8, 512], BF16)
    hqfm = sb(stD, "hqfm", [128, 8, T], BF16)
    h_f = [sb(stD, f"h_f{i}", [128, 1024], F32) for i in range(2)]
    hq_b = [sb(stD, f"hq_b{i}", [128, 1024], BF16) for i in range(2)]

    P.phase()
    ev_xnq = P.emit_dma("sync", lambda eo: eo.dma_start(
        out=xnqfm[:, :, :].rearrange("p c s -> p (c s)"), in_=xnqfm_d))

    pmf = [None, None, None, None]
    ptf4 = [None, None]
    h_free = [None, None]
    hq_free = [None, None]
    thirds = [(0, 8, [512, 512]), (8, 8, [512, 512]), (16, 6, [512, 256])]
    buf = 0
    ev_w_used = [None, None]
    pend = [None, None, None]

    dn_jobs = []
    dn_state = {"loaded": -1, "wd_free": None, "pmf": [None, None],
                "ld_evs": {}, "lastmm": {}, "par": 0}

    def emit_dn_load(k, gbase, ncp):
        evs = []
        for cj in range(ncp):
            j = gbase + cj
            off = (j // 3) * SH + 2 * G_SZ + (j % 3) * DN_SZ
            for nn in range(2):
                evs.append(P.emit_dma("sync", lambda eo, cj=cj, nn=nn, off=off:
                                      eo.dma_start(
                    out=wdq_sb[:, 2 * cj + nn, :],
                    in_=wffnq_d[off:off + DN_SZ]
                    .rearrange("(p i) -> p i", p=128)
                    [:, 512 * nn:512 * (nn + 1)]),
                    waits=[ev_wst, dn_state["wd_free"]]))
        dn_state["ld_evs"][k] = evs
        dn_state["loaded"] = k

    def push_dn_jobs(k, cbase, ncp, tte, ev_hq):
        for ttj in (tte, tte + 1):
            for nn in range(2):
                def dj(k=k, cbase=cbase, ncp=ncp, ttj=ttj, nn=nn, ev_hq=ev_hq):
                    par = dn_state["par"]
                    dn_state["par"] ^= 1

                    def dmm(eo, par=par):
                        ins = None
                        for cj in range(ncp):
                            ins = eo.matmul(psc[par][:, :],
                                            hqfm[:, cbase + cj,
                                                 128 * ttj:128 * (ttj + 1)],
                                            wdq_sb[:, 2 * cj + nn, :],
                                            start=(cj == 0),
                                            stop=(cj == ncp - 1))
                        return ins
                    evm = P.emit("tensor", dmm,
                                 waits=[ev_hq, dn_state["ld_evs"][k],
                                        dn_state["pmf"][par]], inc="e_t")
                    dn_state["lastmm"][k] = evm
                    dn_state["wd_free"] = evm
                    dn_state["pmf"][par] = P.emit(
                        "vector", lambda eo, ttj=ttj, nn=nn, par=par:
                        eo.tensor_tensor(
                            x_sb[:, ttj, 512 * nn:512 * (nn + 1)],
                            x_sb[:, ttj, 512 * nn:512 * (nn + 1)],
                            psc[par][:, :], AL.add),
                        waits=[evm], inc="e_v")
                dn_jobs.append((k, dj))

    def pop_dn_jobs(limit):
        n = 0
        while dn_jobs and dn_jobs[0][0] <= dn_state["loaded"] and n < limit:
            dn_jobs.pop(0)[1]()
            n += 1

    pend_evac = [None]

    def flush_pend():
        tc = pend[0]() if pend[0] is not None else None
        ec = pend[1]() if pend[1] is not None else None
        pend_evac[0] = pend[2]
        pend[0], pend[1], pend[2] = None, tc, ec

    def flush_evac():
        if pend_evac[0] is not None:
            pend_evac[0]()
            pend_evac[0] = None

    def drain_pend():
        for _ in range(3):
            flush_pend()
            flush_evac()

    kglob = -1
    part_info = []
    for (ch0, nch, itiles) in thirds:
        ioff = 0
        for it, iw in enumerate(itiles):
            kglob += 1
            part_info.append((ch0 + ioff // 128, (ioff // 128) % 8, iw // 128))
            if kglob > 0:
                emit_dn_load(kglob - 1, part_info[kglob - 1][0],
                             part_info[kglob - 1][2])
            gbuf, ubuf = wgq_it[buf], wuq_it[buf]
            ev_lg = []
            goff = ch0 * 128 + ioff
            for c in range(8):
                base = c * SH
                ev_lg.append(P.emit_dma("sync", lambda eo, base=base, goff=goff,
                                        iw=iw, c=c, gbuf=gbuf: eo.dma_start(
                    out=gbuf[:, c, :iw],
                    in_=wffnq_d[base:base + G_SZ]
                    .rearrange("(p i) -> p i", p=128)[:, goff:goff + iw]),
                    waits=[ev_wst, ev_w_used[buf]]))
                ev_lg.append(P.emit_dma("sync", lambda eo, base=base, goff=goff,
                                        iw=iw, c=c, ubuf=ubuf: eo.dma_start(
                    out=ubuf[:, c, :iw],
                    in_=wffnq_d[base + G_SZ:base + 2 * G_SZ]
                    .rearrange("(p i) -> p i", p=128)[:, goff:goff + iw]),
                    waits=[ev_wst, ev_w_used[buf]]))
            ev_ldw = ev_lg
            chunkbase = ioff // 128
            nblk = iw // 128
            ev_gu_last = None
            for tt in range(NT):
                par = tt & 1
                pp = (tt >> 1) & 1
                pg, pu = pm[par], pm[2 + par]
                if (tt & 1) == 0:
                    pop_dn_jobs(4)
                    flush_pend()

                def gumm(eo, tt=tt, iw=iw, gbuf=gbuf, ubuf=ubuf, pg=pg, pu=pu):
                    ins = None
                    for c in range(8):
                        eo.matmul(pg[:, :iw],
                                  xnqfm[:, c, 128 * tt:128 * (tt + 1)],
                                  gbuf[:, c, :iw],
                                  start=(c == 0), stop=(c == 7))
                        ins = eo.matmul(pu[:, :iw],
                                        xnqfm[:, c, 128 * tt:128 * (tt + 1)],
                                        ubuf[:, c, :iw],
                                        start=(c == 0), stop=(c == 7))
                    return ins
                evg = P.emit("tensor", gumm,
                             waits=[ev_ldw, ev_xnq, pmf[par], pmf[2 + par]],
                             inc="e_t")
                evu = evg
                ev_gu_last = evu
                ev_sl = P.emit("scalar", lambda eo, iw=iw, pg=pg,
                               pp=pp, tt=tt: eo.activation(
                    h_f[pp][:, iw * (tt & 1):iw * ((tt & 1) + 1)],
                    pg[:, :iw], AF.Silu),
                    waits=[evg, h_free[pp] if (tt & 1) == 0 else None],
                    inc="e_s")
                pmf[par] = ev_sl
                ev_h = P.emit("vector", lambda eo, iw=iw, pu=pu,
                              pp=pp, tt=tt: eo.tensor_tensor(
                    h_f[pp][:, iw * (tt & 1):iw * ((tt & 1) + 1)],
                    h_f[pp][:, iw * (tt & 1):iw * ((tt & 1) + 1)],
                    pu[:, :iw], AL.mult),
                    waits=[ev_sl, evu], inc="e_v")
                pmf[2 + par] = ev_h
                if (tt & 1) == 0:
                    continue
                FP = 2 * iw
                e_sq, e_sg = emit_qd_head(P, h_f[pp][:, :FP], qds4, FP,
                                          par=pp, in_ev=ev_h)
                h_free[pp] = e_sg
                flush_evac()

                def mk_pend(pp=pp, tte=tt - 1, nblk=nblk, chunkbase=chunkbase,
                            FP=FP, e_sq=e_sq, e_sg=e_sg, kg=kglob):
                    def do_tail():
                        evq = emit_qd_tail(P, hq_b[pp][:, :FP], qds4, FP,
                                           par=pp, out_free=hq_free[pp],
                                           sq_ev=e_sq, sign_ev=e_sg)

                        def do_transp(evq=evq):
                            evt = None
                            for jb in range(2 * nblk):
                                evt = P.emit("tensor", lambda eo, jb=jb, pp=pp:
                                             eo.transpose(
                                                 pt[pp][:, jb * 128:(jb + 1) * 128],
                                                 hq_b[pp][:, jb * 128:(jb + 1) * 128],
                                                 id_sb[:, :]),
                                             waits=[evq, ptf4[pp],
                                                    dn_state["lastmm"]
                                                    .get(kg - 2)],
                                             inc="e_t")
                            hq_free[pp] = evt

                            def do_evac(evt=evt):
                                ptf4[pp] = P.emit("scalar", lambda eo, tte=tte,
                                                  chunkbase=chunkbase,
                                                  nblk=nblk,
                                                  pp=pp: eo.copy(
                                    hqfm[:, chunkbase:chunkbase + nblk,
                                         128 * tte:128 * (tte + 2)]
                                    .rearrange("p c (o k) -> p o c k", o=2),
                                    pt[pp][:, :2 * nblk * 128]
                                    .rearrange("p (o c k) -> p o c k",
                                               o=2, k=128)),
                                    waits=[evt], inc="e_s")
                                push_dn_jobs(kg, chunkbase, nblk, tte,
                                             ptf4[pp])
                            return do_evac
                        return do_transp
                    return do_tail
                pend[0] = mk_pend()
            ev_w_used[buf] = ev_gu_last
            buf ^= 1
            ioff += iw
    drain_pend()
    pop_dn_jobs(10 ** 9)
    emit_dn_load(kglob, part_info[kglob][0], part_info[kglob][2])
    for p8 in range(8):
        pop_dn_jobs(4)
        evp = ("e_v", P.cnt["e_v"])
        P.emit("sync", lambda eo, p8=p8: eo.dma_start(
            out=out_d.rearrange("p (t d) -> p t d", d=D)[:, 2 * p8:2 * p8 + 2, :],
            in_=x_sb[:, 2 * p8:2 * p8 + 2, :]),
            waits=[evp], inc="d_out", amt=16)
    pop_dn_jobs(10 ** 9)
    P.emit("sync", None, waits=[("d_out", 128)])
    P.barrier()
    P.run_block()
    stD.close()
    root.close()
    return nc


_NC_CACHE = {}


def kernel(L0_attn_x, L0_attn_wq, L0_attn_wk, L0_attn_wv, L0_attn_wo,
           L0_attn_q_gain, L0_ffn_w_gate, L0_ffn_w_up, L0_ffn_w_down):
    f32 = lambda a: np.ascontiguousarray(np.asarray(a, dtype=np.float32))
    x = f32(L0_attn_x).reshape(S, D)
    wqT = np.ascontiguousarray(f32(L0_attn_wq).T)
    wkT = np.ascontiguousarray(f32(L0_attn_wk).T)
    wvT = np.ascontiguousarray(f32(L0_attn_wv).T)
    woT = np.ascontiguousarray(f32(L0_attn_wo).T)
    wg = f32(L0_ffn_w_gate)
    wu = f32(L0_ffn_w_up)
    wd = f32(L0_ffn_w_down)
    wd_pad = np.concatenate(
        [wd, np.zeros((D, 3 * 128 * NCORES - I_FFN), dtype=np.float32)], axis=1)
    gain128 = np.ascontiguousarray(np.tile(f32(L0_attn_q_gain)[None, :],
                                           (128, 1)))

    inv_freq = (1.0 / 10000.0 ** (np.arange(0, HD, 2, dtype=np.float32)
                                  / np.float32(HD))).astype(np.float32)
    freqs = np.outer(np.arange(S, dtype=np.float32), inv_freq).astype(np.float32)
    cosb = np.cos(freqs.astype(np.float64)).astype(ml_dtypes.bfloat16)
    sinb = np.sin(freqs.astype(np.float64)).astype(ml_dtypes.bfloat16)
    tri = np.tril(np.ones((128, 128), dtype=bool))
    m0 = np.where(tri, np.float32(0), np.float32(FMIN)).astype(np.float32)
    mask384 = np.concatenate(
        [m0, np.zeros((128, 128), dtype=np.float32), m0], axis=1)
    ident = np.eye(128, dtype=ml_dtypes.bfloat16)

    if "nc" not in _NC_CACHE:
        _NC_CACHE["nc"] = build_program()
    nc = _NC_CACHE["nc"]

    def pmaj(a, tile):
        nt = a.shape[0] // 128
        return np.ascontiguousarray(
            a.reshape(nt, 128, tile).swapaxes(0, 1).reshape(128, nt * tile))

    wqTs = pmaj(wqT, D)
    wkTs = pmaj(wkT, D)
    wvTs = pmaj(wvT, D)
    woTs = pmaj(woT, D)
    in_maps = []
    for c in range(NCORES):
        sl = slice(c * T, (c + 1) * T)
        wdc = wd_pad[:, 384 * c:384 * (c + 1)]
        wds = np.ascontiguousarray(
            wdc.reshape(8, 128, 3, 128).transpose(1, 2, 0, 3)
            .reshape(128, 3 * 8 * 128))
        in_maps.append({
            "x": pmaj(x[sl], D),
            "wqT": wqTs, "wkT": wkTs, "wvT": wvTs, "woT": woTs,
            "wg": pmaj(np.ascontiguousarray(wg[:, 128 * c:128 * (c + 1)]), 128),
            "wu": pmaj(np.ascontiguousarray(wu[:, 128 * c:128 * (c + 1)]), 128),
            "wd": wds,
            "cosb": pmaj(cosb[sl], 32),
            "sinb": pmaj(sinb[sl], 32),
            "gain128": gain128, "mask0": mask384, "ident": ident,
        })
    res = run_bass_kernel_spmd(nc, in_maps, core_ids=list(range(NCORES)))
    out = np.concatenate(
        [r["out"].reshape(128, NT, D).swapaxes(0, 1).reshape(T, D)
         for r in res.results], axis=0)
    return out.reshape(1, S, D).astype(np.float32)
